# version 10
# baseline (speedup 1.0000x reference)
import numpy as np
import ml_dtypes
import concourse.bass as bass
import concourse.mybir as mybir

F32 = mybir.dt.float32
BF16 = mybir.dt.bfloat16
I32 = mybir.dt.int32
I16 = mybir.dt.int16
U8 = mybir.dt.uint8
U32 = mybir.dt.uint32
AF = mybir.ActivationFunctionType
ALU = mybir.AluOpType
AX = mybir.AxisListType
ET = mybir.EngineType

NEG = -1.0e9
EPS_LN = 1.0e-5
EPS_RMS = 1.1920928955078125e-07


class Ctr:
    def __init__(self, nc, name):
        self.sem = nc.alloc_semaphore(name)
        self.n = 0

    def inc(self, inst, k=1):
        inst.then_inc(self.sem, k)
        self.n += k
        return inst

    def dma(self, inst):
        return self.inc(inst, 16)

    def wait(self, eng):
        eng.wait_ge(self.sem, self.n)

    def snap(self):
        return self.n

    def wait_at(self, eng, val):
        eng.wait_ge(self.sem, val)


class K:

    def __init__(self, nc, block):
        self.nc = nc
        self.block = block
        self.ld = Ctr(nc, "c_ld")
        self.ws = Ctr(nc, "c_ws")
        self.pe = Ctr(nc, "c_pe")
        self.dv = Ctr(nc, "c_dv")
        self.sc = Ctr(nc, "c_sc")
        self.st = Ctr(nc, "c_st")
        self.gs = Ctr(nc, "c_gs")

    def all_ctrs(self):
        return (self.ld, self.ws, self.pe, self.dv, self.sc, self.st, self.gs)

    def barrier(self):
        ctrs = self.all_ctrs()

        def mk(engname):
            def f(eng):
                for c in ctrs:
                    c.wait(eng)
            return f
        self.block.sync(mk("sync"))
        self.block.vector(mk("vector"))
        self.block.scalar(mk("scalar"))
        self.block.tensor(mk("tensor"))
        self.block.gpsimd(mk("gpsimd"))



def host_bytesT(W_packed):
    b = np.ascontiguousarray(W_packed).view(np.uint8)[:, ::4]
    return np.ascontiguousarray(b.T)


def host_T(W):
    return np.ascontiguousarray(np.asarray(W, np.float32).T)


def l0_rope_tables(positions):
    half = 64
    inv_freq = 10000.0 ** (-np.arange(0, half, dtype=np.float32) / half)
    freqs = positions[:, None].astype(np.float32) * inv_freq[None, :]
    return np.cos(freqs).astype(np.float32), np.sin(freqs).astype(np.float32)


def l1_rope_tables(positions):
    hd = 128
    freqs = 1.0 / 10000.0 ** (np.arange(0, hd, 2, dtype=np.float32) / hd)
    ang = positions[:, None].astype(np.float32) * freqs[None, :]
    cos = np.tile(np.cos(ang), (1, 2)).astype(np.float32)
    sin = np.tile(np.sin(ang), (1, 2)).astype(np.float32)
    return cos, sin


def mask_tiles():
    i = np.arange(128)[:, None]
    j = np.arange(128)[None, :]
    m_first = np.where(j >= i + 1, 0.0, NEG).astype(ml_dtypes.bfloat16)
    m_last = np.where(j <= i, 0.0, NEG).astype(ml_dtypes.bfloat16)
    return m_first, m_last


def deint(ap, rows0, c0, c1):
    assert rows0 % 128 == 0
    sl = ap[rows0:rows0 + 128, c0:c1]
    return sl.rearrange("(a i) n -> i a n", a=2)



def emit_decode(nc, es, block, kk, name, n_kchunks, width, bytes_ap, scales_ap,
                out_ap, ntile=2048):
    W = min(ntile, width)
    nslices = width // W
    sb = lambda nm, shape, dt: es.enter_context(
        nc.sbuf_tensor(f"{name}_{nm}", shape, dt))
    t0 = sb("t0", [128, W], U8)
    t = [sb(f"t{i+1}", [128, W], I16) for i in range(8)]
    f1 = sb("f1", [128, W], F32)
    stile = sb("stile", [128, W], F32)
    obuf = sb("obuf", [128, W], BF16)
    spv = sb("spv", [128, 1], I16)
    ld, dv, sc = kk.ld, kk.dv, kk.sc

    @block.vector
    def _(vector):
        dv.inc(vector.memset(spv[0:64, :], 0))
        dv.inc(vector.memset(spv[64:128, :], 4))

    for ck in range(n_kchunks):
        for ns in range(nslices):
            n0 = ns * W
            bsl = bytes_ap[64 * ck:64 * (ck + 1), n0:n0 + W]
            ssl = scales_ap[4 * ck:4 * (ck + 1), n0:n0 + W]

            @block.sync
            def _(sync, bsl=bsl, ssl=ssl):
                dv.wait(sync)
                ld.dma(sync.dma_start(out=t0[0:64, :], in_=bsl))
                ld.dma(sync.dma_start(out=t0[64:128, :], in_=bsl))
                bc = ssl.unsqueeze(1).broadcast_to([4, 16, W])
                ld.dma(sync.dma_start(out=stile[0:64, :], in_=bc))
                ld.dma(sync.dma_start(out=stile[64:128, :], in_=bc))

            @block.vector
            def _(vector):
                ld.wait(vector)
                sc.wait(vector)
                dv.inc(vector.tensor_copy(t[0][:], t0[:]))
                dv.wait(vector)
                dv.inc(vector.tensor_scalar(out=t[1][:], in0=t[0][:],
                                            scalar1=spv[:], scalar2=15,
                                            op0=ALU.logical_shift_right,
                                            op1=ALU.bitwise_and))
                dv.wait(vector)
                dv.inc(vector.tensor_scalar(out=t[2][:], in0=t[1][:], scalar1=1,
                                            scalar2=2, op0=ALU.bitwise_and,
                                            op1=ALU.bitwise_or))
                dv.inc(vector.tensor_scalar(out=t[3][:], in0=t[1][:], scalar1=1,
                                            scalar2=3, op0=ALU.logical_shift_right,
                                            op1=ALU.bitwise_and))
                dv.inc(vector.tensor_scalar(out=t[4][:], in0=t[1][:], scalar1=7,
                                            scalar2=None, op0=ALU.bitwise_and))
                dv.inc(vector.tensor_scalar(out=t[5][:], in0=t[1][:], scalar1=3,
                                            scalar2=None,
                                            op0=ALU.logical_shift_right))
                dv.wait(vector)
                dv.inc(vector.tensor_tensor(out=t[6][:], in0=t[2][:], in1=t[3][:],
                                            op=ALU.logical_shift_left))
                dv.inc(vector.tensor_scalar(out=t[7][:], in0=t[4][:], scalar1=2,
                                            scalar2=None, op0=ALU.min))
                dv.inc(vector.tensor_scalar(out=t[3][:], in0=t[5][:], scalar1=-2,
                                            scalar2=1, op0=ALU.mult, op1=ALU.add))
                dv.wait(vector)
                dv.inc(vector.scalar_tensor_tensor(out=t[0][:], in0=t[6][:],
                                                   scalar=-2.0, in1=t[7][:],
                                                   op0=ALU.add, op1=ALU.add))
                dv.wait(vector)
                dv.inc(vector.tensor_tensor(out=t[1][:], in0=t[0][:], in1=t[3][:],
                                            op=ALU.mult))

            @block.scalar
            def _(scalar):
                dv.wait(scalar)
                sc.inc(scalar.activation(out=f1[:], in_=t[1][:], func=AF.Copy,
                                         scale=0.25))

            @block.vector
            def _(vector):
                sc.wait(vector)
                kk.gs.wait(vector)
                dv.inc(vector.tensor_tensor(out=obuf[:], in0=f1[:], in1=stile[:],
                                            op=ALU.mult))

            osl = out_ap[128 * ck:128 * (ck + 1), n0:n0 + W]

            @block.gpsimd
            def _(g, osl=osl):
                dv.wait(g)
                kk.gs.dma(g.dma_start(out=osl, in_=obuf[:]))



def emit_layer0(nc, es, block, kk, io):
    from contextlib import ExitStack
    ld, ws, pe, dv, sc, st = kk.ld, kk.ws, kk.pe, kk.dv, kk.sc, kk.st
    gs = kk.gs
    sbuf = lambda nm, shape, dt: nc.sbuf_tensor("sb_" + nm, shape, dt)

    ident = es.enter_context(sbuf("ident", [128, 128], BF16))
    onesb = es.enter_context(sbuf("onesb", [128, 128], BF16))
    onesf = es.enter_context(sbuf("onesf", [1, 128], F32))
    mfirst = es.enter_context(sbuf("mfirst", [128, 128], BF16))
    mlast = es.enter_context(sbuf("mlast", [128, 128], BF16))
    qgain = es.enter_context(sbuf("qgain", [1, 16], F32))
    halon = es.enter_context(sbuf("halon", [1, 2560], BF16))
    epst = es.enter_context(sbuf("epst", [128, 2], F32))
    es2 = ExitStack()
    cosk = es2.enter_context(sbuf("cosk", [128, 20 * 64], F32))
    sink = es2.enter_context(sbuf("sink", [128, 20 * 64], F32))
    qgbc = es2.enter_context(sbuf("qgbc", [128, 16], F32))

    @block.vector
    def _(vector):
        dv.inc(vector.memset(epst[:, 0:1], EPS_LN))
        dv.inc(vector.memset(epst[:, 1:2], EPS_RMS))

    @block.sync
    def _(sync):
        ld.dma(sync.dma_start(out=ident[:], in_=io["ident"]))
        ld.dma(sync.dma_start(out=onesb[:], in_=io["ones_bf"]))
        ld.dma(sync.dma_start(out=onesf[:], in_=io["ones_f"]))
        ld.dma(sync.dma_start(out=mfirst[:], in_=io["m_first"]))
        ld.dma(sync.dma_start(out=mlast[:], in_=io["m_last"]))
        ld.dma(sync.dma_start(out=qgain[:], in_=io["qgain"]))
        ld.dma(sync.dma_start(out=halon[:], in_=io["haloneg"]))
        ld.dma(sync.dma_start(
            out=cosk[:], in_=io["coskey"].rearrange("(a p) c -> p a c", p=128)))
        ld.dma(sync.dma_start(
            out=sink[:], in_=io["sinkey"].rearrange("(a p) c -> p a c", p=128)))

    with ExitStack() as sp:
        pg = sp.enter_context(nc.psum_tensor("pqg", [128, 16], F32))

        @block.tensor
        def _(tensor):
            ld.wait(tensor)
            pe.inc(tensor.matmul(pg[:], onesf[:], qgain[:], start=True, stop=True))

        @block.vector
        def _(vector):
            pe.wait(vector)
            dv.inc(vector.tensor_copy(qgbc[:], pg[:]))

    xnt = [es2.enter_context(sbuf(f"xnt{ck}", [128, 2560], BF16))
           for ck in range(16)]

    kk.barrier()
    with ExitStack() as s1:
        xb = [s1.enter_context(sbuf(f"s1x{i}", [128, 2048], F32)) for i in range(2)]
        xsq = s1.enter_context(sbuf("s1sq", [128, 2048], F32))
        xnb = s1.enter_context(sbuf("s1xn", [128, 2048], BF16))
        stat = s1.enter_context(sbuf("s1st", [128, 6], F32))
        ptr = [s1.enter_context(nc.psum_tensor(f"s1p{i}", [128, 128], BF16))
               for i in range(2)]
        sst = {}
        for tt in range(20):
            own = tt < 16
            src = io["x_my"][tt * 128:(tt + 1) * 128, :] if own else \
                io["x_halo"][(tt - 16) * 128:(tt - 15) * 128, :]
            dcol = 512 + tt * 128 if own else (tt - 16) * 128
            buf = xb[tt % 2]

            @block.sync
            def _(sync, src=src, buf=buf, tt=tt):
                if tt >= 2:
                    dv.wait_at(sync, sst[("xnb", tt - 2)])
                ld.dma(sync.dma_start(out=buf[:], in_=src))

            @block.vector
            def _(vector, buf=buf):
                ld.wait(vector)
                sc.wait(vector)
                dv.inc(vector.reduce_sum(stat[:, 0:1], buf[:], axis=AX.X))
                dv.inc(vector.tensor_tensor(out=xsq[:], in0=buf[:], in1=buf[:],
                                            op=ALU.mult))
                dv.wait(vector)
                dv.inc(vector.reduce_sum(stat[:, 1:2], xsq[:], axis=AX.X))
                dv.inc(vector.tensor_scalar(out=stat[:, 2:3], in0=stat[:, 0:1],
                                            scalar1=1.0 / 2048.0, scalar2=None,
                                            op0=ALU.mult))
                dv.wait(vector)
                dv.inc(vector.tensor_scalar(out=stat[:, 3:4], in0=stat[:, 1:2],
                                            scalar1=1.0 / 2048.0, scalar2=None,
                                            op0=ALU.mult))
                dv.inc(vector.tensor_tensor(out=stat[:, 4:5], in0=stat[:, 2:3],
                                            in1=stat[:, 2:3], op=ALU.mult))
                dv.wait(vector)
                dv.inc(vector.tensor_tensor(out=stat[:, 5:6], in0=stat[:, 3:4],
                                            in1=stat[:, 4:5], op=ALU.subtract))

            @block.scalar
            def _(scalar):
                dv.wait(scalar)
                sc.inc(scalar.activation(out=stat[:, 4:5], in_=stat[:, 5:6],
                                         func=AF.Ln, bias=epst[:, 0:1]))
                sc.wait(scalar)
                sc.inc(scalar.activation(out=stat[:, 5:6], in_=stat[:, 4:5],
                                         func=AF.Exp, scale=-0.5))

            @block.vector
            def _(vector, buf=buf, tt=tt):
                sc.wait(vector)
                if tt >= 1:
                    pe.wait_at(vector, sst[("t15", tt - 1)])
                dv.inc(vector.tensor_scalar(out=xnb[:], in0=buf[:],
                                            scalar1=stat[:, 2:3],
                                            scalar2=stat[:, 5:6],
                                            op0=ALU.subtract, op1=ALU.mult))
            sst[("xnb", tt)] = dv.snap()

            tsn = {}
            csn = {}
            for ck in range(16):
                @block.tensor
                def _(tensor, ck=ck, tt=tt):
                    if ck == 0:
                        dv.wait_at(tensor, sst[("xnb", tt)])
                    elif ck >= 2:
                        dv.wait_at(tensor, csn[ck - 2])
                    pe.inc(tensor.transpose(ptr[ck % 2][:],
                                            xnb[:, ck * 128:(ck + 1) * 128],
                                            ident[:]))
                    tsn[ck] = pe.snap()

                @block.vector
                def _(vector, ck=ck, dcol=dcol):
                    pe.wait_at(vector, tsn[ck])
                    dv.inc(vector.tensor_copy(
                        xnt[ck][:, dcol:dcol + 128], ptr[ck % 2][:]))
                    csn[ck] = dv.snap()
            sst[("t15", tt)] = tsn[15]

    if "xnt_h" in io:
        for ck in range(16):
            @block.sync
            def _(sync, ck=ck):
                dv.wait(sync)
                st.dma(sync.dma_start(
                    out=io["xnt_h"][ck * 128:(ck + 1) * 128, :], in_=xnt[ck][:]))

    def rms_rope_tile(qtf, sq4, rstd4, rr, gain_cols, cidx, scale, dst_bf):
        @block.vector
        def _(vector):
            dv.wait(vector)
            sc.wait(vector)
            dv.inc(vector.tensor_tensor(out=sq4[:], in0=qtf[:], in1=qtf[:],
                                        op=ALU.mult))
            dv.wait(vector)
            dv.inc(vector.tensor_reduce(
                out=rstd4[:, 0:4].unsqueeze(2),
                in_=sq4[:].rearrange("p (h i) -> p h i", h=4),
                axis=AX.X, op=ALU.add))

        @block.scalar
        def _(scalar):
            dv.wait(scalar)
            sc.inc(scalar.activation(out=rstd4[:, 4:8], in_=rstd4[:, 0:4],
                                     func=AF.Ln, scale=1.0 / 128.0,
                                     bias=epst[:, 1:2]))
            sc.wait(scalar)
            sc.inc(scalar.activation(out=rstd4[:, 0:4], in_=rstd4[:, 4:8],
                                     func=AF.Exp, scale=-0.5))

        @block.vector
        def _(vector):
            sc.wait(vector)
            if gain_cols is not None:
                dv.inc(vector.tensor_tensor(out=rstd4[:, 0:4],
                                            in0=rstd4[:, 0:4],
                                            in1=gain_cols, op=ALU.mult))
                dv.wait(vector)
            dv.inc(vector.tensor_tensor(
                out=qtf[:].rearrange("p (h i) -> p h i", h=4),
                in0=qtf[:].rearrange("p (h i) -> p h i", h=4),
                in1=rstd4[:, 0:4].unsqueeze(2).broadcast_to([128, 4, 128]),
                op=ALU.mult))
            dv.wait(vector)
            c_b = cosk[:, cidx * 64:(cidx + 1) * 64].unsqueeze(1).unsqueeze(1) \
                .broadcast_to([128, 4, 2, 64])
            s_b = sink[:, cidx * 64:(cidx + 1) * 64].unsqueeze(1).unsqueeze(1) \
                .broadcast_to([128, 4, 2, 64])
            qv = qtf[:].rearrange("p (h a i) -> p h a i", h=4, a=2)
            ra = rr[0][:].rearrange("p (h a i) -> p h a i", h=4, a=2)
            rb = rr[1][:].rearrange("p (h a i) -> p h a i", h=4, a=2)
            dv.inc(vector.tensor_tensor(out=ra, in0=qv, in1=c_b, op=ALU.mult))
            dv.inc(vector.tensor_tensor(out=rb, in0=qv, in1=s_b, op=ALU.mult))
            dv.wait(vector)
            pre = sq4[:].rearrange("p (h a i) -> p h a i", h=4, a=2)
            dv.inc(vector.scalar_tensor_tensor(
                out=pre[:, :, 0:1, :], in0=ra[:, :, 0:1, :], scalar=1.0,
                in1=rb[:, :, 1:2, :], op0=ALU.mult, op1=ALU.add))
            dv.inc(vector.scalar_tensor_tensor(
                out=pre[:, :, 1:2, :], in0=ra[:, :, 1:2, :], scalar=1.0,
                in1=rb[:, :, 0:1, :], op0=ALU.mult, op1=ALU.subtract))
            dv.wait(vector)
            dv.inc(vector.tensor_scalar(out=dst_bf[:], in0=sq4[:],
                                        scalar1=scale, scalar2=None,
                                        op0=ALU.mult))

    NKT = 20
    with ExitStack() as s23:
        k_dm = [s23.enter_context(sbuf(f"kdm{e}", [128, 2560], BF16))
                for e in range(4)]
        v_tm = [s23.enter_context(sbuf(f"vtm{i}", [128, 512], BF16))
                for i in range(NKT)]
        wbuf = [s23.enter_context(sbuf(f"wk{ck}", [128, 512], BF16))
                for ck in range(16)]
        ktf = s23.enter_context(sbuf("ktf", [128, 512], F32))
        sq4 = s23.enter_context(sbuf("sq4", [128, 512], F32))
        rstd4 = s23.enter_context(sbuf("rstd4", [128, 8], F32))
        rr = [s23.enter_context(sbuf(f"rr{i}", [128, 512], F32)) for i in range(2)]
        tb = s23.enter_context(sbuf("tb", [128, 512], BF16))

        kk.barrier()
        with ExitStack() as sp:
            pk = [sp.enter_context(nc.psum_tensor(f"pk{i}", [128, 512], F32))
                  for i in range(2)]
            ptr = sp.enter_context(nc.psum_tensor("ptr2", [128, 128], BF16))

            @block.gpsimd
            def _(g):
                for ck in range(16):
                    ws.dma(g.dma_start(out=wbuf[ck][:],
                                       in_=io["wkt"][ck * 128:(ck + 1) * 128, :]))

            kcpsn = {}
            for kt in range(NKT):
                @block.tensor
                def _(tensor, kt=kt):
                    ws.wait(tensor)
                    if kt >= 2:
                        dv.wait_at(tensor, kcpsn[("ktf", kt - 2)])
                    else:
                        dv.wait(tensor)
                    for ck in range(16):
                        inst = tensor.matmul(
                            pk[kt % 2][:], xnt[ck][:, kt * 128:(kt + 1) * 128],
                            wbuf[ck][:], start=(ck == 0), stop=(ck == 15))
                    pe.inc(inst)
                kmm_sn = pe.snap()

                @block.vector
                def _(vector, kt=kt, kmm_sn=kmm_sn):
                    pe.wait_at(vector, kmm_sn)
                    dv.inc(vector.tensor_copy(ktf[:], pk[kt % 2][:]))
                kcpsn[("ktf", kt)] = dv.snap()
                rms_rope_tile(ktf, sq4, rstd4, rr, None, kt, 1.0, tb)
                rp_sn = dv.snap()

                ksn = {}
                kdsn = {}
                for e in range(4):
                    @block.tensor
                    def _(tensor, e=e, rp_sn=rp_sn):
                        if e == 0:
                            dv.wait_at(tensor, rp_sn)
                        else:
                            dv.wait_at(tensor, kdsn[e - 1])
                        pe.inc(tensor.transpose(ptr[:], tb[:, e * 128:(e + 1) * 128],
                                                ident[:]))
                        ksn[e] = pe.snap()

                    @block.vector
                    def _(vector, e=e, kt=kt):
                        pe.wait_at(vector, ksn[e])
                        dv.inc(vector.tensor_copy(
                            k_dm[e][:, kt * 128:(kt + 1) * 128], ptr[:]))
                        kdsn[e] = dv.snap()

        kk.barrier()
        with ExitStack() as sp:
            pv = [sp.enter_context(nc.psum_tensor(f"pv{i}", [128, 512], F32))
                  for i in range(2)]

            @block.gpsimd
            def _(g):
                pe.wait(g)
                dv.wait(g)
                for ck in range(16):
                    ws.dma(g.dma_start(out=wbuf[ck][:],
                                       in_=io["wvt"][ck * 128:(ck + 1) * 128, :]))

            vcpsn = {}
            for kt in range(NKT):
                @block.tensor
                def _(tensor, kt=kt):
                    ws.wait(tensor)
                    if kt >= 2:
                        dv.wait_at(tensor, vcpsn[kt - 2])
                    elif kt == 0:
                        dv.wait(tensor)
                    for ck in range(16):
                        inst = tensor.matmul(
                            pv[kt % 2][:], xnt[ck][:, kt * 128:(kt + 1) * 128],
                            wbuf[ck][:], start=(ck == 0), stop=(ck == 15))
                    pe.inc(inst)
                vmm_sn = pe.snap()

                @block.vector
                def _(vector, kt=kt, vmm_sn=vmm_sn):
                    pe.wait_at(vector, vmm_sn)
                    dv.inc(vector.tensor_copy(v_tm[kt][:], pv[kt % 2][:]))
                vcpsn[kt] = dv.snap()

        kk.barrier()
        q_dm2 = [s23.enter_context(sbuf(f"qd2_{i}", [128, 512], BF16))
                 for i in range(4)]
        colnegc = s23.enter_context(sbuf("colnegc", [128, 20], F32))
        mt_l0 = s23.enter_context(sbuf("mt_l0", [128, 8 * 512], BF16))
        identf = s23.enter_context(sbuf("identf", [128, 128], F32))

        @block.sync
        def _(sync):
            ld.dma(sync.dma_start(out=colnegc[:], in_=io["colnegc"]))
            for v_ in range(8):
                ld.dma(sync.dma_start(
                    out=mt_l0[:, v_ * 512:(v_ + 1) * 512], in_=io["mt_l0"][v_]))
            ld.dma(sync.dma_start(out=identf[:], in_=io["identf"]))

        with ExitStack() as spq:
            pq2 = spq.enter_context(nc.psum_tensor("l0pq2", [128, 512], F32))
            ptr2 = spq.enter_context(nc.psum_tensor("l0ptr2", [128, 128], BF16))

            stq = {"kcp": None}

            def fill_q(hg, tg):
                if tg == 0:
                    @block.gpsimd
                    def _(g2, hg=hg):
                        dv.wait(g2)
                        pe.wait(g2)
                        for ck in range(16):
                            ws.dma(g2.dma_start(
                                out=wbuf[ck][:],
                                in_=io["wqt"][ck * 128:(ck + 1) * 128,
                                              hg * 512:(hg + 1) * 512]))

                for sub in range(4):
                    tt = tg * 4 + sub

                    @block.tensor
                    def _(tensor, tt=tt):
                        ws.wait(tensor)
                        if stq["kcp"] is not None:
                            dv.wait_at(tensor, stq["kcp"])
                        else:
                            dv.wait(tensor)
                        for ck in range(16):
                            inst = tensor.matmul(
                                pq2[:],
                                xnt[ck][:, 512 + tt * 128:512 + (tt + 1) * 128],
                                wbuf[ck][:], start=(ck == 0), stop=(ck == 15))
                        pe.inc(inst)
                    qmm_sn = pe.snap()

                    @block.vector
                    def _(vector, tt=tt, qmm_sn=qmm_sn):
                        pe.wait_at(vector, qmm_sn)
                        dv.inc(vector.tensor_copy(ktf[:], pq2[:]))
                    stq["kcp"] = dv.snap()
                    rms_rope_tile(ktf, sq4, rstd4, rr,
                                  qgbc[:, hg * 4:(hg + 1) * 4], 4 + tt,
                                  0.08838834764831845, tb)
                    rq_sn = dv.snap()

                    qtsn = {}
                    qcsn = {}
                    for e in range(4):
                        @block.tensor
                        def _(tensor, e=e, rq_sn=rq_sn):
                            if e == 0:
                                dv.wait_at(tensor, rq_sn)
                            else:
                                dv.wait_at(tensor, qcsn[e - 1])
                            pe.inc(tensor.transpose(
                                ptr2[:], tb[:, e * 128:(e + 1) * 128], ident[:]))
                            qtsn[e] = pe.snap()

                        @block.vector
                        def _(vector, e=e, sub=sub):
                            pe.wait_at(vector, qtsn[e])
                            dv.inc(vector.tensor_copy(
                                q_dm2[e][:, sub * 128:(sub + 1) * 128], ptr2[:]))
                            qcsn[e] = dv.snap()

            def yt_store(h, tg, ytile):
                @block.gpsimd
                def _(g2, h=h, tg=tg, ytile=ytile):
                    dv.wait(g2)
                    gs.dma(g2.dma_start(
                        out=io["yt_h"][h * 128:(h + 1) * 128,
                                       tg * 512:(tg + 1) * 512],
                        in_=ytile[:]))

            emit_attn2(nc, s23, block, kk, dict(
                mode="l0", tag="Z", nkt=20, k_dm=k_dm, v_tm=v_tm, q_dm=q_dm2,
                onesb=onesb, onesf=onesf, ident=ident, identf=identf,
                colneg_row=halon[:], colnegc=colnegc,
                mrow_first=mfirst, mrow_last=mlast,
                mt_tiles=mt_l0, fill_q=fill_q, yt_store=yt_store))

    kk.barrier()
    with ExitStack() as s4:
        wo_sb = [s4.enter_context(sbuf(f"wo{ck}", [128, 2048], BF16))
                 for ck in range(16)]
        ytb = [s4.enter_context(sbuf(f"ytb{i}", [128, 128], BF16))
               for i in range(2)]
        stile = s4.enter_context(sbuf("stile4", [128, 2048], BF16))
        ps4 = [s4.enter_context(nc.psum_tensor(f"ps4{i}", [128, 512], F32))
               for i in range(4)]

        ypar = [Ctr(nc, "s4y0"), Ctr(nc, "s4y1")]
        ysn = {}
        mmsn = {}
        evsn = {}

        @block.gpsimd
        def _(g2):
            dv.wait(g2)
            for ck in range(16):
                ws.dma(g2.dma_start(out=wo_sb[ck][:],
                                    in_=io["wot"][ck * 128:(ck + 1) * 128, :]))

        for tt in range(16):
            for ck in range(16):
                gi = tt * 16 + ck

                @block.gpsimd
                def _(g2, tt=tt, ck=ck, gi=gi):
                    if gi == 0:
                        ws.wait(g2)
                    if gi >= 2:
                        pe.wait_at(g2, mmsn[gi - 2])
                    ypar[gi % 2].dma(g2.dma_start(
                        out=ytb[gi % 2][:],
                        in_=io["yt_h"][ck * 128:(ck + 1) * 128,
                                       tt * 128:(tt + 1) * 128]))
                    ysn[gi] = ypar[gi % 2].snap()

                @block.tensor
                def _(tensor, tt=tt, ck=ck, gi=gi):
                    ypar[gi % 2].wait_at(tensor, ysn[gi])
                    if ck == 0 and tt > 0:
                        dv.wait_at(tensor, evsn[tt - 1])
                    for et in range(4):
                        inst = tensor.matmul(
                            ps4[et][:], ytb[gi % 2][:],
                            wo_sb[ck][:, et * 512:(et + 1) * 512],
                            start=(ck == 0), stop=(ck == 15))
                    pe.inc(inst)
                    mmsn[gi] = pe.snap()

            @block.vector
            def _(vector, tt=tt):
                pe.wait_at(vector, mmsn[tt * 16 + 15])
                gs.wait(vector)
                for et in range(4):
                    dv.inc(vector.tensor_copy(
                        stile[:, et * 512:(et + 1) * 512], ps4[et][:]))
                evsn[tt] = dv.snap()

            @block.gpsimd
            def _(g2, tt=tt):
                dv.wait_at(g2, evsn[tt])
                gs.dma(g2.dma_start(
                    out=io["s_h"][tt * 128:(tt + 1) * 128, :], in_=stile[:]))

    kk.barrier()
    C1 = 0.7978845608028654
    C2 = 0.7978845608028654 * 0.044715
    with ExitStack() as s5:
        gwb = [[s5.enter_context(sbuf(f"gw{s_}_{ck}", [128, 512], BF16))
                for ck in range(16)] for s_ in range(2)]
        uwb = [[s5.enter_context(sbuf(f"uw{s_}_{ck}", [128, 512], BF16))
                for ck in range(16)] for s_ in range(2)]
        intb = [s5.enter_context(sbuf(f"intb{i}", [128, 2048], BF16))
                for i in range(4)]
        ge = [s5.enter_context(sbuf(f"ge{i}", [128, 512], F32)) for i in range(2)]
        ue = [s5.enter_context(sbuf(f"ue{i}", [128, 512], F32)) for i in range(2)]
        g2b = s5.enter_context(sbuf("g2b", [128, 512], F32))
        g3b = s5.enter_context(sbuf("g3b", [128, 512], F32))
        thb = s5.enter_context(sbuf("thb", [128, 512], F32))
        pg = [s5.enter_context(nc.psum_tensor(f"pg{i}", [128, 512], F32))
              for i in range(4)]
        pu = [s5.enter_context(nc.psum_tensor(f"pu{i}", [128, 512], F32))
              for i in range(4)]
        wpar = [Ctr(nc, "s5w0"), Ctr(nc, "s5w1")]
        wpar2 = [Ctr(nc, "s5v0"), Ctr(nc, "s5v1")]
        spar = [Ctr(nc, "s5s0"), Ctr(nc, "s5s1")]
        wsn = {}
        w2sn = {}
        ssn = {}
        mmsn = {}
        cpsn = {}

        def emit_w_loads(h5, gate_sn):
            sw = h5 % 2

            @block.gpsimd
            def _(g2x, h5=h5, sw=sw, gate_sn=gate_sn):
                if gate_sn is not None:
                    pe.wait_at(g2x, gate_sn)
                else:
                    dv.wait(g2x)
                for ck in range(16):
                    wpar[sw].dma(g2x.dma_start(
                        out=uwb[sw][ck][:],
                        in_=deint(io["du"], ck * 128, h5 * 512, (h5 + 1) * 512)))
                wsn[h5] = wpar[sw].snap()

            @block.sync
            def _(sync, h5=h5, sw=sw, gate_sn=gate_sn):
                if gate_sn is not None:
                    pe.wait_at(sync, gate_sn)
                for ck in range(16):
                    wpar2[sw].dma(sync.dma_start(
                        out=gwb[sw][ck][:],
                        in_=deint(io["dg"], ck * 128, h5 * 512, (h5 + 1) * 512)))
                w2sn[h5] = wpar2[sw].snap()

        emit_w_loads(0, None)
        emit_w_loads(1, None)
        for h5 in range(32):
            sw = h5 % 2

            for n4 in range(4):
                nt = h5 * 4 + n4
                for pair in range(2):
                    sidx = nt * 2 + pair
                    sb_ = sidx % 2

                    @block.tensor
                    def _(tensor, h5=h5, sw=sw, n4=n4, pair=pair, sidx=sidx,
                          sb_=sb_):
                        wpar[sw].wait_at(tensor, wsn[h5])
                        wpar2[sw].wait_at(tensor, w2sn[h5])
                        if sidx >= 2:
                            dv.wait_at(tensor, cpsn[sidx - 2])
                        for ck in range(16):
                            for j in range(2):
                                tc = pair * 2 + j
                                tensor.matmul(
                                    pg[sb_ * 2 + j][:],
                                    gwb[sw][ck][:, n4 * 128:(n4 + 1) * 128],
                                    xnt[ck][:, 512 + tc * 512:512 + (tc + 1) * 512],
                                    start=(ck == 0), stop=(ck == 15))
                                inst = tensor.matmul(
                                    pu[sb_ * 2 + j][:],
                                    uwb[sw][ck][:, n4 * 128:(n4 + 1) * 128],
                                    xnt[ck][:, 512 + tc * 512:512 + (tc + 1) * 512],
                                    start=(ck == 0), stop=(ck == 15))
                        pe.inc(inst)
                        mmsn[sidx] = pe.snap()

                    @block.vector
                    def _(vector, nt=nt, pair=pair, sidx=sidx, sb_=sb_):
                        pe.wait_at(vector, mmsn[sidx])
                        dv.wait(vector)
                        for j in range(2):
                            dv.inc(vector.tensor_copy(ge[j][:], pg[sb_ * 2 + j][:]))
                            dv.inc(vector.tensor_copy(ue[j][:], pu[sb_ * 2 + j][:]))
                        cpsn[sidx] = dv.snap()

                    for j in range(2):
                        tc = pair * 2 + j

                        @block.vector
                        def _(vector, j=j):
                            dv.wait(vector)
                            dv.inc(vector.tensor_tensor(out=g2b[:], in0=ge[j][:],
                                                        in1=ge[j][:], op=ALU.mult))
                            dv.wait(vector)
                            dv.inc(vector.tensor_scalar(out=g2b[:], in0=g2b[:],
                                                        scalar1=C2, scalar2=C1,
                                                        op0=ALU.mult, op1=ALU.add))
                            dv.wait(vector)
                            dv.inc(vector.tensor_tensor(out=g3b[:], in0=g2b[:],
                                                        in1=ge[j][:], op=ALU.mult))

                        @block.scalar
                        def _(scalar):
                            dv.wait(scalar)
                            sc.inc(scalar.activation(out=thb[:], in_=g3b[:],
                                                     func=AF.Tanh))

                        @block.vector
                        def _(vector, nt=nt, tc=tc, j=j, pair=pair):
                            sc.wait(vector)
                            if pair == 0 and j == 0 and nt >= 4:
                                spar[nt % 2].wait_at(vector, ssn[nt - 4])
                            dv.inc(vector.tensor_tensor(out=g3b[:], in0=thb[:],
                                                        in1=ge[j][:], op=ALU.mult))
                            dv.wait(vector)
                            dv.inc(vector.tensor_tensor(out=g3b[:], in0=g3b[:],
                                                        in1=ge[j][:], op=ALU.add))
                            dv.wait(vector)
                            dv.inc(vector.scalar_tensor_tensor(
                                out=intb[nt % 4][:, tc * 512:(tc + 1) * 512],
                                in0=g3b[:], scalar=0.5, in1=ue[j][:],
                                op0=ALU.mult, op1=ALU.mult))


            if h5 + 2 < 32:
                emit_w_loads(h5 + 2, mmsn[h5 * 8 + 7])

            @block.sync
            def _(sync, h5=h5):
                dv.wait(sync)
                for n4 in range(4):
                    nt = h5 * 4 + n4
                    spar[nt % 2].dma(sync.dma_start(
                        out=io["inter_h"][nt * 128:(nt + 1) * 128, :],
                        in_=intb[nt % 4][:]))
                    ssn[nt] = spar[nt % 2].snap()

    kk.barrier()
    es2.close()
    with ExitStack() as s6:
        o2f = [s6.enter_context(sbuf(f"o2f{i}", [128, 2048], F32))
               for i in range(8)]
        chk = [[s6.enter_context(sbuf(f"chk{p}_{i}", [128, 1024], BF16))
                for i in range(8)] for p in range(2)]
        dbf = [[s6.enter_context(sbuf(f"dbf{p}_{i}", [128, 2048], BF16))
                for i in range(8)] for p in range(2)]
        pd = [s6.enter_context(nc.psum_tensor(f"pd{i}", [128, 512], F32))
              for i in range(8)]
        xb2 = s6.enter_context(sbuf("xb2", [128, 2048], F32))
        sb2 = s6.enter_context(sbuf("sb2", [128, 2048], BF16))
        gb2 = s6.enter_context(sbuf("gb2", [128, 2048], F32))
        ipar = [Ctr(nc, "s6i0"), Ctr(nc, "s6i1")]
        dpar = [Ctr(nc, "s6d0"), Ctr(nc, "s6d1")]
        isn = {}
        dsn = {}
        qsn = {}
        evsn = {}
        mmsn = {}

        for th2 in range(2):
            for c in range(16):
                ci = th2 * 16 + c
                p_ = ci % 2

                @block.sync
                def _(sync, th2=th2, c=c, ci=ci, p_=p_):
                    if ci >= 2:
                        pe.wait_at(sync, mmsn[ci - 2])
                    for nt8 in range(8):
                        ipar[p_].dma(sync.dma_start(
                            out=dbf[p_][nt8][:],
                            in_=deint(io["dd"], (c * 8 + nt8) * 128, 0, 2048)))
                    isn[ci] = ipar[p_].snap()

                @block.gpsimd
                def _(g2x, th2=th2, c=c, ci=ci, p_=p_):
                    if ci >= 2:
                        pe.wait_at(g2x, mmsn[ci - 2])
                    for nt8 in range(8):
                        r0 = (c * 8 + nt8) * 128
                        dpar[p_].dma(g2x.dma_start(
                            out=chk[p_][nt8][:],
                            in_=io["inter_h"][r0:r0 + 128,
                                              th2 * 1024:(th2 + 1) * 1024]))
                    dsn[ci] = dpar[p_].snap()

                for hc in range(4):
                    q = ci * 4 + hc

                    @block.tensor
                    def _(tensor, ci=ci, p_=p_, hc=hc, q=q):
                        if hc == 0:
                            ipar[p_].wait_at(tensor, isn[ci])
                            dpar[p_].wait_at(tensor, dsn[ci])
                        for nt8 in range(8):
                            for tt8 in range(8):
                                if nt8 == 0 and q >= 1:
                                    dv.wait_at(tensor, evsn[(q - 1, tt8)])
                                inst = tensor.matmul(
                                    pd[tt8][:],
                                    chk[p_][nt8][:, tt8 * 128:(tt8 + 1) * 128],
                                    dbf[p_][nt8][:, hc * 512:(hc + 1) * 512],
                                    start=(nt8 == 0), stop=(nt8 == 7))
                        pe.inc(inst)
                        qsn[q] = pe.snap()

                    @block.vector
                    def _(vector, c=c, hc=hc, q=q):
                        pe.wait_at(vector, qsn[q])
                        for tt8 in range(8):
                            if c == 0:
                                dv.inc(vector.tensor_copy(
                                    o2f[tt8][:, hc * 512:(hc + 1) * 512],
                                    pd[tt8][:]))
                            else:
                                dv.inc(vector.tensor_tensor(
                                    out=o2f[tt8][:, hc * 512:(hc + 1) * 512],
                                    in0=o2f[tt8][:, hc * 512:(hc + 1) * 512],
                                    in1=pd[tt8][:], op=ALU.add))
                            evsn[(q, tt8)] = dv.snap()
                mmsn[ci] = qsn[ci * 4 + 3]

            for i8 in range(8):
                tt = th2 * 8 + i8

                @block.sync
                def _(sync, tt=tt):
                    dv.wait(sync)
                    st.wait(sync)
                    ld.dma(sync.dma_start(
                        out=xb2[:], in_=io["x_my"][tt * 128:(tt + 1) * 128, :]))

                @block.gpsimd
                def _(g2x, tt=tt):
                    dv.wait(g2x)
                    ws.dma(g2x.dma_start(
                        out=sb2[:], in_=io["s_h"][tt * 128:(tt + 1) * 128, :]))

                @block.scalar
                def _(scalar, i8=i8):
                    dv.wait(scalar)
                    sc.inc(scalar.activation(out=gb2[:], in_=o2f[i8][:],
                                             func=AF.Silu))

                @block.vector
                def _(vector, i8=i8):
                    sc.wait(vector)
                    ld.wait(vector)
                    ws.wait(vector)
                    dv.wait(vector)
                    dv.inc(vector.tensor_tensor(out=gb2[:], in0=sb2[:],
                                                in1=gb2[:], op=ALU.mult))
                    dv.wait(vector)
                    dv.inc(vector.tensor_tensor(out=xb2[:], in0=xb2[:],
                                                in1=gb2[:], op=ALU.add))

                @block.sync
                def _(sync, tt=tt):
                    dv.wait(sync)
                    st.dma(sync.dma_start(
                        out=io["x1"][tt * 128:(tt + 1) * 128, :], in_=xb2[:]))

            kk.barrier()



def build_l0_program(debug=False):
    nc = bass.Bass("TRN2", target_bir_lowering=False)
    d = {}
    ein = lambda nm, sh, dt: nc.dram_tensor(nm, sh, dt, kind="ExternalInput").ap()
    d["x_my"] = ein("x_my", [2048, 2048], F32)
    d["x_halo"] = ein("x_halo", [512, 2048], F32)
    d["wqt"] = ein("wqt", [2048, 2048], F32)
    d["wkt"] = ein("wkt", [2048, 512], F32)
    d["wvt"] = ein("wvt", [2048, 512], F32)
    d["wot"] = ein("wot", [2048, 2048], F32)
    d["qgain"] = ein("qgain", [1, 16], F32)
    d["haloneg"] = ein("haloneg", [1, 2560], BF16)
    d["coskey"] = ein("coskey", [2560, 64], F32)
    d["sinkey"] = ein("sinkey", [2560, 64], F32)
    d["ident"] = ein("ident", [128, 128], BF16)
    d["ones_bf"] = ein("ones_bf", [128, 128], BF16)
    d["ones_f"] = ein("ones_f", [1, 128], F32)
    d["m_first"] = ein("m_first", [128, 128], BF16)
    d["m_last"] = ein("m_last", [128, 128], BF16)
    d["identf"] = ein("identf", [128, 128], F32)
    d["colnegc"] = ein("colnegc", [128, 20], F32)
    d["mt_l0"] = ein("mt_l0", [8, 128, 512], BF16)
    d["dg"] = ein("dg", [2048, 16384], BF16)
    d["du"] = ein("du", [2048, 16384], BF16)
    d["dd"] = ein("dd", [16384, 2048], BF16)
    d["x1"] = nc.dram_tensor("x1", [2048, 2048], F32, kind="ExternalOutput").ap()
    kind = "ExternalOutput" if debug else "Internal"
    if debug:
        d["xnt_h"] = nc.dram_tensor("xnt_h", [2048, 2560], BF16,
                                    kind="ExternalOutput").ap()
    d["yt_h"] = nc.dram_tensor("yt_h", [2048, 2048], BF16, kind=kind).ap()
    d["inter_h"] = nc.dram_tensor("inter_h", [16384, 2048], BF16,
                                  kind=kind).ap()
    d["s_h"] = nc.dram_tensor("s_h", [2048, 2048], BF16, kind=kind).ap()

    from contextlib import ExitStack
    with ExitStack() as es:
        block = es.enter_context(nc.Block())
        kk = K(nc, block)
        emit_layer0(nc, es, block, kk, d)
        kk.barrier()
    return nc


def perm128_rows(a):
    n = a.shape[0]
    idx = []
    for b0 in range(0, n, 128):
        idx += list(range(b0, b0 + 128, 2)) + list(range(b0 + 1, b0 + 128, 2))
    return a[np.array(idx)]


def ref_dequant_np(W_packed, scales, block=32):
    LUT = np.array([0.0, 0.5, 1.0, 1.5, 2.0, 3.0, 4.0, 6.0,
                    0.0, -0.5, -1.0, -1.5, -2.0, -3.0, -4.0, -6.0], np.float32)
    N = W_packed.shape[0]
    low = W_packed & 15
    high = (W_packed >> 4) & 15
    idx = np.stack([low, high], axis=-1).reshape(N, -1)
    vals = LUT[idx]
    Kd = vals.shape[1]
    vals = vals.reshape(N, Kd // block, block) * scales[:, :, None].astype(np.float32)
    return vals.reshape(N, Kd)


def l0_host_inputs(core, x_full, Wq, Wk, Wv, Wo, q_gain, dg, du, dd):
    b, half = core // 2, core % 2
    xb = np.asarray(x_full[b], np.float32)
    x_my = np.ascontiguousarray(xb[half * 2048:(half + 1) * 2048])
    if half == 1:
        x_halo = np.ascontiguousarray(xb[2048 - 512:2048])
        halopos = np.arange(2048 - 512, 2048)
        hneg = np.zeros(2560, np.float32)
    else:
        x_halo = np.zeros((512, 2048), np.float32)
        halopos = np.zeros(512, np.int64)
        hneg = np.zeros(2560, np.float32)
        hneg[:512] = NEG
    ownpos = np.arange(half * 2048, (half + 1) * 2048)
    keypos = np.concatenate([halopos, ownpos])
    ck, sk = l0_rope_tables(keypos)
    m_first, m_last = mask_tiles()
    return dict(
        x_my=x_my, x_halo=x_halo,
        wqt=host_T(Wq), wkt=host_T(Wk), wvt=host_T(Wv), wot=host_T(Wo),
        qgain=np.asarray(q_gain, np.float32).reshape(1, 16),
        haloneg=hneg.reshape(1, 2560).astype(ml_dtypes.bfloat16),
        coskey=ck, sinkey=sk,
        ident=np.eye(128, dtype=ml_dtypes.bfloat16),
        identf=np.eye(128, dtype=np.float32),
        ones_bf=np.ones((128, 128), ml_dtypes.bfloat16),
        ones_f=np.ones((1, 128), np.float32),
        m_first=m_first, m_last=m_last,
        colnegc=np.ascontiguousarray(hneg.reshape(20, 128).T).astype(np.float32),
        mt_l0=host_attn_masks_l0(),
        dg=dg, du=du, dd=dd,
    )


def host_decode_ffn(gate_W_packed, gate_scales, up_W_packed, up_scales,
                    down_W_packed, down_scales):
    Wg = ref_dequant_np(np.asarray(gate_W_packed), np.asarray(gate_scales))
    Wu = ref_dequant_np(np.asarray(up_W_packed), np.asarray(up_scales))
    Wd = ref_dequant_np(np.asarray(down_W_packed), np.asarray(down_scales))
    dg = perm128_rows(np.ascontiguousarray(Wg.T)).astype(ml_dtypes.bfloat16)
    du = perm128_rows(np.ascontiguousarray(Wu.T)).astype(ml_dtypes.bfloat16)
    dd = perm128_rows(np.ascontiguousarray(Wd.T)).astype(ml_dtypes.bfloat16)
    return dg, du, dd



def emit_layer1a(nc, es, block, kk, io):
    from contextlib import ExitStack
    ld, ws, pe, dv, sc, st, gs = (kk.ld, kk.ws, kk.pe, kk.dv, kk.sc, kk.st,
                                  kk.gs)
    sbuf = lambda nm, shape, dt: nc.sbuf_tensor("sb_" + nm, shape, dt)

    ident = es.enter_context(sbuf("ident", [128, 128], BF16))
    onesb = es.enter_context(sbuf("onesb", [128, 128], BF16))
    mlast = es.enter_context(sbuf("mlast", [128, 128], BF16))
    colneg = es.enter_context(sbuf("colneg", [1, 4096], BF16))
    cosk = es.enter_context(sbuf("cosk", [128, 32 * 64], BF16))
    sink = es.enter_context(sbuf("sink", [128, 32 * 64], BF16))
    bgb = es.enter_context(sbuf("bgb", [1, 8], BF16))
    beb = es.enter_context(sbuf("beb", [1, 2048], BF16))

    @block.sync
    def _(sync):
        ld.dma(sync.dma_start(out=ident[:], in_=io["ident"]))
        ld.dma(sync.dma_start(out=onesb[:], in_=io["ones_bf"]))
        ld.dma(sync.dma_start(out=mlast[:], in_=io["m_last"]))
        ld.dma(sync.dma_start(out=colneg[:], in_=io["colneg"]))
        ld.dma(sync.dma_start(out=bgb[:], in_=io["bg_bf"]))
        ld.dma(sync.dma_start(out=beb[:], in_=io["be_bf"]))

    @block.gpsimd
    def _(g):
        ws.dma(g.dma_start(
            out=cosk[:], in_=io["cos64"].rearrange("(a p) c -> p a c", p=128)))
        ws.dma(g.dma_start(
            out=sink[:], in_=io["sin64"].rearrange("(a p) c -> p a c", p=128)))

    kk.barrier()
    xnt = [es.enter_context(sbuf(f"xnt{ck}", [128, 2048], BF16))
           for ck in range(16)]

    def rope_l1_tile(qtf, sq4, rr, cidx, scale, dst_bf):
        @block.vector
        def _(vector):
            dv.wait(vector)
            sc.wait(vector)
            c_b = cosk[:, cidx * 64:(cidx + 1) * 64].unsqueeze(1).unsqueeze(1) \
                .broadcast_to([128, 4, 2, 64])
            s_b = sink[:, cidx * 64:(cidx + 1) * 64].unsqueeze(1).unsqueeze(1) \
                .broadcast_to([128, 4, 2, 64])
            qv = qtf[:].rearrange("p (h a i) -> p h a i", h=4, a=2)
            ra = rr[0][:].rearrange("p (h a i) -> p h a i", h=4, a=2)
            rb = rr[1][:].rearrange("p (h a i) -> p h a i", h=4, a=2)
            dv.inc(vector.tensor_tensor(out=ra, in0=qv, in1=c_b, op=ALU.mult))
            dv.inc(vector.tensor_tensor(out=rb, in0=qv, in1=s_b, op=ALU.mult))
            dv.wait(vector)
            pre = sq4[:].rearrange("p (h a i) -> p h a i", h=4, a=2)
            dv.inc(vector.scalar_tensor_tensor(
                out=pre[:, :, 0:1, :], in0=ra[:, :, 0:1, :], scalar=1.0,
                in1=rb[:, :, 1:2, :], op0=ALU.mult, op1=ALU.subtract))
            dv.inc(vector.scalar_tensor_tensor(
                out=pre[:, :, 1:2, :], in0=ra[:, :, 1:2, :], scalar=1.0,
                in1=rb[:, :, 0:1, :], op0=ALU.mult, op1=ALU.add))
            dv.wait(vector)
            dv.inc(vector.tensor_scalar(out=dst_bf[:], in0=sq4[:],
                                        scalar1=scale, scalar2=None,
                                        op0=ALU.mult))

    NKT = 32
    with ExitStack() as s1:
        k_dm = [s1.enter_context(sbuf(f"kdm{e}", [128, 4096], BF16))
                for e in range(4)]
        v_tm = [s1.enter_context(sbuf(f"vtm{i}", [128, 512], BF16))
                for i in range(NKT)]
        wkb = [s1.enter_context(sbuf(f"wkb{ck}", [128, 512], BF16))
               for ck in range(16)]
        epst = s1.enter_context(sbuf("epst", [128, 1], F32))
        ktf = s1.enter_context(sbuf("ktf", [128, 512], F32))
        sq4 = s1.enter_context(sbuf("sq4", [128, 512], F32))
        rr = [s1.enter_context(sbuf(f"rr{i}", [128, 512], BF16)) for i in range(2)]
        tb = s1.enter_context(sbuf("tb", [128, 512], BF16))

        @block.vector
        def _(vector):
            dv.inc(vector.memset(epst[:, 0:1], EPS_LN))

        with ExitStack() as sp:
            ptr = [sp.enter_context(nc.psum_tensor(f"l1p{i}", [128, 128], BF16))
                   for i in range(2)]
            pk = [sp.enter_context(nc.psum_tensor(f"l1k{i}", [128, 512], F32))
                  for i in range(2)]
            ptk = sp.enter_context(nc.psum_tensor("l1ptk", [128, 128], BF16))
            wvb = [sp.enter_context(sbuf(f"wvb{ck}", [128, 512], BF16))
                   for ck in range(16)]

            @block.gpsimd
            def _(g):
                for ck in range(16):
                    ws.dma(g.dma_start(out=wkb[ck][:],
                                       in_=io["wk"][ck * 128:(ck + 1) * 128, :]))
                    ws.dma(g.dma_start(out=wvb[ck][:],
                                       in_=io["wv"][ck * 128:(ck + 1) * 128, :]))
            xb0 = sp.enter_context(sbuf("xb0", [128, 2048], F32))
            xb = [xb0, xb0]
            xsq = sp.enter_context(sbuf("xsq", [128, 2048], BF16))
            xnb = sp.enter_context(sbuf("xnb", [128, 2048], BF16))
            stat = sp.enter_context(sbuf("stat", [128, 6], F32))
            xntt = [sp.enter_context(sbuf(f"xntt{ck}", [128, 128], BF16))
                    for ck in range(16)]

            state1 = {"xnb_sn": None, "t15": None}
            for kt in range(NKT):
                own = kt >= 16
                src = io["x1_my"] if own else io["x1_ctx"]
                r0 = (kt - 16) * 128 if own else kt * 128
                buf = xb[kt % 2]

                @block.sync
                def _(sync, src=src, r0=r0, buf=buf, kt=kt):
                    if state1["xnb_sn"] is not None:
                        dv.wait_at(sync, state1["xnb_sn"])
                    ld.dma(sync.dma_start(out=buf[:], in_=src[r0:r0 + 128, :]))

                @block.vector
                def _(vector, buf=buf):
                    ld.wait(vector)
                    sc.wait(vector)
                    dv.inc(vector.reduce_sum(stat[:, 0:1], buf[:], axis=AX.X))
                    dv.inc(vector.tensor_tensor(out=xsq[:], in0=buf[:],
                                                in1=buf[:], op=ALU.mult))
                    dv.wait(vector)
                    dv.inc(vector.reduce_sum(stat[:, 1:2], xsq[:], axis=AX.X))
                    dv.inc(vector.tensor_scalar(out=stat[:, 2:3], in0=stat[:, 0:1],
                                                scalar1=1.0 / 2048.0, scalar2=None,
                                                op0=ALU.mult))
                    dv.wait(vector)
                    dv.inc(vector.tensor_scalar(out=stat[:, 3:4], in0=stat[:, 1:2],
                                                scalar1=1.0 / 2048.0, scalar2=None,
                                                op0=ALU.mult))
                    dv.inc(vector.tensor_tensor(out=stat[:, 4:5], in0=stat[:, 2:3],
                                                in1=stat[:, 2:3], op=ALU.mult))
                    dv.wait(vector)
                    dv.inc(vector.tensor_tensor(out=stat[:, 5:6], in0=stat[:, 3:4],
                                                in1=stat[:, 4:5], op=ALU.subtract))

                @block.scalar
                def _(scalar):
                    dv.wait(scalar)
                    sc.inc(scalar.activation(out=stat[:, 4:5], in_=stat[:, 5:6],
                                             func=AF.Ln, bias=epst[:, 0:1]))
                    sc.wait(scalar)
                    sc.inc(scalar.activation(out=stat[:, 5:6], in_=stat[:, 4:5],
                                             func=AF.Exp, scale=-0.5))

                @block.vector
                def _(vector, buf=buf):
                    sc.wait(vector)
                    if state1["t15"] is not None:
                        pe.wait_at(vector, state1["t15"])
                    dv.inc(vector.tensor_scalar(out=xnb[:], in0=buf[:],
                                                scalar1=stat[:, 2:3],
                                                scalar2=stat[:, 5:6],
                                                op0=ALU.subtract, op1=ALU.mult))
                xnb_sn = dv.snap()
                state1["xnb_sn"] = xnb_sn

                tsn = {}
                xcp_sn = {}
                for ck in range(16):
                    @block.tensor
                    def _(tensor, ck=ck, xnb_sn=xnb_sn):
                        if ck == 0:
                            dv.wait_at(tensor, xnb_sn)
                        elif ck >= 2:
                            dv.wait_at(tensor, xcp_sn[ck - 2])
                        pe.inc(tensor.transpose(ptr[ck % 2][:],
                                                xnb[:, ck * 128:(ck + 1) * 128],
                                                ident[:]))
                        tsn[ck] = pe.snap()

                    @block.vector
                    def _(vector, ck=ck, kt=kt, own=own):
                        pe.wait_at(vector, tsn[ck])
                        dv.inc(vector.tensor_copy(xntt[ck][:], ptr[ck % 2][:]))
                        if own:
                            dv.inc(vector.tensor_copy(
                                xnt[ck][:, (kt - 16) * 128:(kt - 15) * 128],
                                ptr[ck % 2][:]))
                        xcp_sn[ck] = dv.snap()
                state1["t15"] = tsn[15]

                @block.tensor
                def _(tensor, kt=kt):
                    ws.wait(tensor)
                    dv.wait_at(tensor, xcp_sn[15])
                    for ck in range(16):
                        inst = tensor.matmul(pk[0][:], xntt[ck][:], wkb[ck][:],
                                             start=(ck == 0), stop=(ck == 15))
                    pe.inc(inst)
                kmm_sn = pe.snap()

                @block.tensor
                def _(tensor, kt=kt):
                    for ck in range(16):
                        inst = tensor.matmul(pk[1][:], xntt[ck][:], wvb[ck][:],
                                             start=(ck == 0), stop=(ck == 15))
                    pe.inc(inst)
                vmm_sn = pe.snap()

                @block.vector
                def _(vector, kt=kt, kmm_sn=kmm_sn, vmm_sn=vmm_sn):
                    pe.wait_at(vector, vmm_sn)
                    dv.inc(vector.tensor_copy(v_tm[kt][:], pk[1][:]))
                    dv.inc(vector.tensor_copy(ktf[:], pk[0][:]))
                rope_l1_tile(ktf, sq4, rr, kt, 1.0, tb)
                rope_sn = dv.snap()

                ksn = {}
                kcp_sn = {}
                for e in range(4):
                    @block.tensor
                    def _(tensor, e=e, rope_sn=rope_sn):
                        if e == 0:
                            dv.wait_at(tensor, rope_sn)
                        else:
                            dv.wait_at(tensor, kcp_sn[e - 1])
                        pe.inc(tensor.transpose(ptk[:], tb[:, e * 128:(e + 1) * 128],
                                                ident[:]))
                        ksn[e] = pe.snap()

                    @block.vector
                    def _(vector, e=e, kt=kt):
                        pe.wait_at(vector, ksn[e])
                        dv.inc(vector.tensor_copy(
                            k_dm[e][:, kt * 128:(kt + 1) * 128], ptk[:]))
                        kcp_sn[e] = dv.snap()

        kk.barrier()
        q_dm2 = [s1.enter_context(sbuf(f"qd2_{i}", [128, 512], BF16))
                 for i in range(4)]
        colnegc = s1.enter_context(sbuf("colnegc", [128, 32], F32))
        mt_l1 = s1.enter_context(sbuf("mt_l1", [128, 4 * 512], BF16))
        identf = s1.enter_context(sbuf("identf", [128, 128], F32))
        onesf2 = s1.enter_context(sbuf("onesf2", [1, 128], F32))
        mlastT = s1.enter_context(sbuf("mlastT", [128, 128], BF16))

        @block.sync
        def _(sync):
            ld.dma(sync.dma_start(out=colnegc[:],
                                  in_=io["colnegc"]))
            for v_ in range(4):
                ld.dma(sync.dma_start(
                    out=mt_l1[:, v_ * 512:(v_ + 1) * 512],
                    in_=io["mt_l1"][v_]))
            ld.dma(sync.dma_start(out=identf[:], in_=io["identf"]))
            ld.dma(sync.dma_start(out=onesf2[:], in_=io["ones_f"]))

        with ExitStack() as spq:
            pq2 = spq.enter_context(nc.psum_tensor("l1pq2", [128, 512], F32))
            ptr2 = spq.enter_context(nc.psum_tensor("l1ptr2", [128, 128], BF16))

            stq = {"kcp": None, "qcp": None}

            def fill_q(hg, tg):
                if tg == 0:
                    @block.gpsimd
                    def _(g2, hg=hg):
                        dv.wait(g2)
                        pe.wait(g2)
                        for ck in range(16):
                            ws.dma(g2.dma_start(
                                out=wkb[ck][:],
                                in_=io["wq"][ck * 128:(ck + 1) * 128,
                                             hg * 512:(hg + 1) * 512]))

                for sub in range(4):
                    tt = tg * 4 + sub

                    @block.tensor
                    def _(tensor, tt=tt):
                        ws.wait(tensor)
                        if stq["kcp"] is not None:
                            dv.wait_at(tensor, stq["kcp"])
                        else:
                            dv.wait(tensor)
                        for ck in range(16):
                            inst = tensor.matmul(
                                pq2[:], xnt[ck][:, tt * 128:(tt + 1) * 128],
                                wkb[ck][:], start=(ck == 0), stop=(ck == 15))
                        pe.inc(inst)
                    qmm_sn = pe.snap()

                    @block.vector
                    def _(vector, tt=tt, qmm_sn=qmm_sn):
                        pe.wait_at(vector, qmm_sn)
                        dv.inc(vector.tensor_copy(ktf[:], pq2[:]))
                    stq["kcp"] = dv.snap()
                    rope_l1_tile(ktf, sq4, rr, 16 + tt, 0.08838834764831845, tb)
                    rq_sn = dv.snap()

                    qtsn = {}
                    qcsn = {}
                    for e in range(4):
                        @block.tensor
                        def _(tensor, e=e, rq_sn=rq_sn):
                            if e == 0:
                                dv.wait_at(tensor, rq_sn)
                            else:
                                dv.wait_at(tensor, qcsn[e - 1])
                            pe.inc(tensor.transpose(
                                ptr2[:], tb[:, e * 128:(e + 1) * 128], ident[:]))
                            qtsn[e] = pe.snap()

                        @block.vector
                        def _(vector, e=e, sub=sub):
                            pe.wait_at(vector, qtsn[e])
                            dv.inc(vector.tensor_copy(
                                q_dm2[e][:, sub * 128:(sub + 1) * 128], ptr2[:]))
                            qcsn[e] = dv.snap()

            def yt_store(h, qg, ytile):
                @block.gpsimd
                def _(g2, h=h, qg=qg, ytile=ytile):
                    dv.wait(g2)
                    gs.dma(g2.dma_start(
                        out=io["yt_h"][h * 128:(h + 1) * 128,
                                       qg * 512:(qg + 1) * 512],
                        in_=ytile[:]))

            kk.barrier()
            import os as _os
            dbgcfg = {}
            if _os.environ.get("L1_DBG2") == "1":
                dbgcfg["dbg"] = io["dbgrow"]
                dbgcfg["dbg_py"] = s1.enter_context(sbuf("dbgpy", [128, 512], F32))
                dbgcfg["dbg_pyh"] = io["dbgpy"]
                dbgcfg["dbg_et"] = io["dbget"]
            emit_attn2(nc, s1, block, kk, dict(
                mode="l1", tag="A", nkt=32, k_dm=k_dm, v_tm=v_tm, q_dm=q_dm2,
                onesb=onesb, onesf=onesf2, ident=ident, identf=identf,
                colneg_row=colneg[:], colnegc=colnegc,
                mrow_first=None, mrow_last=mlast,
                mt_tiles=mt_l1, fill_q=fill_q, yt_store=yt_store, **dbgcfg))

    kk.barrier()
    with ExitStack() as s4:
        wo_sb = [s4.enter_context(sbuf(f"wo{ck}", [128, 2048], BF16))
                 for ck in range(16)]
        ytb = [s4.enter_context(sbuf(f"ytb{i}", [128, 128], BF16))
               for i in range(2)]
        stile = s4.enter_context(sbuf("stile4", [128, 2048], BF16))
        ps4 = [s4.enter_context(nc.psum_tensor(f"l1o{i}", [128, 512], F32))
               for i in range(4)]

        @block.gpsimd
        def _(g2):
            dv.wait(g2)
            for ck in range(16):
                ws.dma(g2.dma_start(out=wo_sb[ck][:],
                                    in_=io["wo"][ck * 128:(ck + 1) * 128, :]))

        for tt in range(16):
            for ck in range(16):
                @block.gpsimd
                def _(g2, tt=tt, ck=ck):
                    ws.wait(g2)
                    pe.wait(g2)
                    ws.dma(g2.dma_start(
                        out=ytb[ck % 2][:],
                        in_=io["yt_h"][ck * 128:(ck + 1) * 128,
                                       tt * 128:(tt + 1) * 128]))

                @block.tensor
                def _(tensor, tt=tt, ck=ck):
                    ws.wait(tensor)
                    if ck == 0:
                        dv.wait(tensor)
                    for et in range(4):
                        inst = tensor.matmul(
                            ps4[et][:], ytb[ck % 2][:],
                            wo_sb[ck][:, et * 512:(et + 1) * 512],
                            start=(ck == 0), stop=(ck == 15))
                    pe.inc(inst)

            @block.vector
            def _(vector):
                pe.wait(vector)
                st.wait(vector)
                for et in range(4):
                    dv.inc(vector.tensor_copy(
                        stile[:, et * 512:(et + 1) * 512], ps4[et][:]))

            @block.sync
            def _(sync, tt=tt):
                dv.wait(sync)
                st.dma(sync.dma_start(
                    out=io["s1"][tt * 128:(tt + 1) * 128, :], in_=stile[:]))

    kk.barrier()
    with ExitStack() as s5:
        we_sb = [s5.enter_context(sbuf(f"we{ck}", [128, 2048], BF16))
                 for ck in range(16)]
        wg_sb = s5.enter_context(sbuf("wg", [128, 8 * 16], BF16))
        ytile = s5.enter_context(sbuf("ytile", [128, 2048], BF16))
        lg = s5.enter_context(sbuf("lg", [128, 8], F32))
        lge = s5.enter_context(sbuf("lge", [128, 8], F32))
        m8 = s5.enter_context(sbuf("m8", [128, 8], F32))
        i8 = s5.enter_context(sbuf("i8", [128, 8], U32))
        besti = s5.enter_context(sbuf("besti", [128, 1], I32))
        bp = s5.enter_context(sbuf("bp", [128, 2], F32))
        pyv = [s5.enter_context(nc.psum_tensor(f"l1yv{i}", [128, 512], F32))
               for i in range(4)]
        plg = s5.enter_context(nc.psum_tensor("l1lg", [128, 8], F32))

        @block.gpsimd
        def _(g2):
            dv.wait(g2)
            for ck in range(16):
                ws.dma(g2.dma_start(out=we_sb[ck][:],
                                    in_=io["we"][ck * 128:(ck + 1) * 128, :]))
                ws.dma(g2.dma_start(out=wg_sb[:, ck * 8:(ck + 1) * 8],
                                    in_=io["wg"][ck * 128:(ck + 1) * 128, :]))

        for tt in range(16):
            @block.tensor
            def _(tensor, tt=tt):
                ws.wait(tensor)
                dv.wait(tensor)
                for ck in range(16):
                    for et in range(4):
                        tensor.matmul(
                            pyv[et][:], xnt[ck][:, tt * 128:(tt + 1) * 128],
                            we_sb[ck][:, et * 512:(et + 1) * 512],
                            start=(ck == 0), stop=False)
                    inst = tensor.matmul(
                        plg[:], xnt[ck][:, tt * 128:(tt + 1) * 128],
                        wg_sb[:, ck * 8:(ck + 1) * 8],
                        start=(ck == 0), stop=False)
                for et in range(4):
                    tensor.matmul(pyv[et][:], onesb[0:1, :],
                                  beb[0:1, et * 512:(et + 1) * 512],
                                  start=False, stop=True)
                inst = tensor.matmul(plg[:], onesb[0:1, :], bgb[0:1, :],
                                     start=False, stop=True)
                pe.inc(inst)

            @block.vector
            def _(vector, tt=tt):
                pe.wait(vector)
                st.wait(vector)
                dv.wait(vector)
                for et in range(4):
                    dv.inc(vector.tensor_copy(
                        ytile[:, et * 512:(et + 1) * 512], pyv[et][:]))
                dv.inc(vector.tensor_copy(lg[:], plg[:]))
                dv.wait(vector)
                dv.inc(vector.reduce_max(bp[:, 0:1], lg[:], axis=AX.X,
                                         negate=True))
                dv.inc(vector.max(m8[:], lg[:]))
                dv.wait(vector)
                dv.inc(vector.max_index(i8[:], m8[:], lg[:]))

            @block.scalar
            def _(scalar):
                dv.wait(scalar)
                sc.inc(scalar.activation(out=lge[:], in_=lg[:], func=AF.Exp,
                                         bias=bp[:, 0:1]))

            @block.vector
            def _(vector, tt=tt):
                sc.wait(vector)
                dv.inc(vector.reduce_sum(bp[:, 0:1], lge[:], axis=AX.X))
                dv.wait(vector)
                dv.inc(vector.reciprocal(bp[:, 1:2], bp[:, 0:1]))
                dv.inc(vector.tensor_copy(besti[:], i8[:, 0:1]))

            @block.sync
            def _(sync, tt=tt):
                dv.wait(sync)
                st.dma(sync.dma_start(out=io["yv"][tt * 128:(tt + 1) * 128, :],
                                      in_=ytile[:]))
                st.dma(sync.dma_start(out=io["best"][tt * 128:(tt + 1) * 128, :],
                                      in_=besti[:]))
                st.dma(sync.dma_start(out=io["bestp"][tt * 128:(tt + 1) * 128, :],
                                      in_=bp[:, 1:2]))



def emit_layer1b(nc, es, block, kk, io):
    from contextlib import ExitStack
    ld, ws, pe, dv, sc, st, gs = (kk.ld, kk.ws, kk.pe, kk.dv, kk.sc, kk.st,
                                  kk.gs)
    sbuf = lambda nm, shape, dt: nc.sbuf_tensor("sb_" + nm, shape, dt)

    onesb = es.enter_context(sbuf("onesb", [128, 128], BF16))
    onesf = es.enter_context(sbuf("onesf", [1, 128], F32))
    tri = es.enter_context(sbuf("tri", [128, 128], BF16))
    bmat = es.enter_context(sbuf("bmat", [128, 128], F32))
    coreoff = es.enter_context(sbuf("coreoff", [128, 1], F32))
    iota8 = es.enter_context(sbuf("iota8", [128, 8], I32))
    iota8f = es.enter_context(sbuf("iota8f", [128, 8], F32))
    onh = es.enter_context(sbuf("onh", [128, 1024], BF16))
    onhf = es.enter_context(sbuf("onhf", [128, 1024], F32))
    totrow = es.enter_context(sbuf("totrow", [1, 1024], F32))
    totmat = es.enter_context(sbuf("totmat", [128, 8], BF16))
    cbsb = es.enter_context(sbuf("cbsb", [128, 8], F32))
    cbrow = es.enter_context(sbuf("cbrow", [1, 1024], F32))
    basep = es.enter_context(sbuf("basep", [1, 24], F32))
    combbc = es.enter_context(sbuf("combbc", [128, 1024], F32))
    rankf = es.enter_context(sbuf("rankf", [128, 1024], F32))
    rankm = es.enter_context(sbuf("rankm", [128, 128], F32))
    ranki = es.enter_context(sbuf("ranki", [128, 128], I32))
    jmat = es.enter_context(sbuf("jmat", [128, 128], I32))
    valid = es.enter_context(sbuf("valid", [128, 128], F32))
    tmpm = es.enter_context(sbuf("tmpm", [128, 128], F32))
    ranki_f_dummy = es.enter_context(sbuf("ranksel", [128, 128], F32))

    @block.sync
    def _(sync):
        ld.dma(sync.dma_start(out=onesb[:], in_=io["ones_bf"]))
        ld.dma(sync.dma_start(out=onesf[:], in_=io["ones_f"]))
        ld.dma(sync.dma_start(out=tri[:], in_=io["tri"]))
        ld.dma(sync.dma_start(out=bmat[:], in_=io["bestmat"]))
        ld.dma(sync.dma_start(out=coreoff[:], in_=io["coreoff"]))

    @block.gpsimd
    def _(g):
        g.iota(iota8[:], pattern=[[1, 8]], base=0,
               channel_multiplier=0).then_inc(kk.ws.sem, 1)
        kk.ws.n += 1
        g.iota(jmat[:], pattern=[[128, 128]], base=0,
               channel_multiplier=1).then_inc(kk.ws.sem, 1)
        kk.ws.n += 1

    kk.barrier()
    with ExitStack() as sp:
        ppfx = sp.enter_context(nc.psum_tensor("ppfx", [128, 1024], F32))
        ptot = sp.enter_context(nc.psum_tensor("ptot", [1, 1024], F32))
        pcb = sp.enter_context(nc.psum_tensor("pcb", [128, 8], F32))
        pbc = sp.enter_context(nc.psum_tensor("pbc2", [128, 512], F32))

        @block.vector
        def _(vector):
            ld.wait(vector)
            ws.wait(vector)
            dv.inc(vector.tensor_copy(iota8f[:], iota8[:]))
            dv.wait(vector)
            dv.inc(vector.tensor_tensor(
                out=onh[:].rearrange("p (c e) -> p c e", e=8),
                in0=bmat[:].unsqueeze(2).broadcast_to([128, 128, 8]),
                in1=iota8f[:].unsqueeze(1).broadcast_to([128, 128, 8]),
                op=ALU.is_equal))
            dv.wait(vector)
            dv.inc(vector.tensor_copy(onhf[:], onh[:]))

        @block.tensor
        def _(tensor):
            dv.wait(tensor)
            pe.inc(tensor.matmul(ppfx[:, 0:512], tri[:], onh[:, 0:512],
                                 start=True, stop=True))
            pe.inc(tensor.matmul(ppfx[:, 512:1024], tri[:], onh[:, 512:1024],
                                 start=True, stop=True))
            pe.inc(tensor.matmul(ptot[:, 0:512], onesb[:, 0:1], onh[:, 0:512],
                                 start=True, stop=True))
            inst = tensor.matmul(ptot[:, 512:1024], onesb[:, 0:1],
                                 onh[:, 512:1024], start=True, stop=True)
            pe.inc(inst)

        @block.vector
        def _(vector):
            pe.wait(vector)
            dv.inc(vector.tensor_copy(totrow[:], ptot[:]))

        @block.sync
        def _(sync):
            dv.wait(sync)
            ld.dma(sync.dma_start(out=io["trow_h"], in_=totrow[:]))

        @block.gpsimd
        def _(g):
            ld.wait(g)
            ws.dma(g.dma_start(
                out=totmat[:],
                in_=io["trow_h"].rearrange("o (c e) -> (o c) e", e=8)))

        @block.tensor
        def _(tensor):
            ws.wait(tensor)
            pe.inc(tensor.matmul(pcb[:], tri[:], totmat[:], start=True,
                                 stop=True))

        @block.vector
        def _(vector):
            pe.wait(vector)
            dv.inc(vector.tensor_copy(cbsb[:], pcb[:]))
            dv.inc(vector.tensor_reduce(
                out=basep[0:1, 16:24].unsqueeze(2),
                in_=totrow[:].rearrange("o (c e) -> o e c", e=8),
                axis=AX.X, op=ALU.add))
            dv.inc(vector.memset(basep[0:1, 0:16], 0.0))
            dv.wait(vector)
            for sh in (1, 2, 4):
                dv.inc(vector.tensor_tensor(
                    out=basep[0:1, 16:24], in0=basep[0:1, 16:24],
                    in1=basep[0:1, 16 - sh:24 - sh], op=ALU.add))
                dv.wait(vector)

        @block.sync
        def _(sync):
            dv.wait(sync)
            ld.dma(sync.dma_start(out=io["cb_h"], in_=cbsb[:]))
            ld.wait(sync)
            ld.dma(sync.dma_start(
                out=cbrow[:],
                in_=io["cb_h"].rearrange("c e -> (c e)").unsqueeze(0)))

        @block.vector
        def _(vector):
            ld.wait(vector)
            dv.inc(vector.tensor_tensor(
                out=cbrow[:].rearrange("o (c e) -> o c e", e=8),
                in0=cbrow[:].rearrange("o (c e) -> o c e", e=8),
                in1=basep[0:1, 15:23].unsqueeze(1).broadcast_to([1, 128, 8]),
                op=ALU.add))

        for half in range(2):
            @block.tensor
            def _(tensor, half=half):
                dv.wait(tensor)
                pe.inc(tensor.matmul(pbc[:], onesf[:],
                                     cbrow[0:1, half * 512:(half + 1) * 512],
                                     start=True, stop=True))

            @block.vector
            def _(vector, half=half):
                pe.wait(vector)
                dv.inc(vector.tensor_copy(
                    combbc[:, half * 512:(half + 1) * 512], pbc[:]))

        @block.vector
        def _(vector):
            pe.wait(vector)
            dv.wait(vector)
            dv.inc(vector.tensor_tensor(out=rankf[:], in0=ppfx[:], in1=combbc[:],
                                        op=ALU.add))
            dv.wait(vector)
            dv.inc(vector.tensor_tensor(out=onhf[:], in0=onhf[:], in1=rankf[:],
                                        op=ALU.mult))
            dv.wait(vector)
            dv.inc(vector.tensor_reduce(
                out=rankm[:].unsqueeze(2),
                in_=onhf[:].rearrange("p (c e) -> p c e", e=8),
                axis=AX.X, op=ALU.add))
            dv.wait(vector)
            dv.inc(vector.tensor_scalar(out=rankm[:], in0=rankm[:],
                                        scalar1=coreoff[:, 0:1], scalar2=None,
                                        op0=ALU.subtract))
            dv.wait(vector)
            dv.inc(vector.tensor_scalar(out=valid[:], in0=rankm[:], scalar1=0.0,
                                        scalar2=None, op0=ALU.is_ge))
            dv.inc(vector.tensor_scalar(out=tmpm[:], in0=rankm[:], scalar1=2048.0,
                                        scalar2=None, op0=ALU.is_lt))
            dv.wait(vector)
            dv.inc(vector.tensor_tensor(out=valid[:], in0=valid[:], in1=tmpm[:],
                                        op=ALU.mult))
            dv.inc(vector.tensor_scalar(out=tmpm[:], in0=rankm[:],
                                        scalar1=2048.0, scalar2=None,
                                        op0=ALU.subtract))
            dv.wait(vector)
            dv.inc(vector.tensor_tensor(out=ranki_f_dummy[:], in0=valid[:],
                                        in1=tmpm[:], op=ALU.mult))
            dv.wait(vector)
            dv.inc(vector.tensor_scalar(out=ranki_f_dummy[:],
                                        in0=ranki_f_dummy[:], scalar1=2048.0,
                                        scalar2=None, op0=ALU.add))
            dv.wait(vector)
            dv.inc(vector.tensor_copy(ranki[:], ranki_f_dummy[:]))

        ybc = [sp.enter_context(sbuf(f"ybc{i}", [128, 2048], BF16))
               for i in range(2)]
        ypar = [Ctr(nc, "l1by0"), Ctr(nc, "l1by1")]
        scpar = [Ctr(nc, "l1bs0"), Ctr(nc, "l1bs1")]
        ybsn = {}
        scsn = {}
        for c in range(128):
            p_ = c % 2

            @block.sync
            def _(sync, c=c, p_=p_):
                if c >= 2:
                    scpar[p_].wait_at(sync, scsn[c - 2])
                ypar[p_].dma(sync.dma_start(
                    out=ybc[p_][:], in_=io["yfull"][c * 128:(c + 1) * 128, :]))
                ybsn[c] = ypar[p_].snap()

            @block.gpsimd
            def _(g, c=c, p_=p_):
                if c == 0:
                    dv.wait(g)
                ypar[p_].wait_at(g, ybsn[c])
                scpar[p_].dma(g.indirect_dma_start(
                    out=io["g_loc"][:],
                    out_offset=bass.IndirectOffsetOnAxis(
                        ap=ranki[:, c:c + 1], axis=0),
                    in_=ybc[p_][:],
                    in_offset=None))
                scsn[c] = scpar[p_].snap()

        @block.gpsimd
        def _(g):
            scpar[0].wait(g)
            scpar[1].wait(g)
            ws.inc(g.nop())

    kk.barrier()
    with ExitStack() as s2:
        ybuf = s2.enter_context(sbuf("ybuf", [128, 2048], BF16))
        bpcol = s2.enter_context(sbuf("bpcol", [128, 1], F32))
        sb1 = s2.enter_context(sbuf("sb1", [128, 2048], BF16))
        xb1 = s2.enter_context(sbuf("xb1", [128, 2048], F32))
        gb1 = s2.enter_context(sbuf("gb1", [128, 2048], F32))
        for tt in range(16):
            @block.sync
            def _(sync, tt=tt):
                dv.wait(sync)
                st.wait(sync)
                ld.dma(sync.dma_start(
                    out=bpcol[:], in_=io["bestp_my"][tt * 128:(tt + 1) * 128, :]))
                ld.dma(sync.dma_start(
                    out=sb1[:], in_=io["s1_my"][tt * 128:(tt + 1) * 128, :]))
                ld.dma(sync.dma_start(
                    out=xb1[:], in_=io["x1_my"][tt * 128:(tt + 1) * 128, :]))

            @block.gpsimd
            def _(g, tt=tt):
                dv.wait(g)
                ws.dma(g.dma_start(
                    out=ybuf[:],
                    in_=io["g_loc"][tt * 128:(tt + 1) * 128, :]))

            @block.vector
            def _(vector):
                ws.wait(vector)
                ld.wait(vector)
                dv.inc(vector.tensor_scalar(out=gb1[:], in0=ybuf[:],
                                            scalar1=bpcol[:, 0:1], scalar2=None,
                                            op0=ALU.mult))

            @block.scalar
            def _(scalar):
                dv.wait(scalar)
                sc.inc(scalar.activation(out=gb1[:], in_=gb1[:], func=AF.Silu))

            @block.vector
            def _(vector):
                sc.wait(vector)
                dv.inc(vector.tensor_tensor(out=gb1[:], in0=sb1[:], in1=gb1[:],
                                            op=ALU.mult))
                dv.wait(vector)
                dv.inc(vector.tensor_tensor(out=xb1[:], in0=xb1[:], in1=gb1[:],
                                            op=ALU.add))

            @block.sync
            def _(sync, tt=tt):
                dv.wait(sync)
                st.dma(sync.dma_start(out=io["x2"][tt * 128:(tt + 1) * 128, :],
                                      in_=xb1[:]))



def build_l1a_program():
    nc = bass.Bass("TRN2", target_bir_lowering=False)
    d = {}
    ein = lambda nm, sh, dt: nc.dram_tensor(nm, sh, dt, kind="ExternalInput").ap()
    d["x1_my"] = ein("x1_my", [2048, 2048], F32)
    d["x1_ctx"] = ein("x1_ctx", [2048, 2048], F32)
    d["wq"] = ein("wq", [2048, 2048], F32)
    d["wk"] = ein("wk", [2048, 512], F32)
    d["wv"] = ein("wv", [2048, 512], F32)
    d["wo"] = ein("wo", [2048, 2048], F32)
    d["wg"] = ein("wg", [2048, 8], F32)
    d["bg_bf"] = ein("bg_bf", [1, 8], BF16)
    d["we"] = ein("we", [2048, 2048], F32)
    d["be_bf"] = ein("be_bf", [1, 2048], BF16)
    d["cos64"] = ein("cos64", [4096, 64], F32)
    d["sin64"] = ein("sin64", [4096, 64], F32)
    d["colneg"] = ein("colneg", [1, 4096], BF16)
    d["ident"] = ein("ident", [128, 128], BF16)
    d["identf"] = ein("identf", [128, 128], F32)
    d["ones_bf"] = ein("ones_bf", [128, 128], BF16)
    d["ones_f"] = ein("ones_f", [1, 128], F32)
    d["m_last"] = ein("m_last", [128, 128], BF16)
    d["colnegc"] = ein("colnegc", [128, 32], F32)
    d["mt_l1"] = ein("mt_l1", [4, 128, 512], BF16)
    d["s1"] = nc.dram_tensor("s1", [2048, 2048], BF16, kind="ExternalOutput").ap()
    d["yv"] = nc.dram_tensor("yv", [2048, 2048], BF16, kind="ExternalOutput").ap()
    d["best"] = nc.dram_tensor("best", [2048, 1], I32, kind="ExternalOutput").ap()
    d["bestp"] = nc.dram_tensor("bestp", [2048, 1], F32,
                                kind="ExternalOutput").ap()
    import os
    ytk = "ExternalOutput" if os.environ.get("L1_DBG") == "1" else "Internal"
    d["yt_h"] = nc.dram_tensor("yt_h", [2048, 2048], BF16, kind=ytk).ap()
    if os.environ.get("L1_DBG2") == "1":
        d["dbgrow"] = nc.dram_tensor("dbgrow", [3, 512], F32,
                                     kind="ExternalOutput").ap()
        d["dbgpy"] = nc.dram_tensor("dbgpy", [128, 512], F32,
                                    kind="ExternalOutput").ap()
        d["dbget"] = nc.dram_tensor("dbget", [128, 512], BF16,
                                    kind="ExternalOutput").ap()

    from contextlib import ExitStack
    with ExitStack() as es:
        block = es.enter_context(nc.Block())
        kk = K(nc, block)
        emit_layer1a(nc, es, block, kk, d)
        kk.barrier()
    return nc


def build_l1b_program():
    nc = bass.Bass("TRN2", target_bir_lowering=False)
    d = {}
    ein = lambda nm, sh, dt: nc.dram_tensor(nm, sh, dt, kind="ExternalInput").ap()
    d["yfull"] = ein("yfull", [16384, 2048], BF16)
    d["bestmat"] = ein("bestmat", [128, 128], F32)
    d["x1_my"] = ein("x1_my", [2048, 2048], F32)
    d["s1_my"] = ein("s1_my", [2048, 2048], BF16)
    d["bestp_my"] = ein("bestp_my", [2048, 1], F32)
    d["coreoff"] = ein("coreoff", [128, 1], F32)
    d["tri"] = ein("tri", [128, 128], BF16)
    d["ones_bf"] = ein("ones_bf", [128, 128], BF16)
    d["ones_f"] = ein("ones_f", [1, 128], F32)
    d["x2"] = nc.dram_tensor("x2", [2048, 2048], F32, kind="ExternalOutput").ap()
    d["g_loc"] = nc.dram_tensor("g_loc", [2049, 2048], BF16,
                                kind="Internal").ap()
    d["trow_h"] = nc.dram_tensor("trow_h", [1, 1024], F32, kind="Internal").ap()
    d["cb_h"] = nc.dram_tensor("cb_h", [128, 8], F32, kind="Internal").ap()

    from contextlib import ExitStack
    with ExitStack() as es:
        block = es.enter_context(nc.Block())
        kk = K(nc, block)
        emit_layer1b(nc, es, block, kk, d)
        kk.barrier()
    return nc


def l1a_host_inputs(core, x1_full_f32):
    b, half = core // 2, core % 2
    xb = x1_full_f32[b]
    x1_my = np.ascontiguousarray(xb[half * 2048:(half + 1) * 2048])
    if half == 1:
        x1_ctx = np.ascontiguousarray(xb[0:2048])
        ctxpos = np.arange(0, 2048)
        cneg = np.zeros(4096, np.float32)
    else:
        x1_ctx = np.zeros((2048, 2048), np.float32)
        ctxpos = np.zeros(2048, np.int64)
        cneg = np.zeros(4096, np.float32)
        cneg[:2048] = NEG
    ownpos = np.arange(half * 2048, (half + 1) * 2048)
    keypos = np.concatenate([ctxpos, ownpos])
    c64, s64 = l1_rope_tables(keypos)
    _, m_last = mask_tiles()
    return dict(
        x1_my=x1_my, x1_ctx=x1_ctx,
        cos64=np.ascontiguousarray(c64[:, :64]),
        sin64=np.ascontiguousarray(s64[:, :64]),
        colneg=cneg.reshape(1, 4096).astype(ml_dtypes.bfloat16),
        colnegc=np.ascontiguousarray(cneg.reshape(32, 128).T).astype(np.float32),
        mt_l1=host_attn_masks_l1(),
        ident=np.eye(128, dtype=ml_dtypes.bfloat16),
        identf=np.eye(128, dtype=np.float32),
        ones_bf=np.ones((128, 128), ml_dtypes.bfloat16),
        ones_f=np.ones((1, 128), np.float32),
        m_last=m_last,
    )


def l1a_weights(wq, wk, wv, wo, Wg, bg, We, be):
    return dict(
        wq=np.ascontiguousarray(np.asarray(wq, np.float32)),
        wk=np.ascontiguousarray(np.asarray(wk, np.float32)),
        wv=np.ascontiguousarray(np.asarray(wv, np.float32)),
        wo=np.ascontiguousarray(np.asarray(wo, np.float32)),
        wg=np.ascontiguousarray(np.asarray(Wg, np.float32)),
        bg_bf=np.asarray(bg, np.float32).reshape(1, 8).astype(ml_dtypes.bfloat16),
        we=np.ascontiguousarray(np.asarray(We, np.float32)),
        be_bf=np.asarray(be, np.float32).reshape(1, 2048).astype(
            ml_dtypes.bfloat16),
    )


def l1b_host_inputs(core, yfull_bf, best_full_i32, x1_my, s1_my, bestp_my):
    tri = np.triu(np.ones((128, 128), np.float32), 1).astype(ml_dtypes.bfloat16)
    bestmat = best_full_i32.reshape(128, 128).T
    return dict(
        yfull=yfull_bf,
        bestmat=np.ascontiguousarray(bestmat).astype(np.float32),
        x1_my=x1_my, s1_my=s1_my, bestp_my=bestp_my,
        coreoff=np.full((128, 1), 2048.0 * core, np.float32),
        tri=tri,
        ones_bf=np.ones((128, 128), ml_dtypes.bfloat16),
        ones_f=np.ones((1, 128), np.float32),
    )



def build_dec_program():
    nc = bass.Bass("TRN2", target_bir_lowering=False)
    ein = lambda nm, sh, dt: nc.dram_tensor(nm, sh, dt, kind="ExternalInput").ap()
    gb = ein("gb", [128, 16384], U8)
    gsc = ein("gsc", [8, 16384], F32)
    ub = ein("ub", [128, 16384], U8)
    usc = ein("usc", [8, 16384], F32)
    db = ein("db", [1024, 2048], U8)
    dsc = ein("dsc", [64, 2048], F32)
    og = nc.dram_tensor("og", [256, 16384], BF16, kind="ExternalOutput").ap()
    ou = nc.dram_tensor("ou", [256, 16384], BF16, kind="ExternalOutput").ap()
    od = nc.dram_tensor("od", [2048, 2048], BF16, kind="ExternalOutput").ap()
    from contextlib import ExitStack
    with ExitStack() as es:
        block = es.enter_context(nc.Block())
        kk = K(nc, block)
        emit_decode(nc, es, block, kk, "g", 2, 16384, gb, gsc, og)
        kk.barrier()
        emit_decode(nc, es, block, kk, "u", 2, 16384, ub, usc, ou)
        kk.barrier()
        emit_decode(nc, es, block, kk, "d", 16, 2048, db, dsc, od)
        kk.barrier()
    return nc



_PROG_CACHE = {}


def _prog(name, builder):
    if name not in _PROG_CACHE:
        _PROG_CACHE[name] = builder()
    return _PROG_CACHE[name]


def kernel(**inputs):
    from concourse.bass_utils import run_bass_kernel_spmd
    inp = {k: np.asarray(v) for k, v in inputs.items()}

    gbT = host_bytesT(inp["L0_ffn_gate_W_packed"])
    gscT = host_T(inp["L0_ffn_gate_scales"])
    ubT = host_bytesT(inp["L0_ffn_up_W_packed"])
    uscT = host_T(inp["L0_ffn_up_scales"])
    dbT = host_bytesT(inp["L0_ffn_down_W_packed"])
    dscT = host_T(inp["L0_ffn_down_scales"])
    nc = _prog("dec", build_dec_program)
    in_maps = []
    for c in range(8):
        in_maps.append(dict(
            gb=np.ascontiguousarray(gbT[128 * c:128 * (c + 1)]),
            gsc=np.ascontiguousarray(gscT[8 * c:8 * (c + 1)]),
            ub=np.ascontiguousarray(ubT[128 * c:128 * (c + 1)]),
            usc=np.ascontiguousarray(uscT[8 * c:8 * (c + 1)]),
            db=np.ascontiguousarray(dbT[1024 * c:1024 * (c + 1)]),
            dsc=np.ascontiguousarray(dscT[64 * c:64 * (c + 1)]),
        ))
    res = run_bass_kernel_spmd(nc, in_maps, core_ids=list(range(8)))
    dg = np.concatenate([np.asarray(res.results[c]["og"]) for c in range(8)], 0)
    du = np.concatenate([np.asarray(res.results[c]["ou"]) for c in range(8)], 0)
    dd = np.concatenate([np.asarray(res.results[c]["od"]) for c in range(8)], 0)

    nc = _prog("l0", build_l0_program)
    x = inp["L0_attn_x"]
    in_maps = [l0_host_inputs(c, x, inp["L0_attn_Wq"], inp["L0_attn_Wk"],
                              inp["L0_attn_Wv"], inp["L0_attn_Wo"],
                              inp["L0_attn_q_gain"], dg, du, dd)
               for c in range(8)]
    res = run_bass_kernel_spmd(nc, in_maps, core_ids=list(range(8)))
    x1_full = np.concatenate(
        [np.asarray(res.results[c]["x1"]) for c in range(8)], 0
    ).reshape(4, 4096, 2048)

    nc = _prog("l1a", build_l1a_program)
    wts = l1a_weights(inp["L1_attn_wq"], inp["L1_attn_wk"], inp["L1_attn_wv"],
                      inp["L1_attn_wo"], inp["L1_ffn_Wg"], inp["L1_ffn_bg"],
                      inp["L1_ffn_We"], inp["L1_ffn_be"])
    in_maps = [dict(**l1a_host_inputs(c, x1_full), **wts) for c in range(8)]
    res = run_bass_kernel_spmd(nc, in_maps, core_ids=list(range(8)))
    yfull = np.concatenate(
        [np.asarray(res.results[c]["yv"]) for c in range(8)], 0)
    best_full = np.concatenate(
        [np.asarray(res.results[c]["best"]).ravel() for c in range(8)], 0)
    bestp_all = [np.asarray(res.results[c]["bestp"]) for c in range(8)]
    s1_all = [np.asarray(res.results[c]["s1"]) for c in range(8)]

    nc = _prog("l1b", build_l1b_program)
    in_maps = []
    for c in range(8):
        b, half = c // 2, c % 2
        x1_my = np.ascontiguousarray(x1_full[b][half * 2048:(half + 1) * 2048])
        in_maps.append(l1b_host_inputs(c, yfull, best_full.astype(np.int32),
                                       x1_my, s1_all[c], bestp_all[c]))
    res = run_bass_kernel_spmd(nc, in_maps, core_ids=list(range(8)))
    x2 = np.concatenate(
        [np.asarray(res.results[c]["x2"]) for c in range(8)], 0
    ).reshape(4, 4096, 2048)
    return x2



def host_attn_masks_l0():
    i = np.arange(128)[:, None]
    j = np.arange(128)[None, :]
    first = np.where(j >= i + 1, 0.0, NEG).astype(np.float32).T
    last = np.where(j <= i, 0.0, NEG).astype(np.float32).T
    openb = np.zeros((128, 128), np.float32)
    full = np.full((128, 128), NEG, np.float32)
    tiles = []
    for j0 in range(8):
        cols = []
        for sub in range(4):
            r = j0 - sub
            if r == 0:
                cols.append(first)
            elif r == 4:
                cols.append(last)
            elif 1 <= r <= 3:
                cols.append(openb)
            else:
                cols.append(full)
        tiles.append(np.concatenate(cols, 1))
    return np.stack(tiles).astype(ml_dtypes.bfloat16)


def host_attn_masks_l1():
    i = np.arange(128)[:, None]
    j_ = np.arange(128)[None, :]
    last = np.where(j_ <= i, 0.0, NEG).astype(np.float32).T
    openb = np.zeros((128, 128), np.float32)
    full = np.full((128, 128), NEG, np.float32)
    tiles = []
    for jv in range(4):
        cols = []
        for sub in range(4):
            if sub > jv:
                cols.append(openb)
            elif sub == jv:
                cols.append(last)
            else:
                cols.append(full)
        tiles.append(np.concatenate(cols, 1))
    return np.stack(tiles).astype(ml_dtypes.bfloat16)


def emit_attn2(nc, es, block, kk, cfg):
    ld, ws, pe, dv, sc, st, gs = (kk.ld, kk.ws, kk.pe, kk.dv, kk.sc, kk.st,
                                  kk.gs)
    mode = cfg["mode"]
    k_dm, v_tm, q_dm = cfg["k_dm"], cfg["v_tm"], cfg["q_dm"]
    onesb = cfg["onesb"]
    sbuf = lambda nm, shape, dt: nc.sbuf_tensor("at2_" + cfg["tag"] + nm, shape, dt)
    from contextlib import ExitStack
    with ExitStack() as sp:
        etb = [sp.enter_context(sbuf(f"etb{i}", [128, 512], BF16))
               for i in range(3)]
        zrow = sp.enter_context(sbuf("zrow", [1, 512], F32))
        zrowb = sp.enter_context(sbuf("zrowb", [1, 512], F32))
        ivb = sp.enter_context(sbuf("ivb", [128, 512], F32))
        ytile = sp.enter_context(sbuf("ytile", [128, 512], BF16))
        pSt = [sp.enter_context(nc.psum_tensor(f"a2S{cfg['tag']}{i}",
                                               [128, 512], F32))
               for i in range(3)]
        pZ = sp.enter_context(nc.psum_tensor(f"a2Z{cfg['tag']}", [1, 512], F32))
        pY = sp.enter_context(nc.psum_tensor(f"a2Y{cfg['tag']}", [128, 512], F32))
        pB = sp.enter_context(nc.psum_tensor(f"a2B{cfg['tag']}", [128, 512], F32))

        for engn in ("tensor", "vector", "scalar"):
            getattr(block, engn)(lambda eng: ld.wait(eng))

        state = {"norm_sn": None, "zav_last": None}

        for hg in range(4):
          for tg in range(4):
            cfg["fill_q"](hg, tg)
            for hh in range(4):
                h = hg * 4 + hh
                g = hg
                if mode == "l0":
                    blocks = list(range(tg * 4, tg * 4 + 8))
                else:
                    blocks = list(range(0, 16 + tg * 4 + 4))
                nblk = len(blocks)
                st_sn = {}
                msk_sn = {}
                exp_sn = {}
                zav_sn = {}

                def emit_stmm(bi):
                    b = blocks[bi]

                    @block.tensor
                    def _(tensor, b=b, bi=bi):
                        if bi >= 3:
                            sc.wait_at(tensor, exp_sn[bi - 3])
                        elif bi == 0:
                            if hh == 0:
                                dv.wait(tensor)
                            sc.wait(tensor)
                        inst = tensor.matmul(pSt[bi % 3][:],
                                             k_dm[g][:, b * 128:(b + 1) * 128],
                                             q_dm[hh][:, 0:512],
                                             start=True, stop=True)
                        pe.inc(inst)
                        st_sn[bi] = pe.snap()

                for bi in range(min(3, nblk)):
                    emit_stmm(bi)
                for bi, b in enumerate(blocks):
                    if mode == "l0":
                        jv = b - tg * 4
                        do_mask = True
                    else:
                        jv = b - 16 - tg * 4
                        do_mask = 0 <= jv <= 3

                    if do_mask:
                        @block.vector
                        def _(vector, bi=bi, jv=jv):
                            pe.wait_at(vector, st_sn[bi])
                            dv.inc(vector.tensor_tensor(
                                out=pSt[bi % 3][:], in0=pSt[bi % 3][:],
                                in1=cfg["mt_tiles"][:, jv * 512:(jv + 1) * 512],
                                op=ALU.add))
                        msk_sn[bi] = dv.snap()

                    @block.scalar
                    def _(scalar, b=b, bi=bi, do_mask=do_mask):
                        if do_mask:
                            dv.wait_at(scalar, msk_sn[bi])
                        else:
                            pe.wait_at(scalar, st_sn[bi])
                        if bi >= 3:
                            pe.wait_at(scalar, zav_sn[bi - 3])
                        elif bi == 0 and state["zav_last"] is not None:
                            pe.wait_at(scalar, state["zav_last"])
                        sc.inc(scalar.activation(
                            out=etb[bi % 3][:], in_=pSt[bi % 3][:], func=AF.Exp,
                            bias=cfg["colnegc"][:, b:b + 1]))
                    exp_sn[bi] = sc.snap()

                    if bi + 3 < nblk:
                        emit_stmm(bi + 3)

                    @block.tensor
                    def _(tensor, b=b, bi=bi, nblk=nblk):
                        sc.wait_at(tensor, exp_sn[bi])
                        if bi == 0 and state["norm_sn"] is not None:
                            dv.wait_at(tensor, state["norm_sn"])
                        tensor.matmul(pZ[:], onesb[:, 0:1], etb[bi % 3][:],
                                      start=(bi == 0), stop=(bi == nblk - 1))
                        inst = tensor.matmul(
                            pY[:], v_tm[b][:, g * 128:(g + 1) * 128],
                            etb[bi % 3][:],
                            start=(bi == 0), stop=(bi == nblk - 1))
                        pe.inc(inst)
                    zav_sn[bi] = pe.snap()

                state["zav_last"] = zav_sn[nblk - 1]

                @block.vector
                def _(vector):
                    pe.wait_at(vector, zav_sn[nblk - 1])
                    dv.inc(vector.tensor_copy(zrow[:], pZ[:]))
                    dv.wait(vector)
                    dv.inc(vector.reciprocal(zrowb[:], zrow[:]))
                zr_sn = dv.snap()

                @block.tensor
                def _(tensor):
                    dv.wait_at(tensor, zr_sn)
                    pe.inc(tensor.matmul(pB[:], cfg["onesf"][:], zrowb[:],
                                         start=True, stop=True))
                pb_sn = pe.snap()

                @block.vector
                def _(vector):
                    pe.wait_at(vector, pb_sn)
                    gs.wait(vector)
                    dv.inc(vector.tensor_copy(ivb[:], pB[:]))
                    dv.wait(vector)
                    dv.inc(vector.tensor_tensor(out=ytile[:], in0=pY[:],
                                                in1=ivb[:], op=ALU.mult))
                state["norm_sn"] = dv.snap()

                cfg["yt_store"](h, tg, ytile)


# version 11
# speedup vs baseline: 1.0527x; 1.0527x over previous
import numpy as np
import ml_dtypes
import concourse.bass as bass
import concourse.mybir as mybir

F32 = mybir.dt.float32
BF16 = mybir.dt.bfloat16
I32 = mybir.dt.int32
I16 = mybir.dt.int16
U8 = mybir.dt.uint8
U32 = mybir.dt.uint32
AF = mybir.ActivationFunctionType
ALU = mybir.AluOpType
AX = mybir.AxisListType
ET = mybir.EngineType

NEG = -1.0e9
EPS_LN = 1.0e-5
EPS_RMS = 1.1920928955078125e-07


class Ctr:
    def __init__(self, nc, name):
        self.sem = nc.alloc_semaphore(name)
        self.n = 0

    def inc(self, inst, k=1):
        inst.then_inc(self.sem, k)
        self.n += k
        return inst

    def dma(self, inst):
        return self.inc(inst, 16)

    def wait(self, eng):
        eng.wait_ge(self.sem, self.n)

    def snap(self):
        return self.n

    def wait_at(self, eng, val):
        eng.wait_ge(self.sem, val)


class K:

    def __init__(self, nc, block):
        self.nc = nc
        self.block = block
        self.ld = Ctr(nc, "c_ld")
        self.ws = Ctr(nc, "c_ws")
        self.pe = Ctr(nc, "c_pe")
        self.dv = Ctr(nc, "c_dv")
        self.sc = Ctr(nc, "c_sc")
        self.st = Ctr(nc, "c_st")
        self.gs = Ctr(nc, "c_gs")

    def all_ctrs(self):
        return (self.ld, self.ws, self.pe, self.dv, self.sc, self.st, self.gs)

    def barrier(self):
        ctrs = self.all_ctrs()

        def mk(engname):
            def f(eng):
                for c in ctrs:
                    c.wait(eng)
            return f
        self.block.sync(mk("sync"))
        self.block.vector(mk("vector"))
        self.block.scalar(mk("scalar"))
        self.block.tensor(mk("tensor"))
        self.block.gpsimd(mk("gpsimd"))



def host_bytesT(W_packed):
    b = np.ascontiguousarray(W_packed).view(np.uint8)[:, ::4]
    return np.ascontiguousarray(b.T)


def host_T(W):
    return np.ascontiguousarray(np.asarray(W, np.float32).T)


def l0_rope_tables(positions):
    half = 64
    inv_freq = 10000.0 ** (-np.arange(0, half, dtype=np.float32) / half)
    freqs = positions[:, None].astype(np.float32) * inv_freq[None, :]
    return np.cos(freqs).astype(np.float32), np.sin(freqs).astype(np.float32)


def l1_rope_tables(positions):
    hd = 128
    freqs = 1.0 / 10000.0 ** (np.arange(0, hd, 2, dtype=np.float32) / hd)
    ang = positions[:, None].astype(np.float32) * freqs[None, :]
    cos = np.tile(np.cos(ang), (1, 2)).astype(np.float32)
    sin = np.tile(np.sin(ang), (1, 2)).astype(np.float32)
    return cos, sin


def mask_tiles():
    i = np.arange(128)[:, None]
    j = np.arange(128)[None, :]
    m_first = np.where(j >= i + 1, 0.0, NEG).astype(ml_dtypes.bfloat16)
    m_last = np.where(j <= i, 0.0, NEG).astype(ml_dtypes.bfloat16)
    return m_first, m_last


def deint(ap, rows0, c0, c1):
    assert rows0 % 128 == 0
    sl = ap[rows0:rows0 + 128, c0:c1]
    return sl.rearrange("(a i) n -> i a n", a=2)



def emit_decode(nc, es, block, kk, name, n_kchunks, width, bytes_ap, scales_ap,
                out_ap, ntile=2048):
    W = min(ntile, width)
    nslices = width // W
    sb = lambda nm, shape, dt: es.enter_context(
        nc.sbuf_tensor(f"{name}_{nm}", shape, dt))
    t0 = sb("t0", [128, W], U8)
    t = [sb(f"t{i+1}", [128, W], I16) for i in range(8)]
    f1 = sb("f1", [128, W], F32)
    stile = sb("stile", [128, W], F32)
    obuf = sb("obuf", [128, W], BF16)
    spv = sb("spv", [128, 1], I16)
    ld, dv, sc = kk.ld, kk.dv, kk.sc

    @block.vector
    def _(vector):
        dv.inc(vector.memset(spv[0:64, :], 0))
        dv.inc(vector.memset(spv[64:128, :], 4))

    for ck in range(n_kchunks):
        for ns in range(nslices):
            n0 = ns * W
            bsl = bytes_ap[64 * ck:64 * (ck + 1), n0:n0 + W]
            ssl = scales_ap[4 * ck:4 * (ck + 1), n0:n0 + W]

            @block.sync
            def _(sync, bsl=bsl, ssl=ssl):
                dv.wait(sync)
                ld.dma(sync.dma_start(out=t0[0:64, :], in_=bsl))
                ld.dma(sync.dma_start(out=t0[64:128, :], in_=bsl))
                bc = ssl.unsqueeze(1).broadcast_to([4, 16, W])
                ld.dma(sync.dma_start(out=stile[0:64, :], in_=bc))
                ld.dma(sync.dma_start(out=stile[64:128, :], in_=bc))

            @block.vector
            def _(vector):
                ld.wait(vector)
                sc.wait(vector)
                dv.inc(vector.tensor_copy(t[0][:], t0[:]))
                dv.wait(vector)
                dv.inc(vector.tensor_scalar(out=t[1][:], in0=t[0][:],
                                            scalar1=spv[:], scalar2=15,
                                            op0=ALU.logical_shift_right,
                                            op1=ALU.bitwise_and))
                dv.wait(vector)
                dv.inc(vector.tensor_scalar(out=t[2][:], in0=t[1][:], scalar1=1,
                                            scalar2=2, op0=ALU.bitwise_and,
                                            op1=ALU.bitwise_or))
                dv.inc(vector.tensor_scalar(out=t[3][:], in0=t[1][:], scalar1=1,
                                            scalar2=3, op0=ALU.logical_shift_right,
                                            op1=ALU.bitwise_and))
                dv.inc(vector.tensor_scalar(out=t[4][:], in0=t[1][:], scalar1=7,
                                            scalar2=None, op0=ALU.bitwise_and))
                dv.inc(vector.tensor_scalar(out=t[5][:], in0=t[1][:], scalar1=3,
                                            scalar2=None,
                                            op0=ALU.logical_shift_right))
                dv.wait(vector)
                dv.inc(vector.tensor_tensor(out=t[6][:], in0=t[2][:], in1=t[3][:],
                                            op=ALU.logical_shift_left))
                dv.inc(vector.tensor_scalar(out=t[7][:], in0=t[4][:], scalar1=2,
                                            scalar2=None, op0=ALU.min))
                dv.inc(vector.tensor_scalar(out=t[3][:], in0=t[5][:], scalar1=-2,
                                            scalar2=1, op0=ALU.mult, op1=ALU.add))
                dv.wait(vector)
                dv.inc(vector.scalar_tensor_tensor(out=t[0][:], in0=t[6][:],
                                                   scalar=-2.0, in1=t[7][:],
                                                   op0=ALU.add, op1=ALU.add))
                dv.wait(vector)
                dv.inc(vector.tensor_tensor(out=t[1][:], in0=t[0][:], in1=t[3][:],
                                            op=ALU.mult))

            @block.scalar
            def _(scalar):
                dv.wait(scalar)
                sc.inc(scalar.activation(out=f1[:], in_=t[1][:], func=AF.Copy,
                                         scale=0.25))

            @block.vector
            def _(vector):
                sc.wait(vector)
                kk.gs.wait(vector)
                dv.inc(vector.tensor_tensor(out=obuf[:], in0=f1[:], in1=stile[:],
                                            op=ALU.mult))

            osl = out_ap[128 * ck:128 * (ck + 1), n0:n0 + W]

            @block.gpsimd
            def _(g, osl=osl):
                dv.wait(g)
                kk.gs.dma(g.dma_start(out=osl, in_=obuf[:]))



def emit_layer0(nc, es, block, kk, io):
    from contextlib import ExitStack
    ld, ws, pe, dv, sc, st = kk.ld, kk.ws, kk.pe, kk.dv, kk.sc, kk.st
    gs = kk.gs
    sbuf = lambda nm, shape, dt: nc.sbuf_tensor("sb_" + nm, shape, dt)

    ident = es.enter_context(sbuf("ident", [128, 128], BF16))
    onesb = es.enter_context(sbuf("onesb", [128, 128], BF16))
    onesf = es.enter_context(sbuf("onesf", [1, 128], F32))
    mfirst = es.enter_context(sbuf("mfirst", [128, 128], BF16))
    mlast = es.enter_context(sbuf("mlast", [128, 128], BF16))
    qgain = es.enter_context(sbuf("qgain", [1, 16], F32))
    halon = es.enter_context(sbuf("halon", [1, 2560], BF16))
    epst = es.enter_context(sbuf("epst", [128, 2], F32))
    es2 = ExitStack()
    cosk = es2.enter_context(sbuf("cosk", [128, 20 * 64], F32))
    sink = es2.enter_context(sbuf("sink", [128, 20 * 64], F32))
    qgbc = es2.enter_context(sbuf("qgbc", [128, 16], F32))

    @block.vector
    def _(vector):
        dv.inc(vector.memset(epst[:, 0:1], EPS_LN))
        dv.inc(vector.memset(epst[:, 1:2], EPS_RMS))

    @block.sync
    def _(sync):
        ld.dma(sync.dma_start(out=ident[:], in_=io["ident"]))
        ld.dma(sync.dma_start(out=onesb[:], in_=io["ones_bf"]))
        ld.dma(sync.dma_start(out=onesf[:], in_=io["ones_f"]))
        ld.dma(sync.dma_start(out=mfirst[:], in_=io["m_first"]))
        ld.dma(sync.dma_start(out=mlast[:], in_=io["m_last"]))
        ld.dma(sync.dma_start(out=qgain[:], in_=io["qgain"]))
        ld.dma(sync.dma_start(out=halon[:], in_=io["haloneg"]))
        ld.dma(sync.dma_start(
            out=cosk[:], in_=io["coskey"].rearrange("(a p) c -> p a c", p=128)))
        ld.dma(sync.dma_start(
            out=sink[:], in_=io["sinkey"].rearrange("(a p) c -> p a c", p=128)))

    with ExitStack() as sp:
        pg = sp.enter_context(nc.psum_tensor("pqg", [128, 16], F32))

        @block.tensor
        def _(tensor):
            ld.wait(tensor)
            pe.inc(tensor.matmul(pg[:], onesf[:], qgain[:], start=True, stop=True))

        @block.vector
        def _(vector):
            pe.wait(vector)
            dv.inc(vector.tensor_copy(qgbc[:], pg[:]))

    xnt = [es2.enter_context(sbuf(f"xnt{ck}", [128, 2560], BF16))
           for ck in range(16)]

    kk.barrier()
    with ExitStack() as s1:
        xb = [s1.enter_context(sbuf(f"s1x{i}", [128, 2048], F32)) for i in range(2)]
        xsq = s1.enter_context(sbuf("s1sq", [128, 2048], F32))
        xnb = s1.enter_context(sbuf("s1xn", [128, 2048], BF16))
        stat = s1.enter_context(sbuf("s1st", [128, 6], F32))
        ptr = [s1.enter_context(nc.psum_tensor(f"s1p{i}", [128, 128], BF16))
               for i in range(2)]
        sst = {}
        for tt in range(20):
            own = tt < 16
            src = io["x_my"][tt * 128:(tt + 1) * 128, :] if own else \
                io["x_halo"][(tt - 16) * 128:(tt - 15) * 128, :]
            dcol = 512 + tt * 128 if own else (tt - 16) * 128
            buf = xb[tt % 2]

            @block.sync
            def _(sync, src=src, buf=buf, tt=tt):
                if tt >= 2:
                    dv.wait_at(sync, sst[("xnb", tt - 2)])
                ld.dma(sync.dma_start(out=buf[:], in_=src))

            @block.vector
            def _(vector, buf=buf):
                ld.wait(vector)
                sc.wait(vector)
                dv.inc(vector.reduce_sum(stat[:, 0:1], buf[:], axis=AX.X))
                dv.inc(vector.tensor_tensor(out=xsq[:], in0=buf[:], in1=buf[:],
                                            op=ALU.mult))
                dv.wait(vector)
                dv.inc(vector.reduce_sum(stat[:, 1:2], xsq[:], axis=AX.X))
                dv.inc(vector.tensor_scalar(out=stat[:, 2:3], in0=stat[:, 0:1],
                                            scalar1=1.0 / 2048.0, scalar2=None,
                                            op0=ALU.mult))
                dv.wait(vector)
                dv.inc(vector.tensor_scalar(out=stat[:, 3:4], in0=stat[:, 1:2],
                                            scalar1=1.0 / 2048.0, scalar2=None,
                                            op0=ALU.mult))
                dv.inc(vector.tensor_tensor(out=stat[:, 4:5], in0=stat[:, 2:3],
                                            in1=stat[:, 2:3], op=ALU.mult))
                dv.wait(vector)
                dv.inc(vector.tensor_tensor(out=stat[:, 5:6], in0=stat[:, 3:4],
                                            in1=stat[:, 4:5], op=ALU.subtract))

            @block.scalar
            def _(scalar):
                dv.wait(scalar)
                sc.inc(scalar.activation(out=stat[:, 4:5], in_=stat[:, 5:6],
                                         func=AF.Ln, bias=epst[:, 0:1]))
                sc.wait(scalar)
                sc.inc(scalar.activation(out=stat[:, 5:6], in_=stat[:, 4:5],
                                         func=AF.Exp, scale=-0.5))

            @block.vector
            def _(vector, buf=buf, tt=tt):
                sc.wait(vector)
                if tt >= 1:
                    pe.wait_at(vector, sst[("t15", tt - 1)])
                dv.inc(vector.tensor_scalar(out=xnb[:], in0=buf[:],
                                            scalar1=stat[:, 2:3],
                                            scalar2=stat[:, 5:6],
                                            op0=ALU.subtract, op1=ALU.mult))
            sst[("xnb", tt)] = dv.snap()

            tsn = {}
            csn = {}
            for ck in range(16):
                @block.tensor
                def _(tensor, ck=ck, tt=tt):
                    if ck == 0:
                        dv.wait_at(tensor, sst[("xnb", tt)])
                    elif ck >= 2:
                        dv.wait_at(tensor, csn[ck - 2])
                    pe.inc(tensor.transpose(ptr[ck % 2][:],
                                            xnb[:, ck * 128:(ck + 1) * 128],
                                            ident[:]))
                    tsn[ck] = pe.snap()

                @block.vector
                def _(vector, ck=ck, dcol=dcol):
                    pe.wait_at(vector, tsn[ck])
                    dv.inc(vector.tensor_copy(
                        xnt[ck][:, dcol:dcol + 128], ptr[ck % 2][:]))
                    csn[ck] = dv.snap()
            sst[("t15", tt)] = tsn[15]

    if "xnt_h" in io:
        for ck in range(16):
            @block.sync
            def _(sync, ck=ck):
                dv.wait(sync)
                st.dma(sync.dma_start(
                    out=io["xnt_h"][ck * 128:(ck + 1) * 128, :], in_=xnt[ck][:]))

    def rms_rope_tile(qtf, sq4, rstd4, rr, gain_cols, cidx, scale, dst_bf):
        @block.vector
        def _(vector):
            dv.wait(vector)
            sc.wait(vector)
            dv.inc(vector.tensor_tensor(out=sq4[:], in0=qtf[:], in1=qtf[:],
                                        op=ALU.mult))
            dv.wait(vector)
            dv.inc(vector.tensor_reduce(
                out=rstd4[:, 0:4].unsqueeze(2),
                in_=sq4[:].rearrange("p (h i) -> p h i", h=4),
                axis=AX.X, op=ALU.add))

        @block.scalar
        def _(scalar):
            dv.wait(scalar)
            sc.inc(scalar.activation(out=rstd4[:, 4:8], in_=rstd4[:, 0:4],
                                     func=AF.Ln, scale=1.0 / 128.0,
                                     bias=epst[:, 1:2]))
            sc.wait(scalar)
            sc.inc(scalar.activation(out=rstd4[:, 0:4], in_=rstd4[:, 4:8],
                                     func=AF.Exp, scale=-0.5))

        @block.vector
        def _(vector):
            sc.wait(vector)
            if gain_cols is not None:
                dv.inc(vector.tensor_tensor(out=rstd4[:, 0:4],
                                            in0=rstd4[:, 0:4],
                                            in1=gain_cols, op=ALU.mult))
                dv.wait(vector)
            dv.inc(vector.tensor_tensor(
                out=qtf[:].rearrange("p (h i) -> p h i", h=4),
                in0=qtf[:].rearrange("p (h i) -> p h i", h=4),
                in1=rstd4[:, 0:4].unsqueeze(2).broadcast_to([128, 4, 128]),
                op=ALU.mult))
            dv.wait(vector)
            c_b = cosk[:, cidx * 64:(cidx + 1) * 64].unsqueeze(1).unsqueeze(1) \
                .broadcast_to([128, 4, 2, 64])
            s_b = sink[:, cidx * 64:(cidx + 1) * 64].unsqueeze(1).unsqueeze(1) \
                .broadcast_to([128, 4, 2, 64])
            qv = qtf[:].rearrange("p (h a i) -> p h a i", h=4, a=2)
            ra = rr[0][:].rearrange("p (h a i) -> p h a i", h=4, a=2)
            rb = rr[1][:].rearrange("p (h a i) -> p h a i", h=4, a=2)
            dv.inc(vector.tensor_tensor(out=ra, in0=qv, in1=c_b, op=ALU.mult))
            dv.inc(vector.tensor_tensor(out=rb, in0=qv, in1=s_b, op=ALU.mult))
            dv.wait(vector)
            pre = sq4[:].rearrange("p (h a i) -> p h a i", h=4, a=2)
            dv.inc(vector.scalar_tensor_tensor(
                out=pre[:, :, 0:1, :], in0=ra[:, :, 0:1, :], scalar=1.0,
                in1=rb[:, :, 1:2, :], op0=ALU.mult, op1=ALU.add))
            dv.inc(vector.scalar_tensor_tensor(
                out=pre[:, :, 1:2, :], in0=ra[:, :, 1:2, :], scalar=1.0,
                in1=rb[:, :, 0:1, :], op0=ALU.mult, op1=ALU.subtract))
            dv.wait(vector)
            dv.inc(vector.tensor_scalar(out=dst_bf[:], in0=sq4[:],
                                        scalar1=scale, scalar2=None,
                                        op0=ALU.mult))

    NKT = 20
    with ExitStack() as s23:
        k_dm = [s23.enter_context(sbuf(f"kdm{e}", [128, 2560], BF16))
                for e in range(4)]
        v_tm = [s23.enter_context(sbuf(f"vtm{i}", [128, 512], BF16))
                for i in range(NKT)]
        wbuf = [s23.enter_context(sbuf(f"wk{ck}", [128, 512], BF16))
                for ck in range(16)]
        ktf = s23.enter_context(sbuf("ktf", [128, 512], F32))
        sq4 = s23.enter_context(sbuf("sq4", [128, 512], F32))
        rstd4 = s23.enter_context(sbuf("rstd4", [128, 8], F32))
        rr = [s23.enter_context(sbuf(f"rr{i}", [128, 512], F32)) for i in range(2)]
        tb = s23.enter_context(sbuf("tb", [128, 512], BF16))

        kk.barrier()
        with ExitStack() as sp:
            pk = [sp.enter_context(nc.psum_tensor(f"pk{i}", [128, 512], F32))
                  for i in range(2)]
            ptr = sp.enter_context(nc.psum_tensor("ptr2", [128, 128], BF16))

            @block.gpsimd
            def _(g):
                for ck in range(16):
                    ws.dma(g.dma_start(out=wbuf[ck][:],
                                       in_=io["wkt"][ck * 128:(ck + 1) * 128, :]))

            kcpsn = {}
            for kt in range(NKT):
                @block.tensor
                def _(tensor, kt=kt):
                    ws.wait(tensor)
                    if kt >= 2:
                        dv.wait_at(tensor, kcpsn[("ktf", kt - 2)])
                    else:
                        dv.wait(tensor)
                    for ck in range(16):
                        inst = tensor.matmul(
                            pk[kt % 2][:], xnt[ck][:, kt * 128:(kt + 1) * 128],
                            wbuf[ck][:], start=(ck == 0), stop=(ck == 15))
                    pe.inc(inst)
                kmm_sn = pe.snap()

                @block.vector
                def _(vector, kt=kt, kmm_sn=kmm_sn):
                    pe.wait_at(vector, kmm_sn)
                    dv.inc(vector.tensor_copy(ktf[:], pk[kt % 2][:]))
                kcpsn[("ktf", kt)] = dv.snap()
                rms_rope_tile(ktf, sq4, rstd4, rr, None, kt, 1.0, tb)
                rp_sn = dv.snap()

                ksn = {}
                kdsn = {}
                for e in range(4):
                    @block.tensor
                    def _(tensor, e=e, rp_sn=rp_sn):
                        if e == 0:
                            dv.wait_at(tensor, rp_sn)
                        else:
                            dv.wait_at(tensor, kdsn[e - 1])
                        pe.inc(tensor.transpose(ptr[:], tb[:, e * 128:(e + 1) * 128],
                                                ident[:]))
                        ksn[e] = pe.snap()

                    @block.vector
                    def _(vector, e=e, kt=kt):
                        pe.wait_at(vector, ksn[e])
                        dv.inc(vector.tensor_copy(
                            k_dm[e][:, kt * 128:(kt + 1) * 128], ptr[:]))
                        kdsn[e] = dv.snap()

        kk.barrier()
        with ExitStack() as sp:
            pv = [sp.enter_context(nc.psum_tensor(f"pv{i}", [128, 512], F32))
                  for i in range(2)]

            @block.gpsimd
            def _(g):
                pe.wait(g)
                dv.wait(g)
                for ck in range(16):
                    ws.dma(g.dma_start(out=wbuf[ck][:],
                                       in_=io["wvt"][ck * 128:(ck + 1) * 128, :]))

            vcpsn = {}
            for kt in range(NKT):
                @block.tensor
                def _(tensor, kt=kt):
                    ws.wait(tensor)
                    if kt >= 2:
                        dv.wait_at(tensor, vcpsn[kt - 2])
                    elif kt == 0:
                        dv.wait(tensor)
                    for ck in range(16):
                        inst = tensor.matmul(
                            pv[kt % 2][:], xnt[ck][:, kt * 128:(kt + 1) * 128],
                            wbuf[ck][:], start=(ck == 0), stop=(ck == 15))
                    pe.inc(inst)
                vmm_sn = pe.snap()

                @block.vector
                def _(vector, kt=kt, vmm_sn=vmm_sn):
                    pe.wait_at(vector, vmm_sn)
                    dv.inc(vector.tensor_copy(v_tm[kt][:], pv[kt % 2][:]))
                vcpsn[kt] = dv.snap()

        kk.barrier()
        q_dm2 = [s23.enter_context(sbuf(f"qd2_{i}", [128, 512], BF16))
                 for i in range(4)]
        colnegc = s23.enter_context(sbuf("colnegc", [128, 20], F32))
        mt_l0 = s23.enter_context(sbuf("mt_l0", [128, 8 * 512], BF16))
        identf = s23.enter_context(sbuf("identf", [128, 128], F32))

        @block.sync
        def _(sync):
            ld.dma(sync.dma_start(out=colnegc[:], in_=io["colnegc"]))
            for v_ in range(8):
                ld.dma(sync.dma_start(
                    out=mt_l0[:, v_ * 512:(v_ + 1) * 512], in_=io["mt_l0"][v_]))
            ld.dma(sync.dma_start(out=identf[:], in_=io["identf"]))

        with ExitStack() as spq:
            pq2 = spq.enter_context(nc.psum_tensor("l0pq2", [128, 512], F32))
            ptr2 = spq.enter_context(nc.psum_tensor("l0ptr2", [128, 128], BF16))

            stq = {"kcp": None}

            def fill_q(hg, tg):
                if tg == 0:
                    @block.gpsimd
                    def _(g2, hg=hg):
                        dv.wait(g2)
                        pe.wait(g2)
                        for ck in range(16):
                            ws.dma(g2.dma_start(
                                out=wbuf[ck][:],
                                in_=io["wqt"][ck * 128:(ck + 1) * 128,
                                              hg * 512:(hg + 1) * 512]))

                for sub in range(4):
                    tt = tg * 4 + sub

                    @block.tensor
                    def _(tensor, tt=tt):
                        ws.wait(tensor)
                        if stq["kcp"] is not None:
                            dv.wait_at(tensor, stq["kcp"])
                        else:
                            dv.wait(tensor)
                        for ck in range(16):
                            inst = tensor.matmul(
                                pq2[:],
                                xnt[ck][:, 512 + tt * 128:512 + (tt + 1) * 128],
                                wbuf[ck][:], start=(ck == 0), stop=(ck == 15))
                        pe.inc(inst)
                    qmm_sn = pe.snap()

                    @block.vector
                    def _(vector, tt=tt, qmm_sn=qmm_sn):
                        pe.wait_at(vector, qmm_sn)
                        dv.inc(vector.tensor_copy(ktf[:], pq2[:]))
                    stq["kcp"] = dv.snap()
                    rms_rope_tile(ktf, sq4, rstd4, rr,
                                  qgbc[:, hg * 4:(hg + 1) * 4], 4 + tt,
                                  0.08838834764831845, tb)
                    rq_sn = dv.snap()

                    qtsn = {}
                    qcsn = {}
                    for e in range(4):
                        @block.tensor
                        def _(tensor, e=e, rq_sn=rq_sn):
                            if e == 0:
                                dv.wait_at(tensor, rq_sn)
                            else:
                                dv.wait_at(tensor, qcsn[e - 1])
                            pe.inc(tensor.transpose(
                                ptr2[:], tb[:, e * 128:(e + 1) * 128], ident[:]))
                            qtsn[e] = pe.snap()

                        @block.vector
                        def _(vector, e=e, sub=sub):
                            pe.wait_at(vector, qtsn[e])
                            dv.inc(vector.tensor_copy(
                                q_dm2[e][:, sub * 128:(sub + 1) * 128], ptr2[:]))
                            qcsn[e] = dv.snap()

            def yt_store(h, tg, ytile):
                @block.gpsimd
                def _(g2, h=h, tg=tg, ytile=ytile):
                    dv.wait(g2)
                    gs.dma(g2.dma_start(
                        out=io["yt_h"][h * 128:(h + 1) * 128,
                                       tg * 512:(tg + 1) * 512],
                        in_=ytile[:]))

            emit_attn2(nc, s23, block, kk, dict(
                mode="l0", tag="Z", nkt=20, k_dm=k_dm, v_tm=v_tm, q_dm=q_dm2,
                onesb=onesb, onesf=onesf, ident=ident, identf=identf,
                colneg_row=halon[:], colnegc=colnegc,
                mrow_first=mfirst, mrow_last=mlast,
                mt_tiles=mt_l0, fill_q=fill_q, yt_store=yt_store))

    kk.barrier()
    with ExitStack() as s4:
        wo_sb = [s4.enter_context(sbuf(f"wo{ck}", [128, 2048], BF16))
                 for ck in range(16)]
        ytb = [s4.enter_context(sbuf(f"ytb{i}", [128, 128], BF16))
               for i in range(2)]
        stile = s4.enter_context(sbuf("stile4", [128, 2048], BF16))
        ps4 = [s4.enter_context(nc.psum_tensor(f"ps4{i}", [128, 512], F32))
               for i in range(4)]

        ypar = [Ctr(nc, "s4y0"), Ctr(nc, "s4y1")]
        ysn = {}
        mmsn = {}
        evsn = {}

        @block.gpsimd
        def _(g2):
            dv.wait(g2)
            for ck in range(16):
                ws.dma(g2.dma_start(out=wo_sb[ck][:],
                                    in_=io["wot"][ck * 128:(ck + 1) * 128, :]))

        for tt in range(16):
            for ck in range(16):
                gi = tt * 16 + ck

                @block.gpsimd
                def _(g2, tt=tt, ck=ck, gi=gi):
                    if gi == 0:
                        ws.wait(g2)
                    if gi >= 2:
                        pe.wait_at(g2, mmsn[gi - 2])
                    ypar[gi % 2].dma(g2.dma_start(
                        out=ytb[gi % 2][:],
                        in_=io["yt_h"][ck * 128:(ck + 1) * 128,
                                       tt * 128:(tt + 1) * 128]))
                    ysn[gi] = ypar[gi % 2].snap()

                @block.tensor
                def _(tensor, tt=tt, ck=ck, gi=gi):
                    ypar[gi % 2].wait_at(tensor, ysn[gi])
                    if ck == 0 and tt > 0:
                        dv.wait_at(tensor, evsn[tt - 1])
                    for et in range(4):
                        inst = tensor.matmul(
                            ps4[et][:], ytb[gi % 2][:],
                            wo_sb[ck][:, et * 512:(et + 1) * 512],
                            start=(ck == 0), stop=(ck == 15))
                    pe.inc(inst)
                    mmsn[gi] = pe.snap()

            @block.vector
            def _(vector, tt=tt):
                pe.wait_at(vector, mmsn[tt * 16 + 15])
                gs.wait(vector)
                for et in range(4):
                    dv.inc(vector.tensor_copy(
                        stile[:, et * 512:(et + 1) * 512], ps4[et][:]))
                evsn[tt] = dv.snap()

            @block.gpsimd
            def _(g2, tt=tt):
                dv.wait_at(g2, evsn[tt])
                gs.dma(g2.dma_start(
                    out=io["s_h"][tt * 128:(tt + 1) * 128, :], in_=stile[:]))

    kk.barrier()
    C1 = 0.7978845608028654
    C2 = 0.7978845608028654 * 0.044715
    with ExitStack() as s5:
        gwb = [[s5.enter_context(sbuf(f"gw{s_}_{ck}", [128, 512], BF16))
                for ck in range(16)] for s_ in range(2)]
        uwb = [[s5.enter_context(sbuf(f"uw{s_}_{ck}", [128, 512], BF16))
                for ck in range(16)] for s_ in range(2)]
        intb = [s5.enter_context(sbuf(f"intb{i}", [128, 2048], BF16))
                for i in range(4)]
        ge = [s5.enter_context(sbuf(f"ge{i}", [128, 512], F32)) for i in range(2)]
        ue = [s5.enter_context(sbuf(f"ue{i}", [128, 512], F32)) for i in range(2)]
        g2b = s5.enter_context(sbuf("g2b", [128, 512], F32))
        g3b = s5.enter_context(sbuf("g3b", [128, 512], F32))
        thb = s5.enter_context(sbuf("thb", [128, 512], F32))
        pg = [s5.enter_context(nc.psum_tensor(f"pg{i}", [128, 512], F32))
              for i in range(4)]
        pu = [s5.enter_context(nc.psum_tensor(f"pu{i}", [128, 512], F32))
              for i in range(4)]
        wpar = [Ctr(nc, "s5w0"), Ctr(nc, "s5w1")]
        wpar2 = [Ctr(nc, "s5v0"), Ctr(nc, "s5v1")]
        spar = [Ctr(nc, "s5s0"), Ctr(nc, "s5s1")]
        wsn = {}
        w2sn = {}
        ssn = {}
        mmsn = {}
        cpsn = {}

        def emit_w_loads(h5, gate_sn):
            sw = h5 % 2

            @block.gpsimd
            def _(g2x, h5=h5, sw=sw, gate_sn=gate_sn):
                if gate_sn is not None:
                    pe.wait_at(g2x, gate_sn)
                else:
                    dv.wait(g2x)
                for ck in range(16):
                    wpar[sw].dma(g2x.dma_start(
                        out=uwb[sw][ck][:],
                        in_=deint(io["du"], ck * 128, h5 * 512, (h5 + 1) * 512)))
                wsn[h5] = wpar[sw].snap()

            @block.sync
            def _(sync, h5=h5, sw=sw, gate_sn=gate_sn):
                if gate_sn is not None:
                    pe.wait_at(sync, gate_sn)
                for ck in range(16):
                    wpar2[sw].dma(sync.dma_start(
                        out=gwb[sw][ck][:],
                        in_=deint(io["dg"], ck * 128, h5 * 512, (h5 + 1) * 512)))
                w2sn[h5] = wpar2[sw].snap()

        emit_w_loads(0, None)
        emit_w_loads(1, None)
        for h5 in range(32):
            sw = h5 % 2

            for n4 in range(4):
                nt = h5 * 4 + n4
                for pair in range(2):
                    sidx = nt * 2 + pair
                    sb_ = sidx % 2

                    @block.tensor
                    def _(tensor, h5=h5, sw=sw, n4=n4, pair=pair, sidx=sidx,
                          sb_=sb_):
                        wpar[sw].wait_at(tensor, wsn[h5])
                        wpar2[sw].wait_at(tensor, w2sn[h5])
                        if sidx >= 2:
                            dv.wait_at(tensor, cpsn[sidx - 2])
                        for ck in range(16):
                            for j in range(2):
                                tc = pair * 2 + j
                                tensor.matmul(
                                    pg[sb_ * 2 + j][:],
                                    gwb[sw][ck][:, n4 * 128:(n4 + 1) * 128],
                                    xnt[ck][:, 512 + tc * 512:512 + (tc + 1) * 512],
                                    start=(ck == 0), stop=(ck == 15))
                                inst = tensor.matmul(
                                    pu[sb_ * 2 + j][:],
                                    uwb[sw][ck][:, n4 * 128:(n4 + 1) * 128],
                                    xnt[ck][:, 512 + tc * 512:512 + (tc + 1) * 512],
                                    start=(ck == 0), stop=(ck == 15))
                        pe.inc(inst)
                        mmsn[sidx] = pe.snap()

                    @block.vector
                    def _(vector, nt=nt, pair=pair, sidx=sidx, sb_=sb_):
                        pe.wait_at(vector, mmsn[sidx])
                        dv.wait(vector)
                        for j in range(2):
                            dv.inc(vector.tensor_copy(ge[j][:], pg[sb_ * 2 + j][:]))
                            dv.inc(vector.tensor_copy(ue[j][:], pu[sb_ * 2 + j][:]))
                        cpsn[sidx] = dv.snap()

                    for j in range(2):
                        tc = pair * 2 + j

                        @block.vector
                        def _(vector, j=j):
                            dv.wait(vector)
                            dv.inc(vector.tensor_tensor(out=g2b[:], in0=ge[j][:],
                                                        in1=ge[j][:], op=ALU.mult))
                            dv.wait(vector)
                            dv.inc(vector.tensor_scalar(out=g2b[:], in0=g2b[:],
                                                        scalar1=C2, scalar2=C1,
                                                        op0=ALU.mult, op1=ALU.add))
                            dv.wait(vector)
                            dv.inc(vector.tensor_tensor(out=g3b[:], in0=g2b[:],
                                                        in1=ge[j][:], op=ALU.mult))

                        @block.scalar
                        def _(scalar):
                            dv.wait(scalar)
                            sc.inc(scalar.activation(out=thb[:], in_=g3b[:],
                                                     func=AF.Tanh))

                        @block.vector
                        def _(vector, nt=nt, tc=tc, j=j, pair=pair):
                            sc.wait(vector)
                            if pair == 0 and j == 0 and nt >= 4:
                                spar[nt % 2].wait_at(vector, ssn[nt - 4])
                            dv.inc(vector.tensor_tensor(out=g3b[:], in0=thb[:],
                                                        in1=ge[j][:], op=ALU.mult))
                            dv.wait(vector)
                            dv.inc(vector.tensor_tensor(out=g3b[:], in0=g3b[:],
                                                        in1=ge[j][:], op=ALU.add))
                            dv.wait(vector)
                            dv.inc(vector.scalar_tensor_tensor(
                                out=intb[nt % 4][:, tc * 512:(tc + 1) * 512],
                                in0=g3b[:], scalar=0.5, in1=ue[j][:],
                                op0=ALU.mult, op1=ALU.mult))


            if h5 + 2 < 32:
                emit_w_loads(h5 + 2, mmsn[h5 * 8 + 7])

            @block.sync
            def _(sync, h5=h5):
                dv.wait(sync)
                for n4 in range(4):
                    nt = h5 * 4 + n4
                    spar[nt % 2].dma(sync.dma_start(
                        out=io["inter_h"][nt * 128:(nt + 1) * 128, :],
                        in_=intb[nt % 4][:]))
                    ssn[nt] = spar[nt % 2].snap()

        @block.sync
        def _(sync):
            spar[0].wait(sync)
            spar[1].wait(sync)
            ld.inc(sync.nop())

    kk.barrier()
    es2.close()
    with ExitStack() as s6:
        o2f = [s6.enter_context(sbuf(f"o2f{i}", [128, 2048], F32))
               for i in range(8)]
        chk = [[s6.enter_context(sbuf(f"chk{p}_{i}", [128, 1024], BF16))
                for i in range(8)] for p in range(2)]
        dbf = [[s6.enter_context(sbuf(f"dbf{p}_{i}", [128, 2048], BF16))
                for i in range(8)] for p in range(2)]
        pd = [s6.enter_context(nc.psum_tensor(f"pd{i}", [128, 512], F32))
              for i in range(8)]
        xb2 = s6.enter_context(sbuf("xb2", [128, 2048], F32))
        sb2 = s6.enter_context(sbuf("sb2", [128, 2048], BF16))
        gb2 = s6.enter_context(sbuf("gb2", [128, 2048], F32))
        ipar = [Ctr(nc, "s6i0"), Ctr(nc, "s6i1")]
        dpar = [Ctr(nc, "s6d0"), Ctr(nc, "s6d1")]
        isn = {}
        dsn = {}
        qsn = {}
        evsn = {}
        mmsn = {}

        for th2 in range(2):
            for c in range(16):
                ci = th2 * 16 + c
                p_ = ci % 2

                @block.sync
                def _(sync, th2=th2, c=c, ci=ci, p_=p_):
                    if ci >= 2:
                        pe.wait_at(sync, mmsn[ci - 2])
                    for nt8 in range(8):
                        ipar[p_].dma(sync.dma_start(
                            out=dbf[p_][nt8][:],
                            in_=deint(io["dd"], (c * 8 + nt8) * 128, 0, 2048)))
                    isn[ci] = ipar[p_].snap()

                @block.gpsimd
                def _(g2x, th2=th2, c=c, ci=ci, p_=p_):
                    if ci >= 2:
                        pe.wait_at(g2x, mmsn[ci - 2])
                    for nt8 in range(8):
                        r0 = (c * 8 + nt8) * 128
                        dpar[p_].dma(g2x.dma_start(
                            out=chk[p_][nt8][:],
                            in_=io["inter_h"][r0:r0 + 128,
                                              th2 * 1024:(th2 + 1) * 1024]))
                    dsn[ci] = dpar[p_].snap()

                for hc in range(4):
                    q = ci * 4 + hc

                    @block.tensor
                    def _(tensor, ci=ci, p_=p_, hc=hc, q=q):
                        if hc == 0:
                            ipar[p_].wait_at(tensor, isn[ci])
                            dpar[p_].wait_at(tensor, dsn[ci])
                        for nt8 in range(8):
                            for tt8 in range(8):
                                if nt8 == 0 and q >= 1:
                                    dv.wait_at(tensor, evsn[(q - 1, tt8)])
                                inst = tensor.matmul(
                                    pd[tt8][:],
                                    chk[p_][nt8][:, tt8 * 128:(tt8 + 1) * 128],
                                    dbf[p_][nt8][:, hc * 512:(hc + 1) * 512],
                                    start=(nt8 == 0), stop=(nt8 == 7))
                        pe.inc(inst)
                        qsn[q] = pe.snap()

                    @block.vector
                    def _(vector, c=c, hc=hc, q=q):
                        pe.wait_at(vector, qsn[q])
                        for tt8 in range(8):
                            if c == 0:
                                dv.inc(vector.tensor_copy(
                                    o2f[tt8][:, hc * 512:(hc + 1) * 512],
                                    pd[tt8][:]))
                            else:
                                dv.inc(vector.tensor_tensor(
                                    out=o2f[tt8][:, hc * 512:(hc + 1) * 512],
                                    in0=o2f[tt8][:, hc * 512:(hc + 1) * 512],
                                    in1=pd[tt8][:], op=ALU.add))
                            evsn[(q, tt8)] = dv.snap()
                mmsn[ci] = qsn[ci * 4 + 3]

            for i8 in range(8):
                tt = th2 * 8 + i8

                @block.sync
                def _(sync, tt=tt):
                    dv.wait(sync)
                    st.wait(sync)
                    ld.dma(sync.dma_start(
                        out=xb2[:], in_=io["x_my"][tt * 128:(tt + 1) * 128, :]))

                @block.gpsimd
                def _(g2x, tt=tt):
                    dv.wait(g2x)
                    ws.dma(g2x.dma_start(
                        out=sb2[:], in_=io["s_h"][tt * 128:(tt + 1) * 128, :]))

                @block.scalar
                def _(scalar, i8=i8):
                    dv.wait(scalar)
                    sc.inc(scalar.activation(out=gb2[:], in_=o2f[i8][:],
                                             func=AF.Silu))

                @block.vector
                def _(vector, i8=i8):
                    sc.wait(vector)
                    ld.wait(vector)
                    ws.wait(vector)
                    dv.wait(vector)
                    dv.inc(vector.tensor_tensor(out=gb2[:], in0=sb2[:],
                                                in1=gb2[:], op=ALU.mult))
                    dv.wait(vector)
                    dv.inc(vector.tensor_tensor(out=xb2[:], in0=xb2[:],
                                                in1=gb2[:], op=ALU.add))

                @block.sync
                def _(sync, tt=tt):
                    dv.wait(sync)
                    st.dma(sync.dma_start(
                        out=io["x1"][tt * 128:(tt + 1) * 128, :], in_=xb2[:]))

            kk.barrier()



def build_l0_program(debug=False):
    nc = bass.Bass("TRN2", target_bir_lowering=False)
    d = {}
    ein = lambda nm, sh, dt: nc.dram_tensor(nm, sh, dt, kind="ExternalInput").ap()
    d["x_my"] = ein("x_my", [2048, 2048], F32)
    d["x_halo"] = ein("x_halo", [512, 2048], F32)
    d["wqt"] = ein("wqt", [2048, 2048], F32)
    d["wkt"] = ein("wkt", [2048, 512], F32)
    d["wvt"] = ein("wvt", [2048, 512], F32)
    d["wot"] = ein("wot", [2048, 2048], F32)
    d["qgain"] = ein("qgain", [1, 16], F32)
    d["haloneg"] = ein("haloneg", [1, 2560], BF16)
    d["coskey"] = ein("coskey", [2560, 64], F32)
    d["sinkey"] = ein("sinkey", [2560, 64], F32)
    d["ident"] = ein("ident", [128, 128], BF16)
    d["ones_bf"] = ein("ones_bf", [128, 128], BF16)
    d["ones_f"] = ein("ones_f", [1, 128], F32)
    d["m_first"] = ein("m_first", [128, 128], BF16)
    d["m_last"] = ein("m_last", [128, 128], BF16)
    d["identf"] = ein("identf", [128, 128], F32)
    d["colnegc"] = ein("colnegc", [128, 20], F32)
    d["mt_l0"] = ein("mt_l0", [8, 128, 512], BF16)
    d["dg"] = ein("dg", [2048, 16384], BF16)
    d["du"] = ein("du", [2048, 16384], BF16)
    d["dd"] = ein("dd", [16384, 2048], BF16)
    d["x1"] = nc.dram_tensor("x1", [2048, 2048], F32, kind="ExternalOutput").ap()
    kind = "ExternalOutput" if debug else "Internal"
    if debug:
        d["xnt_h"] = nc.dram_tensor("xnt_h", [2048, 2560], BF16,
                                    kind="ExternalOutput").ap()
    d["yt_h"] = nc.dram_tensor("yt_h", [2048, 2048], BF16, kind=kind).ap()
    d["inter_h"] = nc.dram_tensor("inter_h", [16384, 2048], BF16,
                                  kind=kind).ap()
    d["s_h"] = nc.dram_tensor("s_h", [2048, 2048], BF16, kind=kind).ap()

    from contextlib import ExitStack
    with ExitStack() as es:
        block = es.enter_context(nc.Block())
        kk = K(nc, block)
        emit_layer0(nc, es, block, kk, d)
        kk.barrier()
    return nc


def perm128_rows(a):
    n = a.shape[0]
    idx = []
    for b0 in range(0, n, 128):
        idx += list(range(b0, b0 + 128, 2)) + list(range(b0 + 1, b0 + 128, 2))
    return a[np.array(idx)]


def ref_dequant_np(W_packed, scales, block=32):
    LUT = np.array([0.0, 0.5, 1.0, 1.5, 2.0, 3.0, 4.0, 6.0,
                    0.0, -0.5, -1.0, -1.5, -2.0, -3.0, -4.0, -6.0], np.float32)
    N = W_packed.shape[0]
    low = W_packed & 15
    high = (W_packed >> 4) & 15
    idx = np.stack([low, high], axis=-1).reshape(N, -1)
    vals = LUT[idx]
    Kd = vals.shape[1]
    vals = vals.reshape(N, Kd // block, block) * scales[:, :, None].astype(np.float32)
    return vals.reshape(N, Kd)


def l0_host_inputs(core, x_full, Wq, Wk, Wv, Wo, q_gain, dg, du, dd):
    b, half = core // 2, core % 2
    xb = np.asarray(x_full[b], np.float32)
    x_my = np.ascontiguousarray(xb[half * 2048:(half + 1) * 2048])
    if half == 1:
        x_halo = np.ascontiguousarray(xb[2048 - 512:2048])
        halopos = np.arange(2048 - 512, 2048)
        hneg = np.zeros(2560, np.float32)
    else:
        x_halo = np.zeros((512, 2048), np.float32)
        halopos = np.zeros(512, np.int64)
        hneg = np.zeros(2560, np.float32)
        hneg[:512] = NEG
    ownpos = np.arange(half * 2048, (half + 1) * 2048)
    keypos = np.concatenate([halopos, ownpos])
    ck, sk = l0_rope_tables(keypos)
    m_first, m_last = mask_tiles()
    return dict(
        x_my=x_my, x_halo=x_halo,
        wqt=host_T(Wq), wkt=host_T(Wk), wvt=host_T(Wv), wot=host_T(Wo),
        qgain=np.asarray(q_gain, np.float32).reshape(1, 16),
        haloneg=hneg.reshape(1, 2560).astype(ml_dtypes.bfloat16),
        coskey=ck, sinkey=sk,
        ident=np.eye(128, dtype=ml_dtypes.bfloat16),
        identf=np.eye(128, dtype=np.float32),
        ones_bf=np.ones((128, 128), ml_dtypes.bfloat16),
        ones_f=np.ones((1, 128), np.float32),
        m_first=m_first, m_last=m_last,
        colnegc=np.ascontiguousarray(hneg.reshape(20, 128).T).astype(np.float32),
        mt_l0=host_attn_masks_l0(),
        dg=dg, du=du, dd=dd,
    )


def host_decode_ffn(gate_W_packed, gate_scales, up_W_packed, up_scales,
                    down_W_packed, down_scales):
    Wg = ref_dequant_np(np.asarray(gate_W_packed), np.asarray(gate_scales))
    Wu = ref_dequant_np(np.asarray(up_W_packed), np.asarray(up_scales))
    Wd = ref_dequant_np(np.asarray(down_W_packed), np.asarray(down_scales))
    dg = perm128_rows(np.ascontiguousarray(Wg.T)).astype(ml_dtypes.bfloat16)
    du = perm128_rows(np.ascontiguousarray(Wu.T)).astype(ml_dtypes.bfloat16)
    dd = perm128_rows(np.ascontiguousarray(Wd.T)).astype(ml_dtypes.bfloat16)
    return dg, du, dd



def emit_layer1a(nc, es, block, kk, io):
    from contextlib import ExitStack
    ld, ws, pe, dv, sc, st, gs = (kk.ld, kk.ws, kk.pe, kk.dv, kk.sc, kk.st,
                                  kk.gs)
    sbuf = lambda nm, shape, dt: nc.sbuf_tensor("sb_" + nm, shape, dt)

    ident = es.enter_context(sbuf("ident", [128, 128], BF16))
    onesb = es.enter_context(sbuf("onesb", [128, 128], BF16))
    mlast = es.enter_context(sbuf("mlast", [128, 128], BF16))
    colneg = es.enter_context(sbuf("colneg", [1, 4096], BF16))
    cosk = es.enter_context(sbuf("cosk", [128, 32 * 64], BF16))
    sink = es.enter_context(sbuf("sink", [128, 32 * 64], BF16))
    bgb = es.enter_context(sbuf("bgb", [1, 8], BF16))
    beb = es.enter_context(sbuf("beb", [1, 2048], BF16))

    @block.sync
    def _(sync):
        ld.dma(sync.dma_start(out=ident[:], in_=io["ident"]))
        ld.dma(sync.dma_start(out=onesb[:], in_=io["ones_bf"]))
        ld.dma(sync.dma_start(out=mlast[:], in_=io["m_last"]))
        ld.dma(sync.dma_start(out=colneg[:], in_=io["colneg"]))
        ld.dma(sync.dma_start(out=bgb[:], in_=io["bg_bf"]))
        ld.dma(sync.dma_start(out=beb[:], in_=io["be_bf"]))

    @block.gpsimd
    def _(g):
        ws.dma(g.dma_start(
            out=cosk[:], in_=io["cos64"].rearrange("(a p) c -> p a c", p=128)))
        ws.dma(g.dma_start(
            out=sink[:], in_=io["sin64"].rearrange("(a p) c -> p a c", p=128)))

    kk.barrier()
    xnt = [es.enter_context(sbuf(f"xnt{ck}", [128, 2048], BF16))
           for ck in range(16)]

    def rope_l1_tile(qtf, sq4, rr, cidx, scale, dst_bf):
        @block.vector
        def _(vector):
            dv.wait(vector)
            sc.wait(vector)
            c_b = cosk[:, cidx * 64:(cidx + 1) * 64].unsqueeze(1).unsqueeze(1) \
                .broadcast_to([128, 4, 2, 64])
            s_b = sink[:, cidx * 64:(cidx + 1) * 64].unsqueeze(1).unsqueeze(1) \
                .broadcast_to([128, 4, 2, 64])
            qv = qtf[:].rearrange("p (h a i) -> p h a i", h=4, a=2)
            ra = rr[0][:].rearrange("p (h a i) -> p h a i", h=4, a=2)
            rb = rr[1][:].rearrange("p (h a i) -> p h a i", h=4, a=2)
            dv.inc(vector.tensor_tensor(out=ra, in0=qv, in1=c_b, op=ALU.mult))
            dv.inc(vector.tensor_tensor(out=rb, in0=qv, in1=s_b, op=ALU.mult))
            dv.wait(vector)
            pre = sq4[:].rearrange("p (h a i) -> p h a i", h=4, a=2)
            dv.inc(vector.scalar_tensor_tensor(
                out=pre[:, :, 0:1, :], in0=ra[:, :, 0:1, :], scalar=1.0,
                in1=rb[:, :, 1:2, :], op0=ALU.mult, op1=ALU.subtract))
            dv.inc(vector.scalar_tensor_tensor(
                out=pre[:, :, 1:2, :], in0=ra[:, :, 1:2, :], scalar=1.0,
                in1=rb[:, :, 0:1, :], op0=ALU.mult, op1=ALU.add))
            dv.wait(vector)
            dv.inc(vector.tensor_scalar(out=dst_bf[:], in0=sq4[:],
                                        scalar1=scale, scalar2=None,
                                        op0=ALU.mult))

    NKT = 32
    with ExitStack() as s1:
        k_dm = [s1.enter_context(sbuf(f"kdm{e}", [128, 4096], BF16))
                for e in range(4)]
        v_tm = [s1.enter_context(sbuf(f"vtm{i}", [128, 512], BF16))
                for i in range(NKT)]
        wkb = [s1.enter_context(sbuf(f"wkb{ck}", [128, 512], BF16))
               for ck in range(16)]
        epst = s1.enter_context(sbuf("epst", [128, 1], F32))
        ktf = s1.enter_context(sbuf("ktf", [128, 512], F32))
        sq4 = s1.enter_context(sbuf("sq4", [128, 512], F32))
        rr = [s1.enter_context(sbuf(f"rr{i}", [128, 512], BF16)) for i in range(2)]
        tb = s1.enter_context(sbuf("tb", [128, 512], BF16))

        @block.vector
        def _(vector):
            dv.inc(vector.memset(epst[:, 0:1], EPS_LN))

        with ExitStack() as sp:
            ptr = [sp.enter_context(nc.psum_tensor(f"l1p{i}", [128, 128], BF16))
                   for i in range(2)]
            pk = [sp.enter_context(nc.psum_tensor(f"l1k{i}", [128, 512], F32))
                  for i in range(2)]
            ptk = sp.enter_context(nc.psum_tensor("l1ptk", [128, 128], BF16))
            wvb = [sp.enter_context(sbuf(f"wvb{ck}", [128, 512], BF16))
                   for ck in range(16)]

            @block.gpsimd
            def _(g):
                for ck in range(16):
                    ws.dma(g.dma_start(out=wkb[ck][:],
                                       in_=io["wk"][ck * 128:(ck + 1) * 128, :]))
                    ws.dma(g.dma_start(out=wvb[ck][:],
                                       in_=io["wv"][ck * 128:(ck + 1) * 128, :]))
            xb0 = sp.enter_context(sbuf("xb0", [128, 2048], F32))
            xb = [xb0, xb0]
            xsq = sp.enter_context(sbuf("xsq", [128, 2048], BF16))
            xnb = sp.enter_context(sbuf("xnb", [128, 2048], BF16))
            stat = sp.enter_context(sbuf("stat", [128, 6], F32))
            xntt = [sp.enter_context(sbuf(f"xntt{ck}", [128, 128], BF16))
                    for ck in range(16)]

            state1 = {"xnb_sn": None, "t15": None}
            for kt in range(NKT):
                own = kt >= 16
                src = io["x1_my"] if own else io["x1_ctx"]
                r0 = (kt - 16) * 128 if own else kt * 128
                buf = xb[kt % 2]

                @block.sync
                def _(sync, src=src, r0=r0, buf=buf, kt=kt):
                    if state1["xnb_sn"] is not None:
                        dv.wait_at(sync, state1["xnb_sn"])
                    ld.dma(sync.dma_start(out=buf[:], in_=src[r0:r0 + 128, :]))

                @block.vector
                def _(vector, buf=buf):
                    ld.wait(vector)
                    sc.wait(vector)
                    dv.inc(vector.reduce_sum(stat[:, 0:1], buf[:], axis=AX.X))
                    dv.inc(vector.tensor_tensor(out=xsq[:], in0=buf[:],
                                                in1=buf[:], op=ALU.mult))
                    dv.wait(vector)
                    dv.inc(vector.reduce_sum(stat[:, 1:2], xsq[:], axis=AX.X))
                    dv.inc(vector.tensor_scalar(out=stat[:, 2:3], in0=stat[:, 0:1],
                                                scalar1=1.0 / 2048.0, scalar2=None,
                                                op0=ALU.mult))
                    dv.wait(vector)
                    dv.inc(vector.tensor_scalar(out=stat[:, 3:4], in0=stat[:, 1:2],
                                                scalar1=1.0 / 2048.0, scalar2=None,
                                                op0=ALU.mult))
                    dv.inc(vector.tensor_tensor(out=stat[:, 4:5], in0=stat[:, 2:3],
                                                in1=stat[:, 2:3], op=ALU.mult))
                    dv.wait(vector)
                    dv.inc(vector.tensor_tensor(out=stat[:, 5:6], in0=stat[:, 3:4],
                                                in1=stat[:, 4:5], op=ALU.subtract))

                @block.scalar
                def _(scalar):
                    dv.wait(scalar)
                    sc.inc(scalar.activation(out=stat[:, 4:5], in_=stat[:, 5:6],
                                             func=AF.Ln, bias=epst[:, 0:1]))
                    sc.wait(scalar)
                    sc.inc(scalar.activation(out=stat[:, 5:6], in_=stat[:, 4:5],
                                             func=AF.Exp, scale=-0.5))

                @block.vector
                def _(vector, buf=buf):
                    sc.wait(vector)
                    if state1["t15"] is not None:
                        pe.wait_at(vector, state1["t15"])
                    dv.inc(vector.tensor_scalar(out=xnb[:], in0=buf[:],
                                                scalar1=stat[:, 2:3],
                                                scalar2=stat[:, 5:6],
                                                op0=ALU.subtract, op1=ALU.mult))
                xnb_sn = dv.snap()
                state1["xnb_sn"] = xnb_sn

                tsn = {}
                xcp_sn = {}
                for ck in range(16):
                    @block.tensor
                    def _(tensor, ck=ck, xnb_sn=xnb_sn):
                        if ck == 0:
                            dv.wait_at(tensor, xnb_sn)
                        elif ck >= 2:
                            dv.wait_at(tensor, xcp_sn[ck - 2])
                        pe.inc(tensor.transpose(ptr[ck % 2][:],
                                                xnb[:, ck * 128:(ck + 1) * 128],
                                                ident[:]))
                        tsn[ck] = pe.snap()

                    @block.vector
                    def _(vector, ck=ck, kt=kt, own=own):
                        pe.wait_at(vector, tsn[ck])
                        dv.inc(vector.tensor_copy(xntt[ck][:], ptr[ck % 2][:]))
                        if own:
                            dv.inc(vector.tensor_copy(
                                xnt[ck][:, (kt - 16) * 128:(kt - 15) * 128],
                                ptr[ck % 2][:]))
                        xcp_sn[ck] = dv.snap()
                state1["t15"] = tsn[15]

                @block.tensor
                def _(tensor, kt=kt):
                    ws.wait(tensor)
                    dv.wait_at(tensor, xcp_sn[15])
                    for ck in range(16):
                        inst = tensor.matmul(pk[0][:], xntt[ck][:], wkb[ck][:],
                                             start=(ck == 0), stop=(ck == 15))
                    pe.inc(inst)
                kmm_sn = pe.snap()

                @block.tensor
                def _(tensor, kt=kt):
                    for ck in range(16):
                        inst = tensor.matmul(pk[1][:], xntt[ck][:], wvb[ck][:],
                                             start=(ck == 0), stop=(ck == 15))
                    pe.inc(inst)
                vmm_sn = pe.snap()

                @block.vector
                def _(vector, kt=kt, kmm_sn=kmm_sn, vmm_sn=vmm_sn):
                    pe.wait_at(vector, vmm_sn)
                    dv.inc(vector.tensor_copy(v_tm[kt][:], pk[1][:]))
                    dv.inc(vector.tensor_copy(ktf[:], pk[0][:]))
                rope_l1_tile(ktf, sq4, rr, kt, 1.0, tb)
                rope_sn = dv.snap()

                ksn = {}
                kcp_sn = {}
                for e in range(4):
                    @block.tensor
                    def _(tensor, e=e, rope_sn=rope_sn):
                        if e == 0:
                            dv.wait_at(tensor, rope_sn)
                        else:
                            dv.wait_at(tensor, kcp_sn[e - 1])
                        pe.inc(tensor.transpose(ptk[:], tb[:, e * 128:(e + 1) * 128],
                                                ident[:]))
                        ksn[e] = pe.snap()

                    @block.vector
                    def _(vector, e=e, kt=kt):
                        pe.wait_at(vector, ksn[e])
                        dv.inc(vector.tensor_copy(
                            k_dm[e][:, kt * 128:(kt + 1) * 128], ptk[:]))
                        kcp_sn[e] = dv.snap()

        kk.barrier()
        q_dm2 = [s1.enter_context(sbuf(f"qd2_{i}", [128, 512], BF16))
                 for i in range(4)]
        colnegc = s1.enter_context(sbuf("colnegc", [128, 32], F32))
        mt_l1 = s1.enter_context(sbuf("mt_l1", [128, 4 * 512], BF16))
        identf = s1.enter_context(sbuf("identf", [128, 128], F32))
        onesf2 = s1.enter_context(sbuf("onesf2", [1, 128], F32))
        mlastT = s1.enter_context(sbuf("mlastT", [128, 128], BF16))

        @block.sync
        def _(sync):
            ld.dma(sync.dma_start(out=colnegc[:],
                                  in_=io["colnegc"]))
            for v_ in range(4):
                ld.dma(sync.dma_start(
                    out=mt_l1[:, v_ * 512:(v_ + 1) * 512],
                    in_=io["mt_l1"][v_]))
            ld.dma(sync.dma_start(out=identf[:], in_=io["identf"]))
            ld.dma(sync.dma_start(out=onesf2[:], in_=io["ones_f"]))

        with ExitStack() as spq:
            pq2 = spq.enter_context(nc.psum_tensor("l1pq2", [128, 512], F32))
            ptr2 = spq.enter_context(nc.psum_tensor("l1ptr2", [128, 128], BF16))

            stq = {"kcp": None, "qcp": None}

            def fill_q(hg, tg):
                if tg == 0:
                    @block.gpsimd
                    def _(g2, hg=hg):
                        dv.wait(g2)
                        pe.wait(g2)
                        for ck in range(16):
                            ws.dma(g2.dma_start(
                                out=wkb[ck][:],
                                in_=io["wq"][ck * 128:(ck + 1) * 128,
                                             hg * 512:(hg + 1) * 512]))

                for sub in range(4):
                    tt = tg * 4 + sub

                    @block.tensor
                    def _(tensor, tt=tt):
                        ws.wait(tensor)
                        if stq["kcp"] is not None:
                            dv.wait_at(tensor, stq["kcp"])
                        else:
                            dv.wait(tensor)
                        for ck in range(16):
                            inst = tensor.matmul(
                                pq2[:], xnt[ck][:, tt * 128:(tt + 1) * 128],
                                wkb[ck][:], start=(ck == 0), stop=(ck == 15))
                        pe.inc(inst)
                    qmm_sn = pe.snap()

                    @block.vector
                    def _(vector, tt=tt, qmm_sn=qmm_sn):
                        pe.wait_at(vector, qmm_sn)
                        dv.inc(vector.tensor_copy(ktf[:], pq2[:]))
                    stq["kcp"] = dv.snap()
                    rope_l1_tile(ktf, sq4, rr, 16 + tt, 0.08838834764831845, tb)
                    rq_sn = dv.snap()

                    qtsn = {}
                    qcsn = {}
                    for e in range(4):
                        @block.tensor
                        def _(tensor, e=e, rq_sn=rq_sn):
                            if e == 0:
                                dv.wait_at(tensor, rq_sn)
                            else:
                                dv.wait_at(tensor, qcsn[e - 1])
                            pe.inc(tensor.transpose(
                                ptr2[:], tb[:, e * 128:(e + 1) * 128], ident[:]))
                            qtsn[e] = pe.snap()

                        @block.vector
                        def _(vector, e=e, sub=sub):
                            pe.wait_at(vector, qtsn[e])
                            dv.inc(vector.tensor_copy(
                                q_dm2[e][:, sub * 128:(sub + 1) * 128], ptr2[:]))
                            qcsn[e] = dv.snap()

            def yt_store(h, qg, ytile):
                @block.gpsimd
                def _(g2, h=h, qg=qg, ytile=ytile):
                    dv.wait(g2)
                    gs.dma(g2.dma_start(
                        out=io["yt_h"][h * 128:(h + 1) * 128,
                                       qg * 512:(qg + 1) * 512],
                        in_=ytile[:]))

            kk.barrier()
            import os as _os
            dbgcfg = {}
            if _os.environ.get("L1_DBG2") == "1":
                dbgcfg["dbg"] = io["dbgrow"]
                dbgcfg["dbg_py"] = s1.enter_context(sbuf("dbgpy", [128, 512], F32))
                dbgcfg["dbg_pyh"] = io["dbgpy"]
                dbgcfg["dbg_et"] = io["dbget"]
            emit_attn2(nc, s1, block, kk, dict(
                mode="l1", tag="A", nkt=32, k_dm=k_dm, v_tm=v_tm, q_dm=q_dm2,
                onesb=onesb, onesf=onesf2, ident=ident, identf=identf,
                colneg_row=colneg[:], colnegc=colnegc,
                mrow_first=None, mrow_last=mlast,
                mt_tiles=mt_l1, fill_q=fill_q, yt_store=yt_store, **dbgcfg))

    kk.barrier()
    with ExitStack() as s4:
        wo_sb = [s4.enter_context(sbuf(f"wo{ck}", [128, 2048], BF16))
                 for ck in range(16)]
        ytb = [s4.enter_context(sbuf(f"ytb{i}", [128, 128], BF16))
               for i in range(2)]
        stile_ = [s4.enter_context(sbuf(f"stile4{i}", [128, 2048], BF16))
                  for i in range(2)]
        ps4 = [s4.enter_context(nc.psum_tensor(f"l1o{i}", [128, 512], F32))
               for i in range(4)]
        stpar4 = [Ctr(nc, "l1st40"), Ctr(nc, "l1st41")]
        stsn4 = {}

        @block.gpsimd
        def _(g2):
            dv.wait(g2)
            for ck in range(16):
                ws.dma(g2.dma_start(out=wo_sb[ck][:],
                                    in_=io["wo"][ck * 128:(ck + 1) * 128, :]))

        ypar4 = [Ctr(nc, "l1y40"), Ctr(nc, "l1y41")]
        ysn4 = {}
        mm4 = {}
        ev4 = {}
        for tt in range(16):
            for ck in range(16):
                gi = tt * 16 + ck

                @block.gpsimd
                def _(g2, tt=tt, ck=ck, gi=gi):
                    if gi == 0:
                        ws.wait(g2)
                    if gi >= 2:
                        pe.wait_at(g2, mm4[gi - 2])
                    ypar4[gi % 2].dma(g2.dma_start(
                        out=ytb[gi % 2][:],
                        in_=io["yt_h"][ck * 128:(ck + 1) * 128,
                                       tt * 128:(tt + 1) * 128]))
                    ysn4[gi] = ypar4[gi % 2].snap()

                @block.tensor
                def _(tensor, tt=tt, ck=ck, gi=gi):
                    ypar4[gi % 2].wait_at(tensor, ysn4[gi])
                    if ck == 0 and tt > 0:
                        dv.wait_at(tensor, ev4[tt - 1])
                    elif gi == 0:
                        dv.wait(tensor)
                    for et in range(4):
                        inst = tensor.matmul(
                            ps4[et][:], ytb[gi % 2][:],
                            wo_sb[ck][:, et * 512:(et + 1) * 512],
                            start=(ck == 0), stop=(ck == 15))
                    pe.inc(inst)
                mm4[gi] = pe.snap()

            @block.vector
            def _(vector, tt=tt):
                pe.wait_at(vector, mm4[tt * 16 + 15])
                if tt >= 2:
                    stpar4[tt % 2].wait_at(vector, stsn4[tt - 2])
                for et in range(4):
                    dv.inc(vector.tensor_copy(
                        stile_[tt % 2][:, et * 512:(et + 1) * 512], ps4[et][:]))
            ev4[tt] = dv.snap()

            @block.sync
            def _(sync, tt=tt):
                dv.wait_at(sync, ev4[tt])
                stpar4[tt % 2].dma(sync.dma_start(
                    out=io["s1"][tt * 128:(tt + 1) * 128, :],
                    in_=stile_[tt % 2][:]))
                stsn4[tt] = stpar4[tt % 2].snap()

        @block.sync
        def _(sync):
            stpar4[0].wait(sync)
            stpar4[1].wait(sync)
            ld.inc(sync.nop())

    kk.barrier()
    with ExitStack() as s5:
        we_sb = [s5.enter_context(sbuf(f"we{ck}", [128, 2048], BF16))
                 for ck in range(16)]
        wg_sb = s5.enter_context(sbuf("wg", [128, 8 * 16], BF16))
        ytile = s5.enter_context(sbuf("ytile", [128, 2048], BF16))
        lg = s5.enter_context(sbuf("lg", [128, 8], F32))
        lge = s5.enter_context(sbuf("lge", [128, 8], F32))
        m8 = s5.enter_context(sbuf("m8", [128, 8], F32))
        i8 = s5.enter_context(sbuf("i8", [128, 8], U32))
        besti = s5.enter_context(sbuf("besti", [128, 1], I32))
        bp = s5.enter_context(sbuf("bp", [128, 2], F32))
        pyv = [s5.enter_context(nc.psum_tensor(f"l1yv{i}", [128, 512], F32))
               for i in range(4)]
        plg = s5.enter_context(nc.psum_tensor("l1lg", [128, 8], F32))

        @block.gpsimd
        def _(g2):
            dv.wait(g2)
            for ck in range(16):
                ws.dma(g2.dma_start(out=we_sb[ck][:],
                                    in_=io["we"][ck * 128:(ck + 1) * 128, :]))
                ws.dma(g2.dma_start(out=wg_sb[:, ck * 8:(ck + 1) * 8],
                                    in_=io["wg"][ck * 128:(ck + 1) * 128, :]))

        for tt in range(16):
            @block.tensor
            def _(tensor, tt=tt):
                ws.wait(tensor)
                dv.wait(tensor)
                for ck in range(16):
                    for et in range(4):
                        tensor.matmul(
                            pyv[et][:], xnt[ck][:, tt * 128:(tt + 1) * 128],
                            we_sb[ck][:, et * 512:(et + 1) * 512],
                            start=(ck == 0), stop=False)
                    inst = tensor.matmul(
                        plg[:], xnt[ck][:, tt * 128:(tt + 1) * 128],
                        wg_sb[:, ck * 8:(ck + 1) * 8],
                        start=(ck == 0), stop=False)
                for et in range(4):
                    tensor.matmul(pyv[et][:], onesb[0:1, :],
                                  beb[0:1, et * 512:(et + 1) * 512],
                                  start=False, stop=True)
                inst = tensor.matmul(plg[:], onesb[0:1, :], bgb[0:1, :],
                                     start=False, stop=True)
                pe.inc(inst)

            @block.vector
            def _(vector, tt=tt):
                pe.wait(vector)
                st.wait(vector)
                dv.wait(vector)
                for et in range(4):
                    dv.inc(vector.tensor_copy(
                        ytile[:, et * 512:(et + 1) * 512], pyv[et][:]))
                dv.inc(vector.tensor_copy(lg[:], plg[:]))
                dv.wait(vector)
                dv.inc(vector.reduce_max(bp[:, 0:1], lg[:], axis=AX.X,
                                         negate=True))
                dv.inc(vector.max(m8[:], lg[:]))
                dv.wait(vector)
                dv.inc(vector.max_index(i8[:], m8[:], lg[:]))

            @block.scalar
            def _(scalar):
                dv.wait(scalar)
                sc.inc(scalar.activation(out=lge[:], in_=lg[:], func=AF.Exp,
                                         bias=bp[:, 0:1]))

            @block.vector
            def _(vector, tt=tt):
                sc.wait(vector)
                dv.inc(vector.reduce_sum(bp[:, 0:1], lge[:], axis=AX.X))
                dv.wait(vector)
                dv.inc(vector.reciprocal(bp[:, 1:2], bp[:, 0:1]))
                dv.inc(vector.tensor_copy(besti[:], i8[:, 0:1]))

            @block.sync
            def _(sync, tt=tt):
                dv.wait(sync)
                st.dma(sync.dma_start(out=io["yv"][tt * 128:(tt + 1) * 128, :],
                                      in_=ytile[:]))
                st.dma(sync.dma_start(out=io["best"][tt * 128:(tt + 1) * 128, :],
                                      in_=besti[:]))
                st.dma(sync.dma_start(out=io["bestp"][tt * 128:(tt + 1) * 128, :],
                                      in_=bp[:, 1:2]))



def emit_layer1b(nc, es, block, kk, io):
    from contextlib import ExitStack
    ld, ws, pe, dv, sc, st, gs = (kk.ld, kk.ws, kk.pe, kk.dv, kk.sc, kk.st,
                                  kk.gs)
    sbuf = lambda nm, shape, dt: nc.sbuf_tensor("sb_" + nm, shape, dt)

    onesb = es.enter_context(sbuf("onesb", [128, 128], BF16))
    onesf = es.enter_context(sbuf("onesf", [1, 128], F32))
    tri = es.enter_context(sbuf("tri", [128, 128], BF16))
    bmat = es.enter_context(sbuf("bmat", [128, 128], F32))
    coreoff = es.enter_context(sbuf("coreoff", [128, 1], F32))
    iota8 = es.enter_context(sbuf("iota8", [128, 8], I32))
    iota8f = es.enter_context(sbuf("iota8f", [128, 8], F32))
    onh = es.enter_context(sbuf("onh", [128, 1024], BF16))
    onhf = es.enter_context(sbuf("onhf", [128, 1024], F32))
    totrow = es.enter_context(sbuf("totrow", [1, 1024], F32))
    totmat = es.enter_context(sbuf("totmat", [128, 8], BF16))
    cbsb = es.enter_context(sbuf("cbsb", [128, 8], F32))
    cbrow = es.enter_context(sbuf("cbrow", [1, 1024], F32))
    basep = es.enter_context(sbuf("basep", [1, 24], F32))
    combbc = es.enter_context(sbuf("combbc", [128, 1024], F32))
    rankf = es.enter_context(sbuf("rankf", [128, 1024], F32))
    rankm = es.enter_context(sbuf("rankm", [128, 128], F32))
    ranki = es.enter_context(sbuf("ranki", [128, 128], I32))
    jmat = es.enter_context(sbuf("jmat", [128, 128], I32))
    valid = es.enter_context(sbuf("valid", [128, 128], F32))
    tmpm = es.enter_context(sbuf("tmpm", [128, 128], F32))
    ranki_f_dummy = es.enter_context(sbuf("ranksel", [128, 128], F32))

    @block.sync
    def _(sync):
        ld.dma(sync.dma_start(out=onesb[:], in_=io["ones_bf"]))
        ld.dma(sync.dma_start(out=onesf[:], in_=io["ones_f"]))
        ld.dma(sync.dma_start(out=tri[:], in_=io["tri"]))
        ld.dma(sync.dma_start(out=bmat[:], in_=io["bestmat"]))
        ld.dma(sync.dma_start(out=coreoff[:], in_=io["coreoff"]))

    @block.gpsimd
    def _(g):
        g.iota(iota8[:], pattern=[[1, 8]], base=0,
               channel_multiplier=0).then_inc(kk.ws.sem, 1)
        kk.ws.n += 1
        g.iota(jmat[:], pattern=[[128, 128]], base=0,
               channel_multiplier=1).then_inc(kk.ws.sem, 1)
        kk.ws.n += 1

    kk.barrier()
    with ExitStack() as sp:
        ppfx = sp.enter_context(nc.psum_tensor("ppfx", [128, 1024], F32))
        ptot = sp.enter_context(nc.psum_tensor("ptot", [1, 1024], F32))
        pcb = sp.enter_context(nc.psum_tensor("pcb", [128, 8], F32))
        pbc = sp.enter_context(nc.psum_tensor("pbc2", [128, 512], F32))

        @block.vector
        def _(vector):
            ld.wait(vector)
            ws.wait(vector)
            dv.inc(vector.tensor_copy(iota8f[:], iota8[:]))
            dv.wait(vector)
            dv.inc(vector.tensor_tensor(
                out=onh[:].rearrange("p (c e) -> p c e", e=8),
                in0=bmat[:].unsqueeze(2).broadcast_to([128, 128, 8]),
                in1=iota8f[:].unsqueeze(1).broadcast_to([128, 128, 8]),
                op=ALU.is_equal))
            dv.wait(vector)
            dv.inc(vector.tensor_copy(onhf[:], onh[:]))

        @block.tensor
        def _(tensor):
            dv.wait(tensor)
            pe.inc(tensor.matmul(ppfx[:, 0:512], tri[:], onh[:, 0:512],
                                 start=True, stop=True))
            pe.inc(tensor.matmul(ppfx[:, 512:1024], tri[:], onh[:, 512:1024],
                                 start=True, stop=True))
            pe.inc(tensor.matmul(ptot[:, 0:512], onesb[:, 0:1], onh[:, 0:512],
                                 start=True, stop=True))
            inst = tensor.matmul(ptot[:, 512:1024], onesb[:, 0:1],
                                 onh[:, 512:1024], start=True, stop=True)
            pe.inc(inst)

        @block.vector
        def _(vector):
            pe.wait(vector)
            dv.inc(vector.tensor_copy(totrow[:], ptot[:]))

        @block.sync
        def _(sync):
            dv.wait(sync)
            ld.dma(sync.dma_start(out=io["trow_h"], in_=totrow[:]))

        @block.gpsimd
        def _(g):
            ld.wait(g)
            ws.dma(g.dma_start(
                out=totmat[:],
                in_=io["trow_h"].rearrange("o (c e) -> (o c) e", e=8)))

        @block.tensor
        def _(tensor):
            ws.wait(tensor)
            pe.inc(tensor.matmul(pcb[:], tri[:], totmat[:], start=True,
                                 stop=True))

        @block.vector
        def _(vector):
            pe.wait(vector)
            dv.inc(vector.tensor_copy(cbsb[:], pcb[:]))
            dv.inc(vector.tensor_reduce(
                out=basep[0:1, 16:24].unsqueeze(2),
                in_=totrow[:].rearrange("o (c e) -> o e c", e=8),
                axis=AX.X, op=ALU.add))
            dv.inc(vector.memset(basep[0:1, 0:16], 0.0))
            dv.wait(vector)
            for sh in (1, 2, 4):
                dv.inc(vector.tensor_tensor(
                    out=basep[0:1, 16:24], in0=basep[0:1, 16:24],
                    in1=basep[0:1, 16 - sh:24 - sh], op=ALU.add))
                dv.wait(vector)

        @block.sync
        def _(sync):
            dv.wait(sync)
            ld.dma(sync.dma_start(out=io["cb_h"], in_=cbsb[:]))
            ld.wait(sync)
            ld.dma(sync.dma_start(
                out=cbrow[:],
                in_=io["cb_h"].rearrange("c e -> (c e)").unsqueeze(0)))

        @block.vector
        def _(vector):
            ld.wait(vector)
            dv.inc(vector.tensor_tensor(
                out=cbrow[:].rearrange("o (c e) -> o c e", e=8),
                in0=cbrow[:].rearrange("o (c e) -> o c e", e=8),
                in1=basep[0:1, 15:23].unsqueeze(1).broadcast_to([1, 128, 8]),
                op=ALU.add))

        for half in range(2):
            @block.tensor
            def _(tensor, half=half):
                dv.wait(tensor)
                pe.inc(tensor.matmul(pbc[:], onesf[:],
                                     cbrow[0:1, half * 512:(half + 1) * 512],
                                     start=True, stop=True))

            @block.vector
            def _(vector, half=half):
                pe.wait(vector)
                dv.inc(vector.tensor_copy(
                    combbc[:, half * 512:(half + 1) * 512], pbc[:]))

        @block.vector
        def _(vector):
            pe.wait(vector)
            dv.wait(vector)
            dv.inc(vector.tensor_tensor(out=rankf[:], in0=ppfx[:], in1=combbc[:],
                                        op=ALU.add))
            dv.wait(vector)
            dv.inc(vector.tensor_tensor(out=onhf[:], in0=onhf[:], in1=rankf[:],
                                        op=ALU.mult))
            dv.wait(vector)
            dv.inc(vector.tensor_reduce(
                out=rankm[:].unsqueeze(2),
                in_=onhf[:].rearrange("p (c e) -> p c e", e=8),
                axis=AX.X, op=ALU.add))
            dv.wait(vector)
            dv.inc(vector.tensor_scalar(out=rankm[:], in0=rankm[:],
                                        scalar1=coreoff[:, 0:1], scalar2=None,
                                        op0=ALU.subtract))
            dv.wait(vector)
            dv.inc(vector.tensor_scalar(out=valid[:], in0=rankm[:], scalar1=0.0,
                                        scalar2=None, op0=ALU.is_ge))
            dv.inc(vector.tensor_scalar(out=tmpm[:], in0=rankm[:], scalar1=2048.0,
                                        scalar2=None, op0=ALU.is_lt))
            dv.wait(vector)
            dv.inc(vector.tensor_tensor(out=valid[:], in0=valid[:], in1=tmpm[:],
                                        op=ALU.mult))
            dv.inc(vector.tensor_scalar(out=tmpm[:], in0=rankm[:],
                                        scalar1=2048.0, scalar2=None,
                                        op0=ALU.subtract))
            dv.wait(vector)
            dv.inc(vector.tensor_tensor(out=ranki_f_dummy[:], in0=valid[:],
                                        in1=tmpm[:], op=ALU.mult))
            dv.wait(vector)
            dv.inc(vector.tensor_scalar(out=ranki_f_dummy[:],
                                        in0=ranki_f_dummy[:], scalar1=2048.0,
                                        scalar2=None, op0=ALU.add))
            dv.wait(vector)
            dv.inc(vector.tensor_copy(ranki[:], ranki_f_dummy[:]))

        ybc = [sp.enter_context(sbuf(f"ybc{i}", [128, 2048], BF16))
               for i in range(2)]
        ypar = [Ctr(nc, "l1by0"), Ctr(nc, "l1by1")]
        scpar = [Ctr(nc, "l1bs0"), Ctr(nc, "l1bs1")]
        ybsn = {}
        scsn = {}
        for c in range(128):
            p_ = c % 2

            @block.sync
            def _(sync, c=c, p_=p_):
                if c >= 2:
                    scpar[p_].wait_at(sync, scsn[c - 2])
                ypar[p_].dma(sync.dma_start(
                    out=ybc[p_][:], in_=io["yfull"][c * 128:(c + 1) * 128, :]))
                ybsn[c] = ypar[p_].snap()

            @block.gpsimd
            def _(g, c=c, p_=p_):
                if c == 0:
                    dv.wait(g)
                ypar[p_].wait_at(g, ybsn[c])
                scpar[p_].dma(g.indirect_dma_start(
                    out=io["g_loc"][:],
                    out_offset=bass.IndirectOffsetOnAxis(
                        ap=ranki[:, c:c + 1], axis=0),
                    in_=ybc[p_][:],
                    in_offset=None))
                scsn[c] = scpar[p_].snap()

        @block.gpsimd
        def _(g):
            scpar[0].wait(g)
            scpar[1].wait(g)
            ws.inc(g.nop())

    kk.barrier()
    with ExitStack() as s2:
        ybuf = s2.enter_context(sbuf("ybuf", [128, 2048], BF16))
        bpcol = s2.enter_context(sbuf("bpcol", [128, 1], F32))
        sb1 = s2.enter_context(sbuf("sb1", [128, 2048], BF16))
        xb1 = s2.enter_context(sbuf("xb1", [128, 2048], F32))
        gb1 = s2.enter_context(sbuf("gb1", [128, 2048], F32))
        for tt in range(16):
            @block.sync
            def _(sync, tt=tt):
                dv.wait(sync)
                st.wait(sync)
                ld.dma(sync.dma_start(
                    out=bpcol[:], in_=io["bestp_my"][tt * 128:(tt + 1) * 128, :]))
                ld.dma(sync.dma_start(
                    out=sb1[:], in_=io["s1_my"][tt * 128:(tt + 1) * 128, :]))
                ld.dma(sync.dma_start(
                    out=xb1[:], in_=io["x1_my"][tt * 128:(tt + 1) * 128, :]))

            @block.gpsimd
            def _(g, tt=tt):
                dv.wait(g)
                ws.dma(g.dma_start(
                    out=ybuf[:],
                    in_=io["g_loc"][tt * 128:(tt + 1) * 128, :]))

            @block.vector
            def _(vector):
                ws.wait(vector)
                ld.wait(vector)
                dv.inc(vector.tensor_scalar(out=gb1[:], in0=ybuf[:],
                                            scalar1=bpcol[:, 0:1], scalar2=None,
                                            op0=ALU.mult))

            @block.scalar
            def _(scalar):
                dv.wait(scalar)
                sc.inc(scalar.activation(out=gb1[:], in_=gb1[:], func=AF.Silu))

            @block.vector
            def _(vector):
                sc.wait(vector)
                dv.inc(vector.tensor_tensor(out=gb1[:], in0=sb1[:], in1=gb1[:],
                                            op=ALU.mult))
                dv.wait(vector)
                dv.inc(vector.tensor_tensor(out=xb1[:], in0=xb1[:], in1=gb1[:],
                                            op=ALU.add))

            @block.sync
            def _(sync, tt=tt):
                dv.wait(sync)
                st.dma(sync.dma_start(out=io["x2"][tt * 128:(tt + 1) * 128, :],
                                      in_=xb1[:]))



def build_l1a_program():
    nc = bass.Bass("TRN2", target_bir_lowering=False)
    d = {}
    ein = lambda nm, sh, dt: nc.dram_tensor(nm, sh, dt, kind="ExternalInput").ap()
    d["x1_my"] = ein("x1_my", [2048, 2048], F32)
    d["x1_ctx"] = ein("x1_ctx", [2048, 2048], F32)
    d["wq"] = ein("wq", [2048, 2048], F32)
    d["wk"] = ein("wk", [2048, 512], F32)
    d["wv"] = ein("wv", [2048, 512], F32)
    d["wo"] = ein("wo", [2048, 2048], F32)
    d["wg"] = ein("wg", [2048, 8], F32)
    d["bg_bf"] = ein("bg_bf", [1, 8], BF16)
    d["we"] = ein("we", [2048, 2048], F32)
    d["be_bf"] = ein("be_bf", [1, 2048], BF16)
    d["cos64"] = ein("cos64", [4096, 64], F32)
    d["sin64"] = ein("sin64", [4096, 64], F32)
    d["colneg"] = ein("colneg", [1, 4096], BF16)
    d["ident"] = ein("ident", [128, 128], BF16)
    d["identf"] = ein("identf", [128, 128], F32)
    d["ones_bf"] = ein("ones_bf", [128, 128], BF16)
    d["ones_f"] = ein("ones_f", [1, 128], F32)
    d["m_last"] = ein("m_last", [128, 128], BF16)
    d["colnegc"] = ein("colnegc", [128, 32], F32)
    d["mt_l1"] = ein("mt_l1", [4, 128, 512], BF16)
    d["s1"] = nc.dram_tensor("s1", [2048, 2048], BF16, kind="ExternalOutput").ap()
    d["yv"] = nc.dram_tensor("yv", [2048, 2048], BF16, kind="ExternalOutput").ap()
    d["best"] = nc.dram_tensor("best", [2048, 1], I32, kind="ExternalOutput").ap()
    d["bestp"] = nc.dram_tensor("bestp", [2048, 1], F32,
                                kind="ExternalOutput").ap()
    import os
    ytk = "ExternalOutput" if os.environ.get("L1_DBG") == "1" else "Internal"
    d["yt_h"] = nc.dram_tensor("yt_h", [2048, 2048], BF16, kind=ytk).ap()
    if os.environ.get("L1_DBG2") == "1":
        d["dbgrow"] = nc.dram_tensor("dbgrow", [3, 512], F32,
                                     kind="ExternalOutput").ap()
        d["dbgpy"] = nc.dram_tensor("dbgpy", [128, 512], F32,
                                    kind="ExternalOutput").ap()
        d["dbget"] = nc.dram_tensor("dbget", [128, 512], BF16,
                                    kind="ExternalOutput").ap()

    from contextlib import ExitStack
    with ExitStack() as es:
        block = es.enter_context(nc.Block())
        kk = K(nc, block)
        emit_layer1a(nc, es, block, kk, d)
        kk.barrier()
    return nc


def build_l1b_program():
    nc = bass.Bass("TRN2", target_bir_lowering=False)
    d = {}
    ein = lambda nm, sh, dt: nc.dram_tensor(nm, sh, dt, kind="ExternalInput").ap()
    d["yfull"] = ein("yfull", [16384, 2048], BF16)
    d["bestmat"] = ein("bestmat", [128, 128], F32)
    d["x1_my"] = ein("x1_my", [2048, 2048], F32)
    d["s1_my"] = ein("s1_my", [2048, 2048], BF16)
    d["bestp_my"] = ein("bestp_my", [2048, 1], F32)
    d["coreoff"] = ein("coreoff", [128, 1], F32)
    d["tri"] = ein("tri", [128, 128], BF16)
    d["ones_bf"] = ein("ones_bf", [128, 128], BF16)
    d["ones_f"] = ein("ones_f", [1, 128], F32)
    d["x2"] = nc.dram_tensor("x2", [2048, 2048], F32, kind="ExternalOutput").ap()
    d["g_loc"] = nc.dram_tensor("g_loc", [2049, 2048], BF16,
                                kind="Internal").ap()
    d["trow_h"] = nc.dram_tensor("trow_h", [1, 1024], F32, kind="Internal").ap()
    d["cb_h"] = nc.dram_tensor("cb_h", [128, 8], F32, kind="Internal").ap()

    from contextlib import ExitStack
    with ExitStack() as es:
        block = es.enter_context(nc.Block())
        kk = K(nc, block)
        emit_layer1b(nc, es, block, kk, d)
        kk.barrier()
    return nc


def l1a_host_inputs(core, x1_full_f32):
    b, half = core // 2, core % 2
    xb = x1_full_f32[b]
    x1_my = np.ascontiguousarray(xb[half * 2048:(half + 1) * 2048])
    if half == 1:
        x1_ctx = np.ascontiguousarray(xb[0:2048])
        ctxpos = np.arange(0, 2048)
        cneg = np.zeros(4096, np.float32)
    else:
        x1_ctx = np.zeros((2048, 2048), np.float32)
        ctxpos = np.zeros(2048, np.int64)
        cneg = np.zeros(4096, np.float32)
        cneg[:2048] = NEG
    ownpos = np.arange(half * 2048, (half + 1) * 2048)
    keypos = np.concatenate([ctxpos, ownpos])
    c64, s64 = l1_rope_tables(keypos)
    _, m_last = mask_tiles()
    return dict(
        x1_my=x1_my, x1_ctx=x1_ctx,
        cos64=np.ascontiguousarray(c64[:, :64]),
        sin64=np.ascontiguousarray(s64[:, :64]),
        colneg=cneg.reshape(1, 4096).astype(ml_dtypes.bfloat16),
        colnegc=np.ascontiguousarray(cneg.reshape(32, 128).T).astype(np.float32),
        mt_l1=host_attn_masks_l1(),
        ident=np.eye(128, dtype=ml_dtypes.bfloat16),
        identf=np.eye(128, dtype=np.float32),
        ones_bf=np.ones((128, 128), ml_dtypes.bfloat16),
        ones_f=np.ones((1, 128), np.float32),
        m_last=m_last,
    )


def l1a_weights(wq, wk, wv, wo, Wg, bg, We, be):
    return dict(
        wq=np.ascontiguousarray(np.asarray(wq, np.float32)),
        wk=np.ascontiguousarray(np.asarray(wk, np.float32)),
        wv=np.ascontiguousarray(np.asarray(wv, np.float32)),
        wo=np.ascontiguousarray(np.asarray(wo, np.float32)),
        wg=np.ascontiguousarray(np.asarray(Wg, np.float32)),
        bg_bf=np.asarray(bg, np.float32).reshape(1, 8).astype(ml_dtypes.bfloat16),
        we=np.ascontiguousarray(np.asarray(We, np.float32)),
        be_bf=np.asarray(be, np.float32).reshape(1, 2048).astype(
            ml_dtypes.bfloat16),
    )


def l1b_host_inputs(core, yfull_bf, best_full_i32, x1_my, s1_my, bestp_my):
    tri = np.triu(np.ones((128, 128), np.float32), 1).astype(ml_dtypes.bfloat16)
    bestmat = best_full_i32.reshape(128, 128).T
    return dict(
        yfull=yfull_bf,
        bestmat=np.ascontiguousarray(bestmat).astype(np.float32),
        x1_my=x1_my, s1_my=s1_my, bestp_my=bestp_my,
        coreoff=np.full((128, 1), 2048.0 * core, np.float32),
        tri=tri,
        ones_bf=np.ones((128, 128), ml_dtypes.bfloat16),
        ones_f=np.ones((1, 128), np.float32),
    )



def build_dec_program():
    nc = bass.Bass("TRN2", target_bir_lowering=False)
    ein = lambda nm, sh, dt: nc.dram_tensor(nm, sh, dt, kind="ExternalInput").ap()
    gb = ein("gb", [128, 16384], U8)
    gsc = ein("gsc", [8, 16384], F32)
    ub = ein("ub", [128, 16384], U8)
    usc = ein("usc", [8, 16384], F32)
    db = ein("db", [1024, 2048], U8)
    dsc = ein("dsc", [64, 2048], F32)
    og = nc.dram_tensor("og", [256, 16384], BF16, kind="ExternalOutput").ap()
    ou = nc.dram_tensor("ou", [256, 16384], BF16, kind="ExternalOutput").ap()
    od = nc.dram_tensor("od", [2048, 2048], BF16, kind="ExternalOutput").ap()
    from contextlib import ExitStack
    with ExitStack() as es:
        block = es.enter_context(nc.Block())
        kk = K(nc, block)
        emit_decode(nc, es, block, kk, "g", 2, 16384, gb, gsc, og)
        kk.barrier()
        emit_decode(nc, es, block, kk, "u", 2, 16384, ub, usc, ou)
        kk.barrier()
        emit_decode(nc, es, block, kk, "d", 16, 2048, db, dsc, od)
        kk.barrier()
    return nc



_PROG_CACHE = {}


def _prog(name, builder):
    if name not in _PROG_CACHE:
        _PROG_CACHE[name] = builder()
    return _PROG_CACHE[name]


def kernel(**inputs):
    from concourse.bass_utils import run_bass_kernel_spmd
    inp = {k: np.asarray(v) for k, v in inputs.items()}

    gbT = host_bytesT(inp["L0_ffn_gate_W_packed"])
    gscT = host_T(inp["L0_ffn_gate_scales"])
    ubT = host_bytesT(inp["L0_ffn_up_W_packed"])
    uscT = host_T(inp["L0_ffn_up_scales"])
    dbT = host_bytesT(inp["L0_ffn_down_W_packed"])
    dscT = host_T(inp["L0_ffn_down_scales"])
    nc = _prog("dec", build_dec_program)
    in_maps = []
    for c in range(8):
        in_maps.append(dict(
            gb=np.ascontiguousarray(gbT[128 * c:128 * (c + 1)]),
            gsc=np.ascontiguousarray(gscT[8 * c:8 * (c + 1)]),
            ub=np.ascontiguousarray(ubT[128 * c:128 * (c + 1)]),
            usc=np.ascontiguousarray(uscT[8 * c:8 * (c + 1)]),
            db=np.ascontiguousarray(dbT[1024 * c:1024 * (c + 1)]),
            dsc=np.ascontiguousarray(dscT[64 * c:64 * (c + 1)]),
        ))
    res = run_bass_kernel_spmd(nc, in_maps, core_ids=list(range(8)))
    dg = np.concatenate([np.asarray(res.results[c]["og"]) for c in range(8)], 0)
    du = np.concatenate([np.asarray(res.results[c]["ou"]) for c in range(8)], 0)
    dd = np.concatenate([np.asarray(res.results[c]["od"]) for c in range(8)], 0)

    nc = _prog("l0", build_l0_program)
    x = inp["L0_attn_x"]
    in_maps = [l0_host_inputs(c, x, inp["L0_attn_Wq"], inp["L0_attn_Wk"],
                              inp["L0_attn_Wv"], inp["L0_attn_Wo"],
                              inp["L0_attn_q_gain"], dg, du, dd)
               for c in range(8)]
    res = run_bass_kernel_spmd(nc, in_maps, core_ids=list(range(8)))
    x1_full = np.concatenate(
        [np.asarray(res.results[c]["x1"]) for c in range(8)], 0
    ).reshape(4, 4096, 2048)

    nc = _prog("l1a", build_l1a_program)
    wts = l1a_weights(inp["L1_attn_wq"], inp["L1_attn_wk"], inp["L1_attn_wv"],
                      inp["L1_attn_wo"], inp["L1_ffn_Wg"], inp["L1_ffn_bg"],
                      inp["L1_ffn_We"], inp["L1_ffn_be"])
    in_maps = [dict(**l1a_host_inputs(c, x1_full), **wts) for c in range(8)]
    res = run_bass_kernel_spmd(nc, in_maps, core_ids=list(range(8)))
    yfull = np.concatenate(
        [np.asarray(res.results[c]["yv"]) for c in range(8)], 0)
    best_full = np.concatenate(
        [np.asarray(res.results[c]["best"]).ravel() for c in range(8)], 0)
    bestp_all = [np.asarray(res.results[c]["bestp"]) for c in range(8)]
    s1_all = [np.asarray(res.results[c]["s1"]) for c in range(8)]

    nc = _prog("l1b", build_l1b_program)
    in_maps = []
    for c in range(8):
        b, half = c // 2, c % 2
        x1_my = np.ascontiguousarray(x1_full[b][half * 2048:(half + 1) * 2048])
        in_maps.append(l1b_host_inputs(c, yfull, best_full.astype(np.int32),
                                       x1_my, s1_all[c], bestp_all[c]))
    res = run_bass_kernel_spmd(nc, in_maps, core_ids=list(range(8)))
    x2 = np.concatenate(
        [np.asarray(res.results[c]["x2"]) for c in range(8)], 0
    ).reshape(4, 4096, 2048)
    return x2



def host_attn_masks_l0():
    i = np.arange(128)[:, None]
    j = np.arange(128)[None, :]
    first = np.where(j >= i + 1, 0.0, NEG).astype(np.float32).T
    last = np.where(j <= i, 0.0, NEG).astype(np.float32).T
    openb = np.zeros((128, 128), np.float32)
    full = np.full((128, 128), NEG, np.float32)
    tiles = []
    for j0 in range(8):
        cols = []
        for sub in range(4):
            r = j0 - sub
            if r == 0:
                cols.append(first)
            elif r == 4:
                cols.append(last)
            elif 1 <= r <= 3:
                cols.append(openb)
            else:
                cols.append(full)
        tiles.append(np.concatenate(cols, 1))
    return np.stack(tiles).astype(ml_dtypes.bfloat16)


def host_attn_masks_l1():
    i = np.arange(128)[:, None]
    j_ = np.arange(128)[None, :]
    last = np.where(j_ <= i, 0.0, NEG).astype(np.float32).T
    openb = np.zeros((128, 128), np.float32)
    full = np.full((128, 128), NEG, np.float32)
    tiles = []
    for jv in range(4):
        cols = []
        for sub in range(4):
            if sub > jv:
                cols.append(openb)
            elif sub == jv:
                cols.append(last)
            else:
                cols.append(full)
        tiles.append(np.concatenate(cols, 1))
    return np.stack(tiles).astype(ml_dtypes.bfloat16)


def emit_attn2(nc, es, block, kk, cfg):
    ld, ws, pe, dv, sc, st, gs = (kk.ld, kk.ws, kk.pe, kk.dv, kk.sc, kk.st,
                                  kk.gs)
    mode = cfg["mode"]
    k_dm, v_tm, q_dm = cfg["k_dm"], cfg["v_tm"], cfg["q_dm"]
    onesb = cfg["onesb"]
    sbuf = lambda nm, shape, dt: nc.sbuf_tensor("at2_" + cfg["tag"] + nm, shape, dt)
    from contextlib import ExitStack
    with ExitStack() as sp:
        etb = [sp.enter_context(sbuf(f"etb{i}", [128, 512], BF16))
               for i in range(3)]
        zrow = sp.enter_context(sbuf("zrow", [1, 512], F32))
        zrowb = sp.enter_context(sbuf("zrowb", [1, 512], F32))
        ivb = sp.enter_context(sbuf("ivb", [128, 512], F32))
        ytile = sp.enter_context(sbuf("ytile", [128, 512], BF16))
        pSt = [sp.enter_context(nc.psum_tensor(f"a2S{cfg['tag']}{i}",
                                               [128, 512], F32))
               for i in range(3)]
        pZ = sp.enter_context(nc.psum_tensor(f"a2Z{cfg['tag']}", [1, 512], F32))
        pY = sp.enter_context(nc.psum_tensor(f"a2Y{cfg['tag']}", [128, 512], F32))
        pB = sp.enter_context(nc.psum_tensor(f"a2B{cfg['tag']}", [128, 512], F32))

        for engn in ("tensor", "vector", "scalar"):
            getattr(block, engn)(lambda eng: ld.wait(eng))

        state = {"norm_sn": None, "zav_last": None}

        for hg in range(4):
          for tg in range(4):
            cfg["fill_q"](hg, tg)
            for hh in range(4):
                h = hg * 4 + hh
                g = hg
                if mode == "l0":
                    blocks = list(range(tg * 4, tg * 4 + 8))
                else:
                    blocks = list(range(0, 16 + tg * 4 + 4))
                nblk = len(blocks)
                st_sn = {}
                msk_sn = {}
                exp_sn = {}
                zav_sn = {}

                def emit_stmm(bi):
                    b = blocks[bi]

                    @block.tensor
                    def _(tensor, b=b, bi=bi):
                        if bi >= 3:
                            sc.wait_at(tensor, exp_sn[bi - 3])
                        elif bi == 0:
                            if hh == 0:
                                dv.wait(tensor)
                            sc.wait(tensor)
                        inst = tensor.matmul(pSt[bi % 3][:],
                                             k_dm[g][:, b * 128:(b + 1) * 128],
                                             q_dm[hh][:, 0:512],
                                             start=True, stop=True)
                        pe.inc(inst)
                        st_sn[bi] = pe.snap()

                for bi in range(min(3, nblk)):
                    emit_stmm(bi)
                for bi, b in enumerate(blocks):
                    if mode == "l0":
                        jv = b - tg * 4
                        do_mask = True
                    else:
                        jv = b - 16 - tg * 4
                        do_mask = 0 <= jv <= 3

                    if do_mask:
                        @block.vector
                        def _(vector, bi=bi, jv=jv):
                            pe.wait_at(vector, st_sn[bi])
                            dv.inc(vector.tensor_tensor(
                                out=pSt[bi % 3][:], in0=pSt[bi % 3][:],
                                in1=cfg["mt_tiles"][:, jv * 512:(jv + 1) * 512],
                                op=ALU.add))
                        msk_sn[bi] = dv.snap()

                    @block.scalar
                    def _(scalar, b=b, bi=bi, do_mask=do_mask):
                        if do_mask:
                            dv.wait_at(scalar, msk_sn[bi])
                        else:
                            pe.wait_at(scalar, st_sn[bi])
                        if bi >= 3:
                            pe.wait_at(scalar, zav_sn[bi - 3])
                        elif bi == 0 and state["zav_last"] is not None:
                            pe.wait_at(scalar, state["zav_last"])
                        sc.inc(scalar.activation(
                            out=etb[bi % 3][:], in_=pSt[bi % 3][:], func=AF.Exp,
                            bias=cfg["colnegc"][:, b:b + 1]))
                    exp_sn[bi] = sc.snap()

                    if bi + 3 < nblk:
                        emit_stmm(bi + 3)

                    @block.tensor
                    def _(tensor, b=b, bi=bi, nblk=nblk):
                        sc.wait_at(tensor, exp_sn[bi])
                        if bi == 0 and state["norm_sn"] is not None:
                            dv.wait_at(tensor, state["norm_sn"])
                        tensor.matmul(pZ[:], onesb[:, 0:1], etb[bi % 3][:],
                                      start=(bi == 0), stop=(bi == nblk - 1))
                        inst = tensor.matmul(
                            pY[:], v_tm[b][:, g * 128:(g + 1) * 128],
                            etb[bi % 3][:],
                            start=(bi == 0), stop=(bi == nblk - 1))
                        pe.inc(inst)
                    zav_sn[bi] = pe.snap()

                state["zav_last"] = zav_sn[nblk - 1]

                @block.vector
                def _(vector):
                    pe.wait_at(vector, zav_sn[nblk - 1])
                    dv.inc(vector.tensor_copy(zrow[:], pZ[:]))
                    dv.wait(vector)
                    dv.inc(vector.reciprocal(zrowb[:], zrow[:]))
                zr_sn = dv.snap()

                @block.tensor
                def _(tensor):
                    dv.wait_at(tensor, zr_sn)
                    pe.inc(tensor.matmul(pB[:], cfg["onesf"][:], zrowb[:],
                                         start=True, stop=True))
                pb_sn = pe.snap()

                @block.vector
                def _(vector):
                    pe.wait_at(vector, pb_sn)
                    gs.wait(vector)
                    dv.inc(vector.tensor_copy(ivb[:], pB[:]))
                    dv.wait(vector)
                    dv.inc(vector.tensor_tensor(out=ytile[:], in0=pY[:],
                                                in1=ivb[:], op=ALU.mult))
                state["norm_sn"] = dv.snap()

                cfg["yt_store"](h, tg, ytile)
